# version 24
# baseline (speedup 1.0000x reference)
import numpy as np
import ml_dtypes
import concourse.bass as bass
import concourse.mybir as mybir
from concourse.bass_utils import run_bass_kernel_spmd

F32 = mybir.dt.float32
BF16 = mybir.dt.bfloat16
AF = mybir.ActivationFunctionType
ALU = mybir.AluOpType
AX = mybir.AxisListType

import os
P = 128
D = 1024
DT = 8
NS = 16
TC = 512
NCH = int(os.environ.get("ANT_NCHUNK", "8"))
TAPS = os.environ.get("ANT_TAPS", "") == "1"
SELFSYNC = os.environ.get("ANT_SELFSYNC", "0") == "1"
DR = mybir.MatmulPerfMode.DoubleRow
TT4 = 4
IF_ = 2816
IT = 22
ISH = 352
IQ = 704
TAU = [0.0625, 0.5625, 1.5625, 3.0625, 6.25, 12.25, 25.0]

_CACHE = {}


class Plan:
    ENGS = ("sync", "vector", "scalar", "tensor", "gpsimd")
    OWN = {"sync": None, "vector": "v", "scalar": "a", "tensor": "t",
           "gpsimd": "g"}

    def __init__(self):
        self.ops = {e: [] for e in self.ENGS}
        self.cnt = {}
        self.pending = {e: [] for e in self.ENGS}
        self.groups = set()

    def bump(self, name, amt):
        self.cnt[name] = self.cnt.get(name, 0) + amt
        self.groups.add(name)
        return self.cnt[name]

    def add(self, eng, fn, waits=(), inc=None, amt=None):
        w = self.pending[eng] + [x for x in waits if x[1]]
        self.pending[eng] = []
        mark = None
        if inc is not None:
            if amt is None:
                amt = 1 if inc in ("v", "a", "t") else 16
            mark = self.bump(inc, amt)
        self.ops[eng].append((fn, tuple(w), inc, amt))
        return mark

    def barrier(self):
        snap = dict(self.cnt)
        for e in self.ENGS:
            own = self.OWN[e]
            self.pending[e] += [(c, v) for c, v in snap.items()
                                if c != own and v > 0]

    def emit(self, eng, handle, sems, selfsync=False):
        seen = {}
        own = self.OWN[eng]
        pos = 0
        for fn, waits, inc, amt in self.ops[eng]:
            best = {}
            forced = 0
            for w in waits:
                ctr, val = w[0], w[1]
                if ctr == own:
                    if len(w) >= 3:
                        forced = max(forced, val)
                    continue
                if val > best.get(ctr, seen.get(ctr, 0)):
                    best[ctr] = val
            if selfsync and own is not None:
                forced = max(forced, pos)
            if forced:
                handle.wait_ge(sems[own], forced)
            for ctr, val in best.items():
                handle.wait_ge(sems[ctr], val)
                seen[ctr] = val
            last = fn(handle)
            if inc is not None:
                last.then_inc(sems[inc], amt)
                if inc == own:
                    pos += amt


class Arena:
    def __init__(self, nc, ec, name, words):
        self.t = ec(nc.sbuf_tensor(name, [P, words], F32))
        self.off = 0
        self.words = words

    def carve(self, cols, dtype=F32):
        w = cols if dtype == F32 else (cols + 1) // 2
        w = (w + 15) // 16 * 16
        assert self.off + w <= self.words, f"arena overflow {self.off + w} > {self.words}"
        base = self.t[:, self.off:self.off + w]
        self.off += w
        if dtype != F32:
            return base.bitcast(dtype)[:, 0:cols]
        return base[:, 0:cols]

    def mark(self):
        return self.off

    def reset(self, m):
        self.off = m


def build_program():
    if "nc" in _CACHE:
        return _CACHE["nc"]
    nc = bass.Bass("TRN2", target_bir_lowering=False)

    dx = nc.dram_tensor("x", [4096, D], F32, kind="ExternalInput").ap()
    dxtail = nc.dram_tensor("xtail", [3, D], F32, kind="ExternalInput").ap()
    dhmask = nc.dram_tensor("hmask", [P, 1], F32, kind="ExternalInput").ap()
    dwin_f = nc.dram_tensor("win_f", [D, 2 * D], F32, kind="ExternalInput").ap()
    dwout_f = nc.dram_tensor("wout_f", [D, D], F32, kind="ExternalInput").ap()
    dwg_f = nc.dram_tensor("wg_f", [IF_, D], F32, kind="ExternalInput").ap()
    dwu_f = nc.dram_tensor("wu_f", [IF_, D], F32, kind="ExternalInput").ap()
    dwd_f = nc.dram_tensor("wd_f", [D, IF_], F32, kind="ExternalInput").ap()
    dprty = nc.dram_tensor("prty", [1, 1], mybir.dt.int32, kind="ExternalInput").ap()
    dwxp = nc.dram_tensor("wxp", [DT * P, 80], F32, kind="ExternalInput").ap()
    dbxp = nc.dram_tensor("bxp", [80, 1], F32, kind="ExternalInput").ap()
    dwdt = nc.dram_tensor("wdt", [NS, D], F32, kind="ExternalInput").ap()
    dbdt = nc.dram_tensor("bdt", [P, DT], F32, kind="ExternalInput").ap()
    dbin = nc.dram_tensor("bin", [P, 16], F32, kind="ExternalInput").ap()
    dtaps = nc.dram_tensor("taps", [P, DT * 4], F32, kind="ExternalInput").ap()
    dbconv = nc.dram_tensor("bconv", [P, DT], F32, kind="ExternalInput").ap()
    dalog = nc.dram_tensor("alog", [P, DT * NS], F32, kind="ExternalInput").ap()
    dbout_row = nc.dram_tensor("bout_row", [1, D], F32, kind="ExternalInput").ap()
    dout = nc.dram_tensor("out", [4096, D], F32, kind="ExternalOutput").ap()

    def idram(name, shape, dt=F32, shared=False):
        return nc.dram_tensor(name, shape, dt, kind="Internal",
                              addr_space="Shared" if shared else "Local").ap()

    g_win = idram("g_win", [D, 2 * D], BF16)
    g_wout = idram("g_wout", [D, D], BF16)
    g_wgT = idram("g_wgT", [D, IF_], BF16)
    g_wuT = idram("g_wuT", [D, IF_], BF16)
    g_wdT = idram("g_wdT", [IF_, D], BF16)
    sh_state = idram("sh_state", [2 * P, P], shared=True)
    sh_flag = idram("sh_flag", [2, 64], mybir.dt.int32, shared=True)
    sp_P = idram("sp_P", [NCH * DT, P, TC], F32)
    sp_y = idram("sp_y", [NCH * DT, P, TC], F32)
    sp_zs = idram("sp_zs", [NCH * DT, P, TC], BF16)
    sp_brow = idram("sp_brow", [NCH, 1, NS * TC], BF16)
    sp_crow = idram("sp_crow", [NCH, 1, NS * TC], BF16)

    dident = nc.inline_tensor(np.eye(P, dtype=ml_dtypes.bfloat16), name="ident").ap()

    pl = Plan()
    from contextlib import ExitStack
    stack = ExitStack()
    ec = stack.enter_context

    def sb(name, shape, dt=F32):
        return ec(nc.sbuf_tensor(name, shape, dt))

    s_ident = sb("s_ident", [P, P], BF16)
    s_hmask = sb("s_hmask", [P, 1])
    s_taps = sb("s_taps", [P, DT * 4])
    s_bconv = sb("s_bconv", [P, DT])
    s_bdt = sb("s_bdt", [P, DT])
    s_bin = sb("s_bin", [P, 16])
    s_A = sb("s_A", [P, DT * NS])
    s_wxp = sb("s_wxp", [P, DT * 80], BF16)
    s_wdt = sb("s_wdt", [NS, D], BF16)
    s_bxp = sb("s_bxp", [80, 1])
    s_ones1 = sb("s_ones1", [1, P], BF16)
    s_boutr = sb("s_boutr", [1, D], BF16)
    s_hscol = sb("s_hscol", [P, P])
    s_pcol = sb("s_pcol", [P, DT])
    s_hb = sb("s_hb", [P, P])
    s_col = sb("s_col", [P, 16])
    s_utail = sb("s_utail", [P, 3 * DT])
    s_fli = sb("s_fli", [1, 128], mybir.dt.int32)
    s_prk = sb("s_prk", [1, 1], mybir.dt.int32)
    s_spfl = sb("s_spfl", [1, 64], mybir.dt.int32)
    spin_sem = ec(nc.semaphore("spin_sem"))
    prty_reg = ec(nc.sync.register(name="prty_reg"))
    sh_flagv = sh_flag.rearrange("(s o) c -> s o c", o=1)
    sh_statev = sh_state.rearrange("(s p) c -> s p c", p=P)
    s_cst = sb("s_cst", [P, 12])
    CONSTS = [1e-5, -0.5, -1.0, 1.0, 1.0 / D, -6e-12, 6e-12, 1.0 / 12.0, 6.0,
              5e-13]
    for ci, cv in enumerate(CONSTS):
        nc.const_aps.register(F32, cv, s_cst[:, ci:ci + 1])

    AR = Arena(nc, ec, "arena", 50992)
    FR = AR.mark()

    q_m = AR.carve(IF_ // 16)
    q_r = AR.carve(IF_ // 16)
    q_rsq = AR.carve(IF_ // 16)
    q_sc2 = AR.carve(IF_ // 16)
    q_xsq = AR.carve(IF_)
    q_acc = AR.carve(IF_)
    q_b7 = AR.carve(IF_)
    q_sgn = AR.carve(IF_, BF16)
    QFR = AR.mark()

    psf = [ec(nc.psum_tensor(f"psf{i}", [P, 512], F32)) for i in range(8)]

    def psbf(i):
        return psf[i][:].bitcast(BF16)

    sems = {}
    block = ec(nc.Block())

    W = lambda *pairs: list(pairs)

    def dma(dst, src, grp, waits=(), gq=False):
        eng = "gpsimd" if gq else "sync"
        return pl.add(eng, lambda e, dst=dst, src=src: e.dma_start(out=dst, in_=src),
                      waits=waits, inc=grp)

    tap_t = {}
    def tap(name, shape, dtype, src_ap, waits):
        if not TAPS or src_ap is None:
            return
        tap_t[name] = nc.dram_tensor("tap_" + name, shape, dtype,
                                     kind="ExternalOutput").ap()
        dma(tap_t[name], src_ap, "tap", waits=waits)
        pl.barrier()

    def ostore(dst, src, waits=()):
        return pl.add("sync", lambda e, dst=dst, src=src: e.dma_start(out=dst, in_=src),
                      waits=waits, inc="outs")

    QS0 = dict(m=q_m, r=q_r, rsq=q_rsq, sc2=q_sc2, xsq=q_xsq, acc=q_acc,
               b7=q_b7, sgn=q_sgn, xsqb=q_xsq.bitcast(BF16),
               accb=q_acc.bitcast(BF16), b7b=q_b7.bitcast(BF16),
               rsqb=q_rsq.bitcast(BF16))

    def qhead(qs, xin_ap, PN, F, wait_v=(), wait_a=(), lowp=False):
        nb = F // 16
        xsq = qs["xsqb"] if lowp else qs["xsq"]
        m_red = pl.add("vector", lambda v: v.tensor_reduce(
            qs["m"][0:PN, 0:nb].unsqueeze(2),
            xin_ap.rearrange("p (a c) -> p a c", c=16),
            axis=AX.X, op=ALU.max, apply_absolute_value=True),
            waits=list(wait_v), inc="v")
        m_sq = pl.add("scalar", lambda a: a.activation(
            xsq[0:PN, 0:F], xin_ap, AF.Square), waits=list(wait_a), inc="a")
        m_sg = pl.add("scalar", lambda a: a.activation(
            qs["sgn"][0:PN, 0:F], xin_ap, AF.Sign), inc="a")
        m_cl1 = pl.add("scalar", lambda a: a.activation(
            qs["r"][0:PN, 0:nb], qs["m"][0:PN, 0:nb], AF.Relu, bias=-6e-12),
            waits=W(("v", m_red)), inc="a")
        pl.add("scalar", lambda a: a.activation(
            qs["sc2"][0:PN, 0:nb], qs["r"][0:PN, 0:nb], AF.Copy,
            scale=1.0 / 12.0, bias=5e-13), waits=W(("a", m_cl1, 1)), inc="a")
        m_cl = pl.add("scalar", lambda a: a.activation(
            qs["m"][0:PN, 0:nb], qs["r"][0:PN, 0:nb], AF.Copy, bias=6e-12),
            waits=W(("a", m_cl1, 1)), inc="a")
        m_rc = pl.add("vector", lambda v: v.reciprocal(
            qs["r"][0:PN, 0:nb], qs["m"][0:PN, 0:nb]),
            waits=W(("a", m_cl)), inc="v")
        m_rsq = pl.add("scalar", lambda a: a.activation(
            (qs["rsqb"] if lowp else qs["rsq"])[0:PN, 0:nb],
            qs["r"][0:PN, 0:nb], AF.Square, scale=6.0),
            waits=W(("v", m_rc)), inc="a")
        return (m_red, m_sq, m_sg, m_rsq)

    def qtail(qs, xout_ap, PN, F, marks, wait_v=(), lowp=False):
        nb = F // 16
        m_red, m_sq, m_sg, m_rsq = marks
        xsq = qs["xsqb"] if lowp else qs["xsq"]
        acc = qs["accb"] if lowp else qs["acc"]
        b7 = qs["b7b"] if lowp else qs["b7"]
        rsq = qs["rsqb"] if lowp else qs["rsq"]
        rsq_bc = rsq[0:PN, 0:nb].unsqueeze(2).broadcast_to([PN, nb, 16])
        sc2_bc = qs["sc2"][0:PN, 0:nb].unsqueeze(2).broadcast_to([PN, nb, 16])
        pl.add("vector", lambda v: v.tensor_tensor(
            xsq[0:PN, 0:F].rearrange("p (a c) -> p a c", c=16),
            xsq[0:PN, 0:F].rearrange("p (a c) -> p a c", c=16), rsq_bc,
            op=ALU.mult),
            waits=W(("a", m_sq), ("a", m_rsq)) + list(wait_v), inc="v")
        pl.add("vector", lambda v: v.tensor_scalar(
            acc[0:PN, 0:F], xsq[0:PN, 0:F], TAU[0], None, op0=ALU.is_gt),
            inc="v")
        for j in (1, 2, 3):
            pl.add("vector", lambda v, j=j: v.scalar_tensor_tensor(
                acc[0:PN, 0:F], xsq[0:PN, 0:F], TAU[j], acc[0:PN, 0:F],
                op0=ALU.is_gt, op1=ALU.add), inc="v")
        pl.add("vector", lambda v: v.tensor_scalar(
            b7[0:PN, 0:F], xsq[0:PN, 0:F], TAU[6], None, op0=ALU.is_gt),
            inc="v")
        for j in (6, 4, 5):
            pl.add("vector", lambda v, j=j: v.scalar_tensor_tensor(
                b7[0:PN, 0:F], xsq[0:PN, 0:F], TAU[j], b7[0:PN, 0:F],
                op0=ALU.is_gt, op1=ALU.add), inc="v")
        pl.add("vector", lambda v: v.scalar_tensor_tensor(
            acc[0:PN, 0:F], b7[0:PN, 0:F], 2.0, acc[0:PN, 0:F],
            op0=ALU.mult, op1=ALU.add), inc="v")
        pl.add("vector", lambda v: v.tensor_tensor(
            acc[0:PN, 0:F], acc[0:PN, 0:F], qs["sgn"][0:PN, 0:F],
            op=ALU.mult), waits=W(("a", m_sg)), inc="v")
        return pl.add("vector", lambda v: v.tensor_tensor(
            xout_ap.rearrange("p (a c) -> p a c", c=16),
            acc[0:PN, 0:F].rearrange("p (a c) -> p a c", c=16), sc2_bc,
            op=ALU.mult), inc="v")

    def quant_tile(xin_ap, xout_ap, PN, F, wait_v=(), wait_a=(), lowp=False):
        marks = qhead(QS0, xin_ap, PN, F, wait_v=wait_v, wait_a=wait_a,
                      lowp=lowp)
        return qtail(QS0, xout_ap, PN, F, marks, lowp=lowp)

    def ln_tile(x_ap, out_ap, scr, wait_v=(), wait_a=()):
        ms = pl.add("vector", lambda v: v.reduce_sum(s_col[:, 0:1], x_ap, axis=AX.X),
                    waits=list(wait_v), inc="v")
        ma = pl.add("scalar", lambda a: a.activation(scr, x_ap, AF.Square,
                                                     accum_out=s_col[:, 1:2]),
                    waits=list(wait_a), inc="a")

        mam1 = pl.add("scalar", lambda a: a.activation(
            s_col[:, 2:3], s_col[:, 0:1], AF.Copy, scale=1.0 / D),
            waits=W(("v", ms)), inc="a")
        mam = pl.add("scalar", lambda a: a.activation(
            s_col[:, 3:4], s_col[:, 2:3], AF.Square),
            waits=W(("a", mam1, 1)), inc="a")
        mr = pl.add("vector", lambda v: v.scalar_tensor_tensor(
            s_col[:, 4:5], s_col[:, 1:2], 1.0 / D, s_col[:, 3:4],
            op0=ALU.mult, op1=ALU.subtract), waits=W(("a", mam)), inc="v")

        mrs1 = pl.add("scalar", lambda a: a.activation(
            s_col[:, 5:6], s_col[:, 4:5], AF.Ln, bias=1e-5),
            waits=W(("v", mr)), inc="a")
        mrs = pl.add("scalar", lambda a: a.activation(
            s_col[:, 6:7], s_col[:, 5:6], AF.Exp, scale=-0.5),
            waits=W(("a", mrs1, 1)), inc="a")
        return pl.add("vector", lambda v: v.tensor_scalar(
            out_ap, x_ap, s_col[:, 2:3], s_col[:, 6:7],
            op0=ALU.subtract, op1=ALU.mult), waits=W(("a", mrs)), inc="v")

    AR.reset(QFR)
    wp_stg2 = [AR.carve(IF_) for _ in range(2)]
    wp_stg = wp_stg2[0]
    wp_bf2 = [AR.carve(IF_, BF16) for _ in range(2)]
    wp_bf = wp_bf2[0]
    wp_t = AR.carve(8 * P, BF16)
    def mk_qs():
        xsq = AR.carve(IF_)
        acc = AR.carve(IF_)
        b7 = AR.carve(IF_)
        sgn = AR.carve(IF_, BF16)
        sml = AR.carve(4 * (IF_ // 16))
        return dict(m=sml[:, 0:176], r=sml[:, 176:352],
                    rsq=sml[:, 352:528], sc2=sml[:, 528:704],
                    xsq=xsq, acc=acc, b7=b7, sgn=sgn,
                    xsqb=xsq.bitcast(BF16), accb=acc.bitcast(BF16),
                    b7b=b7.bitcast(BF16), rsqb=sml[:, 352:528].bitcast(BF16))
    QSW2 = [mk_qs(), mk_qs()]
    s_xt3 = AR.carve(D)
    s_xtln = AR.carve(D, BF16)
    s_xtT = sb("s_xtT", [P, 3 * DT], BF16)
    wp_scr = AR.carve(D)

    for dst, src in [(s_ident[:], dident), (s_hmask[:], dhmask), (s_taps[:], dtaps),
                     (s_bconv[:], dbconv), (s_bdt[:], dbdt), (s_bin[:], dbin),
                     (s_bxp[:], dbxp), (s_xt3[0:3, :], dxtail), (s_A[:], dalog)]:
        dma(dst, src, "wpl")
    dma(q_acc[0:NS, 0:D], dwdt, "wpl")
    dma(q_b7[:, 0:DT * 80].rearrange("p (kt c) -> p kt c", kt=DT),
        dwxp.rearrange("(kt p) c -> p kt c", p=P), "wpl")

    dpr = dma(s_prk[:], dprty, "wpl")
    d_consts = pl.cnt["wpl"]

    def wp_small(v):
        for ci, cv in enumerate(CONSTS):
            v.memset(s_cst[:, ci:ci + 1], cv)
        v.tensor_copy(s_wxp[:], q_b7[:, 0:DT * 80])
        v.tensor_copy(s_wdt[:], q_acc[0:NS, 0:D])
        v.memset(s_ones1[:], 1.0)
        v.memset(s_hscol[:], 0.0)
        v.memset(s_pcol[:], 0.0)
        return v.memset(s_utail[:], 0.0)
    m_wpsm = pl.add("vector", wp_small, waits=W(("wpl", d_consts)), inc="v")
    m_eA = pl.add("scalar", lambda a: a.activation(s_A[:], s_A[:], AF.Exp),
                  waits=W(("wpl", d_consts), ("v", m_wpsm)), inc="a")
    m_negA = pl.add("vector", lambda v: v.tensor_scalar_mul(s_A[:], s_A[:], -1.0),
                    waits=W(("a", m_eA)), inc="v")

    m_fli = pl.add("vector", lambda v: (v.memset(s_fli[0:1, 0:64], 0),
                                        v.memset(s_fli[0:1, 64:128], 1))[-1], inc="v")

    def zero_flag(sync):
        sync.reg_load(prty_reg, s_prk[0:1, 0:1])
        idx = sync.snap(prty_reg, min_val=0, max_val=1)
        return sync.dma_start(out=sh_flagv[idx], in_=s_fli[0:1, 0:64])
    pl.add("sync", zero_flag, waits=W(("wpl", dpr), ("v", m_fli)), inc="fz")

    last_v = m_fli
    for k in range(DT):
        dL = dma(wp_stg[:, 0:2 * D], dwin_f[k * P:(k + 1) * P, :], "wstg",
                 waits=W(("v", last_v)))
        last_v = pl.add("vector", lambda v: v.tensor_copy(
            wp_bf[:, 0:2 * D], wp_stg[:, 0:2 * D]),
            waits=W(("wstg", dL), ("wsto", pl.cnt.get("wsto", 0))), inc="v")
        dma(g_win[k * P:(k + 1) * P, :], wp_bf[:, 0:2 * D], "wsto", waits=W(("v", last_v)))
    for k in range(DT):
        dL = dma(wp_stg[:, 0:D], dwout_f[k * P:(k + 1) * P, :], "wstg",
                 waits=W(("v", last_v)))
        last_v = pl.add("vector", lambda v: v.tensor_copy(
            wp_bf[:, 0:D], wp_stg[:, 0:D]),
            waits=W(("wstg", dL), ("wsto", pl.cnt.get("wsto", 0))), inc="v")
        dma(g_wout[k * P:(k + 1) * P, :], wp_bf[:, 0:D], "wsto", waits=W(("v", last_v)))

    tj = [0]
    tstores = []

    def tstore(dst_ap, src_ps, mt):
        j = tj[0]
        tj[0] += 1
        sl = (j % 8) * P
        wts = [("t", mt)]
        if j >= 8:
            wts.append(("wsto", tstores[j - 8]))
        mv = pl.add("scalar", lambda a, sl=sl, src_ps=src_ps: a.activation(
            wp_t[:, sl:sl + P], src_ps, AF.Copy), waits=W(*wts), inc="a")
        ds = dma(dst_ap, wp_t[:, sl:sl + P], "wsto", waits=W(("a", mv)))
        tstores.append(ds)
        return mv

    jobs = []
    for gdst, gsrc in ((g_wgT, dwg_f), (g_wuT, dwu_f)):
        for it in range(IT):
            jobs.append((gdst, gsrc[it * P:(it + 1) * P, :], D, it, None))
    for k in range(DT):
        jobs.append((g_wdT, dwd_f[k * P:(k + 1) * P, :], IF_, None, k))
    job_mq = [None] * len(jobs)
    job_tr = [0] * len(jobs)
    dLs = [None] * len(jobs)
    heads = [None] * len(jobs)

    def issue_load(ji):
        gdst, gsrc, F, it, kk = jobs[ji]
        wl = W(("v", m_fli))
        if ji >= 2:
            wl += W(("v", heads[ji - 2][0]), ("a", heads[ji - 2][3]))
        dLs[ji] = dma(wp_stg2[ji % 2][:, 0:F], gsrc, "wstg", waits=wl)
    issue_load(0)
    issue_load(1)
    heads[0] = qhead(QSW2[0], wp_stg2[0][:, 0:jobs[0][2]], P, jobs[0][2],
                     wait_v=W(("wstg", dLs[0])), wait_a=W(("wstg", dLs[0])))
    for ji, (gdst, gsrc, F, it, kk) in enumerate(jobs):
        jb = ji % 2
        if ji + 1 < len(jobs):
            F1 = jobs[ji + 1][2]
            heads[ji + 1] = qhead(QSW2[(ji + 1) % 2],
                                  wp_stg2[(ji + 1) % 2][:, 0:F1], P, F1,
                                  wait_v=W(("wstg", dLs[ji + 1])),
                                  wait_a=W(("wstg", dLs[ji + 1])))
        mq = qtail(QSW2[jb], wp_bf2[jb][:, 0:F], P, F, heads[ji],
                   wait_v=W(("t", job_tr[ji - 2] if ji >= 2 else 0)))
        job_mq[ji] = mq
        last_w = ("v", mq)
        if ji + 2 < len(jobs):
            issue_load(ji + 2)
        nt = F // P
        for hk in range(nt):
            wt = W(last_w)
            if pl.cnt.get("t", 0) == 0:
                wt += W(("wpl", d_consts))
            mt = pl.add("tensor", lambda te, hk=hk, jb=jb: te.transpose(
                psbf(hk % 2)[0:P, 0:P], wp_bf2[jb][:, hk * P:(hk + 1) * P],
                s_ident[:]), waits=wt, inc="t")
            if it is not None:
                dst = gdst[hk * P:(hk + 1) * P, it * P:(it + 1) * P]
            else:
                dst = gdst[hk * P:(hk + 1) * P, kk * P:(kk + 1) * P]
            last_w = ("a", tstore(dst, psbf(hk % 2)[0:P, 0:P], mt))
        job_tr[ji] = pl.cnt["t"]
    dB = dma(wp_stg[0:1, 0:D], dbout_row, "wstg",
             waits=W(last_w, ("v", job_mq[-1])))
    m_br = pl.add("vector", lambda v: v.tensor_copy(s_boutr[:], wp_stg[0:1, 0:D]),
                  waits=W(("wstg", dB)), inc="v")

    m_ts = pl.add("vector", lambda v: v.reduce_sum(s_col[0:3, 0:1], s_xt3[0:3, :],
                                                   axis=AX.X),
                  waits=W(("wpl", d_consts), ("v", m_br)), inc="v")
    m_ta = pl.add("scalar", lambda a: a.activation(wp_scr[0:3, :], s_xt3[0:3, :],
                                                   AF.Square, accum_out=s_col[0:3, 1:2]),
                  waits=W(("wpl", d_consts)), inc="a")

    m_tam1 = pl.add("scalar", lambda a: a.activation(
        s_col[0:3, 2:3], s_col[0:3, 0:1], AF.Copy, scale=1.0 / D),
        waits=W(("v", m_ts)), inc="a")
    m_tam = pl.add("scalar", lambda a: a.activation(
        s_col[0:3, 3:4], s_col[0:3, 2:3], AF.Square),
        waits=W(("a", m_tam1, 1)), inc="a")
    m_tr = pl.add("vector", lambda v: v.scalar_tensor_tensor(
        s_col[0:3, 4:5], s_col[0:3, 1:2], 1.0 / D, s_col[0:3, 3:4],
        op0=ALU.mult, op1=ALU.subtract), waits=W(("a", m_tam)), inc="v")

    m_trs1 = pl.add("scalar", lambda a: a.activation(
        s_col[0:3, 5:6], s_col[0:3, 4:5], AF.Ln, bias=1e-5),
        waits=W(("v", m_tr)), inc="a")
    m_trs = pl.add("scalar", lambda a: a.activation(
        s_col[0:3, 6:7], s_col[0:3, 5:6], AF.Exp, scale=-0.5),
        waits=W(("a", m_trs1, 1)), inc="a")
    m_tap = pl.add("vector", lambda v: v.tensor_scalar(
        s_xtln[0:3, :], s_xt3[0:3, :], s_col[0:3, 2:3], s_col[0:3, 6:7],
        op0=ALU.subtract, op1=ALU.mult), waits=W(("a", m_trs)), inc="v")
    last_ev = m_tap
    for k in range(DT):
        mt = pl.add("tensor", lambda te, k=k: te.transpose(
            psbf(2)[0:P, 0:3], s_xtln[0:3, k * P:(k + 1) * P], s_ident[0:3, 0:3]),
            waits=W(("v", last_ev)), inc="t")
        last_ev = pl.add("vector", lambda v, k=k: v.tensor_copy(
            s_xtT[:, 3 * k:3 * k + 3], psbf(2)[0:P, 0:3]), waits=W(("t", mt)), inc="v")
    m_xtT = last_ev

    pl.barrier()

    AR.reset(FR)
    s_x = [AR.carve(D) for _ in range(TT4)]
    s_xln = [AR.carve(D, BF16) for _ in range(2)]
    s_winK_all = AR.carve(DT * 2 * D, BF16)
    s_winK = [s_winK_all[:, k * 2 * D:(k + 1) * 2 * D] for k in range(DT)]
    s_xlnT_all = AR.carve(DT * TC, BF16)
    s_xlnT = [s_xlnT_all[:, k * TC:(k + 1) * TC] for k in range(DT)]
    s_uext = [AR.carve(TC + 4) for _ in range(DT)]
    s_ufin = [AR.carve(TC) for _ in range(DT)]
    s_ubf_all = AR.carve(2 * TC, BF16)
    s_ubf = [s_ubf_all[:, k * TC:(k + 1) * TC] for k in range(2)]
    s_zsil = [AR.carve(TC, BF16) for _ in range(2)]
    s_dtb = [AR.carve(TC) for _ in range(2)]
    s_Pb = [AR.carve(TC) for _ in range(2)]
    s_dtu = [AR.carve(TC, BF16) for _ in range(2)]
    s_xdb = AR.carve(TC)
    s_dtin = AR.carve(TC, BF16)
    s_bbf = AR.carve(TC, BF16)
    s_cbf = AR.carve(TC, BF16)
    s_brep = AR.carve(NS * TC, BF16)
    s_crep1 = AR.carve(NS * TC, BF16)
    s_da = AR.carve(3 * TC)
    s_dbu = AR.carve(NS * TC, BF16)
    s_h = AR.carve(NS * TC, BF16)
    s_tree = AR.carve(8 * TC)
    s_scr = s_tree
    P1_END = AR.mark()

    for c in range(NCH):
        dxm = [dma(s_x[mi], dx[c * TC + mi * P:c * TC + (mi + 1) * P, :], "xl")
               for mi in range(TT4)]
        dwin = [dma(s_winK[k], g_win[k * P:(k + 1) * P, :], "winl", gq=True)
                for k in range(DT)]

        last_cp = None
        for mi in range(TT4):
            wl = W(("xl", dxm[-1]))
            if last_cp is not None:
                wl += W(("a", last_cp))
            mln = ln_tile(s_x[mi], s_xln[mi % 2], s_scr[:, 0:D],
                          wait_v=wl, wait_a=W(("xl", dxm[-1])))
            for g in range(2):
                bank = 2 + (g % 2)
                wt = W(("v", mln))
                if last_cp is not None:
                    wt += W(("a", last_cp))

                def tr4(te, g=g, mi=mi, bank=bank):
                    for j in range(4):
                        r = te.transpose(psbf(bank)[0:P, j * P:(j + 1) * P],
                                         s_xln[mi % 2][:, (g * 4 + j) * P:
                                                       (g * 4 + j + 1) * P],
                                         s_ident[:])
                    return r
                mt = pl.add("tensor", tr4, waits=wt, inc="t")
                last_cp = pl.add("scalar", lambda a, g=g, mi=mi, bank=bank: a.activation(
                    s_xlnT_all.rearrange("p (k l) -> p k l", l=TC)
                    [:, g * 4:(g + 1) * 4, mi * P:(mi + 1) * P],
                    psbf(bank)[0:P, 0:4 * P].rearrange("p (k l) -> p k l", l=P),
                    AF.Copy), waits=W(("t", mt)), inc="a")
        m_xlnT = last_cp
        if c == 0:
            tap("xlnT0", [P, TC], BF16, s_xlnT[0][:], W(("a", m_xlnT)))

        if c == 0:
            prev_a = None
            for m8 in range(DT):
                def u3mm(te, m8=m8):
                    last = None
                    for k in range(DT):
                        last = te.matmul(psf[4][0:P, 0:3],
                                         s_winK[k][:, m8 * P:(m8 + 1) * P],
                                         s_xtT[:, 3 * k:3 * k + 3],
                                         start=(k == 0), stop=(k == DT - 1))
                    return last
                wt = W(("v", m_xtT), ("winl", dwin[-1]))
                if prev_a is not None:
                    wt += W(("a", prev_a))
                mt = pl.add("tensor", u3mm, waits=wt, inc="t")
                prev_a = pl.add("scalar", lambda a, m8=m8: a.activation(
                    s_utail[:, 3 * m8:3 * m8 + 3], psf[4][0:P, 0:3], AF.Identity,
                    bias=s_bin[:, m8:m8 + 1]), waits=W(("t", mt)), inc="a")
            m_umask = pl.add("vector", lambda v: v.tensor_scalar(
                s_utail[:], s_utail[:], s_hmask[:, 0:1], None, op0=ALU.mult),
                waits=W(("a", prev_a)), inc="v")
            tap("ut0", [P, 3 * DT], F32, s_utail[:], W(("v", m_umask)))
            tap("xtT0", [P, 3 * DT], BF16, s_xtT[:], W(("v", m_xtT)))
            tap("win0", [P, 2 * D], BF16, s_winK[0][:], W(("winl", dwin[-1])))

        ev_u = [None] * DT
        ev_z = [None] * DT
        bank_free = {0: None, 1: None}
        for mm in range(16):
            bank = mm % 2

            def mmfn(te, mm=mm, bank=bank):
                last = None
                for k in range(DT):
                    last = te.matmul(psf[bank][:, 0:TC],
                                     s_winK[k][:, mm * P:(mm + 1) * P],
                                     s_xlnT[k][:], start=(k == 0), stop=(k == DT - 1))
                return last
            waits = W(("a", m_xlnT), ("winl", dwin[-1]))
            if bank_free[bank] is not None:
                waits += W(("a", bank_free[bank]))
            mt = pl.add("tensor", mmfn, waits=waits, inc="t")
            if mm < DT:
                k = mm
                tcw = W(("v", m_umask)) if c == 0 else ()
                tcp = pl.add("scalar", lambda a, k=k: a.activation(
                    s_uext[k][:, 0:3], s_utail[:, 3 * k:3 * k + 3], AF.Copy),
                    waits=tcw, inc="a")
                ev_u[k] = pl.add("scalar", lambda a, k=k, bank=bank: a.activation(
                    s_uext[k][:, 3:3 + TC], psf[bank][:, 0:TC], AF.Identity,
                    bias=s_bin[:, k:k + 1]), waits=W(("t", mt)), inc="a")
                bank_free[bank] = ev_u[k]
            else:
                k = mm - DT
                wz = W(("t", mt))
                if k >= 2:
                    wz += W(("zsp", pl.cnt.get("zsp", 0)))
                ev_z[k] = pl.add("scalar", lambda a, k=k, bank=bank: a.activation(
                    s_zsil[k % 2][:], psf[bank][:, 0:TC], AF.Silu,
                    bias=s_bin[:, DT + k:DT + k + 1]), waits=wz, inc="a")
                bank_free[bank] = ev_z[k]
                dma(sp_zs[c * DT + k], s_zsil[k % 2][:], "zsp", waits=W(("a", ev_z[k])))
        for k in range(DT):
            pl.add("vector", lambda v, k=k: v.tensor_copy(
                s_utail[:, 3 * k:3 * k + 3], s_uext[k][:, TC:TC + 3]),
                waits=W(("a", ev_u[k])), inc="v")

        if c == 0:
            tap("uraw0", [P, TC], F32, s_uext[0][:, 3:3 + TC], W(("a", ev_u[0])))
        m_uf = [None] * DT
        for k in range(DT):
            t0 = pl.add("scalar", lambda a, k=k: a.activation(
                s_ufin[k], s_uext[k][:, 0:TC], AF.Copy,
                scale=s_taps[:, 4 * k:4 * k + 1]),
                waits=W(("a", ev_u[k]), ("v", pl.cnt["v"])), inc="a")
            pl.add("vector", lambda v, k=k: v.tensor_scalar(
                s_scr[:, 0:TC], s_uext[k][:, 1:1 + TC], s_taps[:, 4 * k + 1:4 * k + 2],
                None, op0=ALU.mult), waits=W(("a", ev_u[k])), inc="v")
            pl.add("vector", lambda v, k=k: v.scalar_tensor_tensor(
                s_scr[:, TC:2 * TC], s_uext[k][:, 2:2 + TC],
                s_taps[:, 4 * k + 2:4 * k + 3], s_scr[:, 0:TC],
                op0=ALU.mult, op1=ALU.add), inc="v")
            pl.add("vector", lambda v, k=k: v.scalar_tensor_tensor(
                s_scr[:, 0:TC], s_uext[k][:, 3:3 + TC], s_taps[:, 4 * k + 3:4 * k + 4],
                s_scr[:, TC:2 * TC], op0=ALU.mult, op1=ALU.add), inc="v")
            t4 = pl.add("vector", lambda v, k=k: v.tensor_add(
                s_ufin[k], s_ufin[k], s_scr[:, 0:TC]), waits=W(("a", t0)), inc="v")
            m_uf[k] = pl.add("scalar", lambda a, k=k: a.activation(
                s_ufin[k], s_ufin[k], AF.Silu, bias=s_bconv[:, k:k + 1]),
                waits=W(("v", t4)), inc="a")

        if c == 0:
            tap("ufin0", [P, TC], F32, s_ufin[0][:], W(("a", m_uf[0])))
        m_uc = [None] * DT
        xp_mark = [None] * DT
        for k in range(DT):
            wcast = W(("a", m_uf[k]))
            if k >= 2:
                wcast += W(("t", xp_mark[k - 2]))
            m_uc[k] = pl.add("scalar", lambda a, k=k: a.activation(
                s_ubf[k % 2], s_ufin[k], AF.Copy), waits=wcast, inc="a")
            xp_mark[k] = pl.add("tensor", lambda te, k=k: te.matmul(
                psf[4][0:80, 0:TC], s_wxp[:, 80 * k:80 * (k + 1)],
                s_ubf[k % 2], start=(k == 0), stop=(k == DT - 1)),
                waits=W(("a", m_uc[k])), inc="t")
        mt_xp = xp_mark[-1]
        m_xdb = pl.add("scalar", lambda a: a.activation(
            s_xdb[0:80, :], psf[4][0:80, 0:TC], AF.Identity, bias=s_bxp[:, 0:1]),
            waits=W(("t", mt_xp)), inc="a")
        if c == 0:
            tap("xdb0", [80, TC], F32, s_xdb[0:80, :], W(("a", m_xdb)))
        m_dtin = pl.add("scalar", lambda a: a.activation(
            s_dtin[0:NS, :], s_xdb[0:NS, :], AF.Copy), inc="a")
        m_bbf = pl.add("scalar", lambda a: a.activation(
            s_bbf[0:NS, :], s_xdb[32:48, :], AF.Copy), inc="a")
        m_cbf = pl.add("scalar", lambda a: a.activation(
            s_cbf[0:NS, :], s_xdb[64:80, :], AF.Copy), inc="a")
        dbr = dma(sp_brow[c][0:1, :].rearrange("p (a l) -> p a l", a=NS),
                  s_bbf[0:NS, :], "rows", waits=W(("a", m_bbf)))
        dcr = dma(sp_crow[c][0:1, :].rearrange("p (a l) -> p a l", a=NS),
                  s_cbf[0:NS, :], "rows", waits=W(("a", m_cbf)))
        dbrep = dma(s_brep, sp_brow[c][0:1, :].broadcast_to([P, NS * TC]), "reps",
                    waits=W(("rows", dcr)))
        dcrep = dma(s_crep1, sp_crow[c][0:1, :].broadcast_to([P, NS * TC]), "reps",
                    waits=W(("rows", dcr)))

        prev_sp = None
        prev_spill = None
        prev_scan = [None] * NS
        for k in range(DT):
            rb = k % 2

            def dtmm(te, k=k):
                return te.matmul(psf[5][:, 0:TC], s_wdt[:, k * P:(k + 1) * P],
                                 s_dtin[0:NS, :], start=True, stop=True)
            wt = W(("a", m_dtin))
            if prev_sp is not None:
                wt += W(("a", prev_sp))
            mt = pl.add("tensor", dtmm, waits=wt, inc="t")
            wv = W(("t", mt), ("v", pl.cnt["v"]))
            if prev_spill is not None:
                wv += W(("psp", prev_spill))
            a1 = pl.add("scalar", lambda a, k=k, rb=rb: a.activation(
                s_Pb[rb], psf[5][:, 0:TC], AF.Abs, bias=s_bdt[:, k:k + 1]),
                waits=wv, inc="a")
            a2 = pl.add("scalar", lambda a, rb=rb: a.activation(
                s_dtb[rb], s_Pb[rb], AF.Exp, scale=-1.0),
                waits=W(("a", a1, 1)), inc="a")
            a3 = pl.add("scalar", lambda a, rb=rb: a.activation(
                s_Pb[rb], s_dtb[rb], AF.Ln, bias=1.0),
                waits=W(("a", a2, 1)), inc="a")
            a4 = pl.add("scalar", lambda a, k=k, rb=rb: a.activation(
                s_dtb[rb], psf[5][:, 0:TC], AF.Relu, bias=s_bdt[:, k:k + 1]),
                waits=W(("a", a3, 1)), inc="a")
            prev_sp = a4
            v1 = pl.add("vector", lambda v, rb=rb: v.tensor_add(
                s_dtb[rb], s_dtb[rb], s_Pb[rb]), waits=W(("a", a4)), inc="v")
            v2 = pl.add("vector", lambda v, k=k, rb=rb: v.tensor_mul(
                s_dtu[rb], s_dtb[rb], s_ufin[k]), inc="v")
            if c == 0 and k == 0:
                tap("dt0", [P, TC], F32, s_dtb[0][:], W(("v", v2)))
            v3 = pl.add("vector", lambda v, k=k, rb=rb: v.tensor_tensor_scan(
                s_Pb[rb], s_dtb[rb], s_dtb[rb], s_pcol[:, k:k + 1],
                op0=ALU.add, op1=ALU.bypass), inc="v")
            pl.add("scalar", lambda a, k=k, rb=rb: a.activation(
                s_pcol[:, k:k + 1], s_Pb[rb][:, TC - 1:TC], AF.Copy),
                waits=W(("v", v3)), inc="a")
            prev_spill = dma(sp_P[c * DT + k], s_Pb[rb], "psp", waits=W(("v", v3)))
            v5a = pl.add("vector", lambda v, rb=rb: v.tensor_tensor(
                s_dbu[:, 0:4 * TC].rearrange("p (a l) -> p a l", a=4),
                s_dtu[rb].unsqueeze(1).broadcast_to([P, 4, TC]),
                s_brep[:, 0:4 * TC].rearrange("p (a l) -> p a l", a=4),
                op=ALU.mult), waits=W(("v", v2), ("reps", dbrep)), inc="v")
            v5 = pl.add("vector", lambda v, rb=rb: v.tensor_tensor(
                s_dbu[:, 4 * TC:].rearrange("p (a l) -> p a l", a=NS - 4),
                s_dtu[rb].unsqueeze(1).broadcast_to([P, NS - 4, TC]),
                s_brep[:, 4 * TC:].rearrange("p (a l) -> p a l", a=NS - 4),
                op=ALU.mult), inc="v")
            scan_marks = [None] * NS
            for n in range(NS):
                pg = n % 3
                wda = W(("v", v1))
                if n >= 3:
                    wda += W(("v", scan_marks[n - 3]))
                elif prev_scan[NS - 3 + n] is not None:
                    wda += W(("v", prev_scan[NS - 3 + n]))
                mda = pl.add("scalar", lambda a, k=k, n=n, pg=pg, rb=rb: a.activation(
                    s_da[:, pg * TC:(pg + 1) * TC], s_dtb[rb], AF.Exp,
                    scale=s_A[:, NS * k + n:NS * k + n + 1]), waits=wda, inc="a")
                scan_marks[n] = pl.add("vector", lambda v, k=k, n=n, pg=pg: v.tensor_tensor_scan(
                    s_h[:, n * TC:(n + 1) * TC], s_da[:, pg * TC:(pg + 1) * TC],
                    s_dbu[:, n * TC:(n + 1) * TC],
                    s_hscol[:, NS * k + n:NS * k + n + 1],
                    op0=ALU.mult, op1=ALU.add),
                    waits=W(("a", mda), ("v", v5a if n < 4 else v5)), inc="v")
            pl.add("scalar", lambda a, k=k: a.activation(
                s_hscol[:, NS * k:NS * (k + 1)].rearrange("p (n o) -> p n o", o=1),
                s_h.rearrange("p (n l) -> p n l", l=TC)[:, :, TC - 1:TC],
                AF.Copy), waits=W(("v", scan_marks[NS - 1])), inc="a")
            if c == 0 and k == 0:
                tap("dbu0", [P, NS * TC], BF16, s_dbu[:], W(("v", scan_marks[0])))
                tap("h0", [P, NS * TC], BF16, s_h[:], W(("v", pl.cnt["v"])))
            prev_scan = scan_marks
            vz = pl.add("vector", lambda v: v.tensor_tensor(
                s_dbu, s_h, s_crep1, op=ALU.mult), waits=W(("reps", dcrep)), inc="v")
            tb = s_tree.bitcast(BF16)
            vt = pl.add("vector", lambda v, tb=tb: v.tensor_tensor(
                tb[:, 0:8 * TC].rearrange("p (a l) -> p a l", l=TC),
                s_dbu.rearrange("p (a b l) -> p a b l", b=2, l=TC)[:, :, 0, :],
                s_dbu.rearrange("p (a b l) -> p a b l", b=2, l=TC)[:, :, 1, :],
                op=ALU.add), waits=W(("v", vz)), inc="v")
            vt = pl.add("vector", lambda v, tb=tb: v.tensor_tensor(
                tb[:, 8 * TC:12 * TC].rearrange("p (a l) -> p a l", l=TC),
                tb[:, 0:8 * TC].rearrange("p (a b l) -> p a b l", b=2, l=TC)
                [:, :, 0, :],
                tb[:, 0:8 * TC].rearrange("p (a b l) -> p a b l", b=2, l=TC)
                [:, :, 1, :], op=ALU.add), inc="v")
            vt = pl.add("vector", lambda v, tb=tb: v.tensor_tensor(
                tb[:, 12 * TC:14 * TC].rearrange("p (a l) -> p a l", l=TC),
                tb[:, 8 * TC:12 * TC].rearrange("p (a b l) -> p a b l", b=2, l=TC)
                [:, :, 0, :],
                tb[:, 8 * TC:12 * TC].rearrange("p (a b l) -> p a b l", b=2, l=TC)
                [:, :, 1, :], op=ALU.add), inc="v")
            vt = pl.add("vector", lambda v, tb=tb: v.tensor_tensor(
                s_tree[:, 7 * TC:8 * TC], tb[:, 12 * TC:13 * TC],
                tb[:, 13 * TC:14 * TC], op=ALU.add), inc="v")
            dma(sp_y[c * DT + k], s_tree[:, 7 * TC:8 * TC], "ysp",
                waits=W(("v", vt)))
        pl.barrier()

    def store_state(sync):
        idx = sync.snap(prty_reg, min_val=0, max_val=1)
        return sync.dma_start(out=sh_statev[idx], in_=s_hscol[:])
    dhs = pl.add("sync", store_state, waits=W(("v", pl.cnt["v"]),
                                              ("a", pl.cnt["a"])), inc="hss")

    def set_flag(sync):
        idx = sync.snap(prty_reg, min_val=0, max_val=1)
        return sync.dma_start(out=sh_flagv[idx], in_=s_fli[0:1, 64:128])
    pl.add("sync", set_flag, waits=W(("hss", dhs)), inc="hsf")

    def spin_and_load(sync):
        with (nc.sync.register(name="spin_fl") as flr,
              nc.sync.register(name="spin_cond") as cnd):
            sync.reg_mov(cnd, 1)
            with sync.While(cnd):
                sync.sem_clear(spin_sem)
                sync.dma_start(out=s_spfl[:], in_=sh_flag[0:1, :]
                               ).then_inc(spin_sem, 16)
                sync.wait_ge(spin_sem, 16)
                sync.reg_load(flr, s_spfl[0:1, 0:1])
                sync.reg_alu(cnd, flr, 0, ALU.is_equal)
        return sync.dma_start(out=s_hb[:], in_=sh_state[0:P, :])
    dhb = pl.add("sync", spin_and_load, inc="hbl")
    m_hb = pl.add("vector", lambda v: v.tensor_scalar(
        s_hb[:], s_hb[:], s_hmask[:, 0:1], None, op0=ALU.mult),
        waits=W(("hbl", dhb)), inc="v")

    pl.barrier()

    AR.reset(QFR)
    s2_x = [AR.carve(D) for _ in range(TT4)]
    s2_yz_all = AR.carve(DT * TC, BF16)
    s2_yz = [s2_yz_all[:, k * TC:(k + 1) * TC] for k in range(DT)]
    s2_xln2 = [AR.carve(D) for _ in range(2)]
    s2_xq = [AR.carve(D, BF16) for _ in range(2)]
    s2_xqT_all = AR.carve(DT * TC, BF16)
    s2_xqT = [s2_xqT_all[:, k * TC:(k + 1) * TC] for k in range(DT)]
    s2_wo_all = AR.carve(DT * 256, BF16)
    s2_wo = [s2_wo_all[:, k * 256:(k + 1) * 256] for k in range(DT)]
    s2_hqT_all = AR.carve(IT * TC, BF16)
    s2_hqT = [s2_hqT_all[:, k * TC:(k + 1) * TC] for k in range(IT)]
    s2_wg2 = [AR.carve(DT * 384, BF16) for _ in range(2)]
    s2_wu2 = [AR.carve(DT * 384, BF16) for _ in range(2)]
    OV = AR.mark()
    s2_Pl = [AR.carve(TC) for _ in range(2)]
    s2_yl = [AR.carve(TC) for _ in range(2)]
    s2_zl = [AR.carve(TC, BF16) for _ in range(2)]
    s2_crep = AR.carve(NS * TC, BF16)
    s2_E = AR.carve(2 * TC, BF16)
    s2_z2 = AR.carve(NS * TC, BF16)
    s2_tree = AR.carve(8 * TC)
    s2_y = AR.carve(TC)
    AR.reset(OV)
    s2_wd2 = [AR.carve(IT * 256, BF16) for _ in range(2)]
    hffn_buf = [AR.carve(IF_, BF16) for _ in range(4)]
    hq_sl = [AR.carve(1408, BF16) for _ in range(2)]
    s2_gs2 = [AR.carve(384, BF16) for _ in range(2)]


    for c in range(NCH):
        dxm = [dma(s2_x[mi], dx[c * TC + mi * P:c * TC + (mi + 1) * P, :], "x2l")
               for mi in range(TT4)]
        dcrep = dma(s2_crep, sp_crow[c][0:1, :].broadcast_to([P, NS * TC]), "crep2")

        m_yz = [None] * DT
        for k in range(DT):
            rb = k % 2
            tw = W(("v", m_yz[k - 2])) if k >= 2 else ()
            dPl = dma(s2_Pl[rb], sp_P[c * DT + k], "trio", waits=tw)
            dyl = dma(s2_yl[rb], sp_y[c * DT + k], "trio")
            dzl = dma(s2_zl[rb], sp_zs[c * DT + k], "trio")
            em = [None] * NS
            for n in range(NS):
                pg = n % 2
                we = W(("trio", dzl), ("v", m_hb))
                if n >= 2:
                    we += W(("v", em[n - 2]))
                me = pl.add("scalar", lambda a, k=k, n=n, pg=pg, rb=rb: a.activation(
                    s2_E[:, pg * TC:(pg + 1) * TC], s2_Pl[rb], AF.Exp,
                    scale=s_A[:, NS * k + n:NS * k + n + 1]), waits=we, inc="a")
                em[n] = pl.add("vector", lambda g, k=k, n=n, pg=pg:
                    g.scalar_tensor_tensor(
                        s2_z2[:, n * TC:(n + 1) * TC],
                        s2_E[:, pg * TC:(pg + 1) * TC],
                        s_hb[:, NS * k + n:NS * k + n + 1],
                        s2_crep[:, n * TC:(n + 1) * TC],
                        op0=ALU.mult, op1=ALU.mult),
                    waits=W(("a", me), ("crep2", dcrep), ("v", m_hb)), inc="v")
            tb2 = s2_tree.bitcast(BF16)
            vt = pl.add("vector", lambda v, tb2=tb2: v.tensor_tensor(
                tb2[:, 0:8 * TC].rearrange("p (a l) -> p a l", l=TC),
                s2_z2.rearrange("p (a b l) -> p a b l", b=2, l=TC)[:, :, 0, :],
                s2_z2.rearrange("p (a b l) -> p a b l", b=2, l=TC)[:, :, 1, :],
                op=ALU.add), inc="v")
            vt = pl.add("vector", lambda v, tb2=tb2: v.tensor_tensor(
                tb2[:, 8 * TC:12 * TC].rearrange("p (a l) -> p a l", l=TC),
                tb2[:, 0:8 * TC].rearrange("p (a b l) -> p a b l", b=2, l=TC)
                [:, :, 0, :],
                tb2[:, 0:8 * TC].rearrange("p (a b l) -> p a b l", b=2, l=TC)
                [:, :, 1, :], op=ALU.add), inc="v")
            vt = pl.add("vector", lambda v, tb2=tb2: v.tensor_tensor(
                tb2[:, 12 * TC:14 * TC].rearrange("p (a l) -> p a l", l=TC),
                tb2[:, 8 * TC:12 * TC].rearrange("p (a b l) -> p a b l", b=2,
                                                 l=TC)[:, :, 0, :],
                tb2[:, 8 * TC:12 * TC].rearrange("p (a b l) -> p a b l", b=2,
                                                 l=TC)[:, :, 1, :],
                op=ALU.add), inc="v")
            vy = pl.add("vector", lambda v, rb=rb, tb2=tb2: v.scalar_tensor_tensor(
                s2_y, tb2[:, 12 * TC:13 * TC], 1.0, tb2[:, 13 * TC:14 * TC],
                op0=ALU.bypass, op1=ALU.add, accum_out=None),
                waits=W(("trio", dzl)), inc="v")
            vy = pl.add("vector", lambda v, rb=rb: v.tensor_tensor(
                s2_y, s2_y, s2_yl[rb], op=ALU.add), inc="v")
            m_yz[k] = pl.add("vector", lambda v, k=k, rb=rb: v.tensor_tensor(
                s2_yz[k], s2_y, s2_zl[rb], op=ALU.mult),
                waits=W(("trio", dzl)), inc="v")

        m_res1 = {}
        for wq4 in range(4):
            dwo = [dma(s2_wo[k], g_wout[k * P:(k + 1) * P, wq4 * 256:(wq4 + 1) * 256],
                       "wol", waits=W(("t", pl.cnt.get("t", 0))) if wq4 > 0 else ())
                   for k in range(DT)]
            for mi in range(TT4):
                def womm(te, mi=mi, wq4=wq4, dwo=dwo):
                    te.wait_ge(sems["wol"], dwo[-1])
                    for k in range(DT):
                        te.matmul(psf[mi][:, 0:256], s2_yz[k][:, mi * P:(mi + 1) * P],
                                  s2_wo[k], start=(k == 0), stop=False)
                    return te.matmul(psf[mi][:, 0:256], s_ones1[0:1, :],
                                     s_boutr[0:1, wq4 * 256:(wq4 + 1) * 256],
                                     start=False, stop=True)
                wt = W(("v", m_yz[DT - 1]))
                prev = m_res1.get((wq4 - 1, mi))
                if prev is not None:
                    wt += W(("v", prev))
                mt = pl.add("tensor", womm, waits=wt, inc="t")
                m_res1[(wq4, mi)] = pl.add("vector",
                    lambda v, mi=mi, wq4=wq4: v.tensor_tensor(
                        s2_x[mi][:, wq4 * 256:(wq4 + 1) * 256],
                        s2_x[mi][:, wq4 * 256:(wq4 + 1) * 256],
                        psf[mi][:, 0:256], op=ALU.add),
                    waits=W(("t", mt), ("x2l", dxm[-1])), inc="v")

        if TAPS and c == 0:
            tap("xmid0", [P, D], F32, s2_x[0][:], W(("v", m_res1[(3, 0)])))
            tap("xmid3", [P, D], F32, s2_x[3][:], W(("v", m_res1[(3, 3)])))
            tap("yz0", [P, TC], BF16, s2_yz[0][:, 0:TC], W(("v", m_yz[0])))
        m_xqT_mi = [None] * TT4
        for mi in range(TT4):
            rb = mi % 2
            wv = W(("v", m_res1[(3, mi)]))
            mln = ln_tile(s2_x[mi], s2_xln2[rb], s2_xln2[1 - rb][:, 0:D],
                          wait_v=wv, wait_a=wv)
            mq = quant_tile(s2_xln2[rb], s2_xq[rb], P, D,
                            wait_v=W(("v", mln), ("t", pl.cnt.get("t", 0))),
                            wait_a=W(("v", mln)))
            if TAPS and c == 0 and mi == 0:
                tap("xq0", [P, D], BF16, s2_xq[0][:], W(("v", mq)))
            last_tc = None
            for g in range(2):
                bank = 4 + (g % 2)
                wt = W(("v", mq))
                if last_tc is not None:
                    wt += W(("a", last_tc))

                def tr4(te, g=g, rb=rb, bank=bank):
                    for j in range(4):
                        r = te.transpose(psbf(bank)[0:P, j * P:(j + 1) * P],
                                         s2_xq[rb][:, (g * 4 + j) * P:
                                                   (g * 4 + j + 1) * P],
                                         s_ident[:])
                    return r
                mt = pl.add("tensor", tr4, waits=wt, inc="t")
                last_tc = pl.add("scalar", lambda a, g=g, mi=mi, bank=bank: a.activation(
                    s2_xqT_all.rearrange("p (k l) -> p k l", l=TC)
                    [:, g * 4:(g + 1) * 4, mi * P:(mi + 1) * P],
                    psbf(bank)[0:P, 0:4 * P].rearrange("p (k l) -> p k l", l=P),
                    AF.Copy), waits=W(("t", mt)), inc="a")
            m_xqT_mi[mi] = last_tc
        m_xqT = pl.cnt["a"]

        SLICES = [(j * 384, 384) for j in range(7)] + [(2688, 128)]
        hq_marks = [None] * TT4
        prev_gu = {}
        bank_ev = {}
        prev_tr = {}
        prev_hq_w = {}
        last_mq2 = 0
        tpos = 0
        res1_last = m_res1[(3, 3)]
        for si, (w0, wN) in enumerate(SLICES):
            ws = si % 2
            wqw = W(("t", prev_gu[ws])) if ws in prev_gu else ()
            dgu = []
            for k in range(DT):
                dgu.append(dma(s2_wg2[ws][:, k * 384:k * 384 + wN],
                               g_wgT[k * P:(k + 1) * P, w0:w0 + wN], "gul",
                               waits=wqw, gq=True))
                dgu.append(dma(s2_wu2[ws][:, k * 384:k * 384 + wN],
                               g_wuT[k * P:(k + 1) * P, w0:w0 + wN], "gul",
                               gq=True))
            dlast = dgu[-1]
            for mi in range(TT4):
                bp = mi % 2
                bg, bu = 2 * bp, 2 * bp + 1

                def gumm(te, mi=mi, ws=ws, wN=wN, bg=bg, bu=bu, dlast=dlast):
                    te.wait_ge(sems["gul"], dlast)
                    for k in range(DT):
                        te.matmul(psf[bg][:, 0:wN],
                                  s2_xqT[k][:, mi * P:(mi + 1) * P],
                                  s2_wg2[ws][:, k * 384:k * 384 + wN],
                                  start=(k == 0), stop=(k == DT - 1))
                    last = None
                    for k in range(DT):
                        last = te.matmul(psf[bu][:, 0:wN],
                                         s2_xqT[k][:, mi * P:(mi + 1) * P],
                                         s2_wu2[ws][:, k * 384:k * 384 + wN],
                                         start=(k == 0), stop=(k == DT - 1))
                    return last
                wt = W(("a", m_xqT_mi[mi]))
                if si == 0:
                    wt += W(("v", res1_last))
                if bp in bank_ev:
                    wt += W(("a", bank_ev[bp][0]), ("v", bank_ev[bp][1]))
                mt = pl.add("tensor", gumm, waits=wt, inc="t")
                prev_gu[ws] = mt
                ma = pl.add("scalar", lambda a, bp=bp, bg=bg, wN=wN: a.activation(
                    s2_gs2[bp][:, 0:wN], psf[bg][:, 0:wN], AF.Silu),
                    waits=W(("t", mt)), inc="a")
                h1 = pl.add("vector", lambda v, mi=mi, bu=bu, wN=wN, w0=w0:
                    v.scalar_tensor_tensor(
                        hffn_buf[mi][:, w0:w0 + wN], psf[bu][:, 0:wN], 0.0,
                        s2_gs2[mi % 2][:, 0:wN], op0=ALU.bypass, op1=ALU.mult),
                    waits=W(("a", ma)), inc="v")
                bank_ev[bp] = (ma, h1)
                if si in (3, 7):
                    h = 0 if si == 3 else 1
                    hw0 = h * 1408
                    wv = W(("v", h1))
                    if bp in prev_hq_w:
                        wv += W(("t", prev_hq_w[bp]))
                    mq = quant_tile(hffn_buf[mi][:, hw0:hw0 + 1408],
                                    hq_sl[bp][:, 0:1408], P, 1408,
                                    wait_v=wv,
                                    wait_a=W(("v", h1), ("v", last_mq2)),
                                    lowp=True)
                    last_mq2 = mq
                    for g in range(3):
                        ntile = (4, 4, 3)[g]
                        bank = 6 + (tpos % 2)
                        tpos += 1
                        wtr = W(("v", mq))
                        if bank in prev_tr:
                            wtr += W(("a", prev_tr[bank]))

                        def trN(te, bp=bp, bank=bank, ntile=ntile, g=g):
                            for j in range(ntile):
                                r = te.transpose(
                                    psbf(bank)[0:P, j * P:(j + 1) * P],
                                    hq_sl[bp][:, (g * 4 + j) * P:
                                              (g * 4 + j + 1) * P],
                                    s_ident[:])
                            return r
                        mtr = pl.add("tensor", trN, waits=wtr, inc="t")
                        ev = pl.add("scalar", lambda a, h=h, g=g, mi=mi,
                                    bank=bank, ntile=ntile: a.activation(
                            s2_hqT_all.rearrange("p (k l) -> p k l", l=TC)
                            [:, h * 11 + g * 4:h * 11 + g * 4 + ntile,
                             mi * P:(mi + 1) * P],
                            psbf(bank)[0:P, 0:ntile * P]
                            .rearrange("p (k l) -> p k l", l=P),
                            AF.Copy), waits=W(("t", mtr)), inc="a")
                        prev_tr[bank] = ev
                    prev_hq_w[bp] = mtr
                    hq_marks[mi] = ev

        res2_marks = {}
        prev_dmm = {}
        for dp in range(4):
            ws = dp % 2
            wqw = W(("t", prev_dmm[ws])) if ws in prev_dmm else ()
            dwdl = []
            for k in range(IT):
                dwdl.append(dma(s2_wd2[ws][:, k * 256:(k + 1) * 256],
                                g_wdT[k * P:(k + 1) * P,
                                      dp * 256:(dp + 1) * 256], "wdl",
                                waits=wqw, gq=True))
            dlast = dwdl[-1]
            for mi in range(TT4):
                bank = 4 + (mi % 2)

                def dmm(te, mi=mi, ws=ws, bank=bank, dlast=dlast):
                    te.wait_ge(sems["wdl"], dlast)
                    last = None
                    for k in range(IT):
                        last = te.matmul(psf[bank][:, 0:256],
                                         s2_hqT[k][:, mi * P:(mi + 1) * P],
                                         s2_wd2[ws][:, k * 256:(k + 1) * 256],
                                         start=(k == 0), stop=(k == IT - 1))
                    return last
                wt = W(("a", hq_marks[mi]))
                key = (dp - 1, mi) if dp > 0 else None
                prevu = res2_marks.get((dp, mi - 2), res2_marks.get((dp - 1, mi + 2)))
                if prevu is not None:
                    wt += W(("v", prevu))
                mt = pl.add("tensor", dmm, waits=wt, inc="t")
                prev_dmm[ws] = mt
                res2_marks[(dp, mi)] = pl.add("vector",
                    lambda v, mi=mi, dp=dp, bank=bank: v.tensor_tensor(
                        s2_x[mi][:, dp * 256:(dp + 1) * 256],
                        s2_x[mi][:, dp * 256:(dp + 1) * 256],
                        psf[bank][:, 0:256], op=ALU.add),
                    waits=W(("t", mt)), inc="v")
        if TAPS and c == 0:
            tap("hqT0", [P, TC], BF16, s2_hqT[0][:, 0:TC],
                W(("a", hq_marks[0])))
            tap("out0", [P, D], F32, s2_x[0][:], W(("v", res2_marks[(3, 0)])))
        for mi in range(TT4):
            ostore(dout[c * TC + mi * P:c * TC + (mi + 1) * P, :], s2_x[mi],
                   waits=W(("v", res2_marks[(3, mi)])))
        pl.barrier()

    tap("wgT0", [P, IF_], BF16, None, ())
    if TAPS:
        tap_t["wgT0"] = nc.dram_tensor("tap_wgT0", [P, IF_], BF16,
                                       kind="ExternalOutput").ap()
        dma(tap_t["wgT0"], g_wgT[0:P, :], "tap", waits=())
    o_final = pl.cnt["outs"]
    pl.add("sync", lambda e: e.wait_ge(sems["outs"], o_final))

    for g in sorted(pl.groups):
        sems[g] = ec(nc.semaphore("sem_" + g))

    @block.sync
    def _(e):
        pl.emit("sync", e, sems)

    @block.vector
    def _(e):
        pl.emit("vector", e, sems, selfsync=SELFSYNC)

    @block.scalar
    def _(e):
        pl.emit("scalar", e, sems, selfsync=SELFSYNC)

    @block.tensor
    def _(e):
        pl.emit("tensor", e, sems, selfsync=SELFSYNC)

    @block.gpsimd
    def _(e):
        pl.emit("gpsimd", e, sems)

    stack.close()
    _CACHE["nc"] = nc
    return nc


def shard_inputs(inputs):
    x = np.ascontiguousarray(inputs["L0_attn_x"])
    W_in = inputs["L0_attn_W_in"]
    b_in = inputs["L0_attn_b_in"]
    W_conv = np.asarray(inputs["L0_attn_W_conv"]).reshape(D, 4)
    b_conv = inputs["L0_attn_b_conv"]
    W_xp = inputs["L0_attn_W_xp"]
    b_xp = inputs["L0_attn_b_xp"]
    W_dt = inputs["L0_attn_W_dt"]
    b_dt = inputs["L0_attn_b_dt"]
    A_log = inputs["L0_attn_A_log"]
    W_out = inputs["L0_attn_W_out"]
    b_out = inputs["L0_attn_b_out"]
    w_gate = inputs["L0_ffn_w_gate"]
    w_up = inputs["L0_ffn_w_up"]
    w_down = inputs["L0_ffn_w_down"]

    f32 = np.float32
    wxp_pad = np.zeros((D, 80), f32)
    wxp_pad[:, 0:16] = W_xp[:, 0:16]
    wxp_pad[:, 32:48] = W_xp[:, 16:32]
    wxp_pad[:, 64:80] = W_xp[:, 32:48]
    bxp_pad = np.zeros((80, 1), f32)
    bxp_pad[0:16, 0] = b_xp[0:16]
    bxp_pad[32:48, 0] = b_xp[16:32]
    bxp_pad[64:80, 0] = b_xp[32:48]
    bdt_c = np.ascontiguousarray(np.asarray(b_dt).reshape(DT, P).T, dtype=f32)
    bconv_c = np.ascontiguousarray(np.asarray(b_conv).reshape(DT, P).T, dtype=f32)
    bin_c = np.ascontiguousarray(np.asarray(b_in).reshape(16, P).T, dtype=f32)
    taps_c = np.zeros((P, DT * 4), f32)
    for k in range(DT):
        taps_c[:, 4 * k:4 * (k + 1)] = W_conv[k * P:(k + 1) * P, :]
    alog_c = np.zeros((P, DT * NS), f32)
    for k in range(DT):
        alog_c[:, NS * k:NS * (k + 1)] = A_log[k * P:(k + 1) * P, :]

    win_f = np.ascontiguousarray(W_in, dtype=f32)
    wout_f = np.ascontiguousarray(W_out, dtype=f32)
    wg_f = np.ascontiguousarray(w_gate, dtype=f32)
    wu_f = np.ascontiguousarray(w_up, dtype=f32)
    wd_f = np.ascontiguousarray(w_down, dtype=f32)
    maps = []
    for c in range(8):
        b, h = c // 2, c % 2
        xs = np.ascontiguousarray(x[b, h * 4096:(h + 1) * 4096, :], dtype=f32)
        if h == 1:
            xt = np.ascontiguousarray(x[b, 4093:4096, :], dtype=f32)
            hm = np.ones((P, 1), f32)
        else:
            xt = np.zeros((3, D), f32)
            hm = np.zeros((P, 1), f32)
        maps.append({
            "x": xs, "xtail": xt, "hmask": hm,
            "win_f": win_f, "wout_f": wout_f, "wg_f": wg_f, "wu_f": wu_f,
            "wd_f": wd_f,
            "prty": np.array([[h]], np.int32),
            "wxp": wxp_pad, "bxp": bxp_pad,
            "wdt": np.ascontiguousarray(W_dt, dtype=f32),
            "bdt": bdt_c, "bin": bin_c, "taps": taps_c, "bconv": bconv_c,
            "alog": alog_c,
            "bout_row": np.ascontiguousarray(np.asarray(b_out).reshape(1, D), dtype=f32),
        })
    return maps


def kernel(**inputs):
    nc = build_program()
    maps = shard_inputs(inputs)
    res = run_bass_kernel_spmd(nc, maps, core_ids=list(range(8)))
    out = np.zeros((4, 8192, D), np.float32)
    for c in range(8):
        b, h = c // 2, c % 2
        out[b, h * 4096:(h + 1) * 4096, :] = np.asarray(res.results[c]["out"])
    if TAPS:
        global _LAST_TAPS
        _LAST_TAPS = {}
        for c in (0, 1):
            for k, v in res.results[c].items():
                if k.startswith("tap_"):
                    _LAST_TAPS[f"c{c}_{k[4:]}"] = np.asarray(v)
    return out


# version 26
# speedup vs baseline: 1.1974x; 1.1974x over previous
import numpy as np
import ml_dtypes
import concourse.bass as bass
import concourse.mybir as mybir
from concourse.bass_utils import run_bass_kernel_spmd

F32 = mybir.dt.float32
BF16 = mybir.dt.bfloat16
AF = mybir.ActivationFunctionType
ALU = mybir.AluOpType
AX = mybir.AxisListType

import os
P = 128
D = 1024
DT = 8
NS = 16
TC = 512
NCH = int(os.environ.get("ANT_NCHUNK", "8"))
TAPS = os.environ.get("ANT_TAPS", "") == "1"
SELFSYNC = os.environ.get("ANT_SELFSYNC", "0") == "1"
DR = mybir.MatmulPerfMode.DoubleRow
TT4 = 4
IF_ = 2816
IT = 22
ISH = 352
IQ = 704
TAU = [0.0625, 0.5625, 1.5625, 3.0625, 6.25, 12.25, 25.0]

_CACHE = {}


class Plan:
    ENGS = ("sync", "vector", "scalar", "tensor", "gpsimd")
    OWN = {"sync": None, "vector": "v", "scalar": "a", "tensor": "t",
           "gpsimd": "g"}

    def __init__(self):
        self.ops = {e: [] for e in self.ENGS}
        self.cnt = {}
        self.pending = {e: [] for e in self.ENGS}
        self.groups = set()

    def bump(self, name, amt):
        self.cnt[name] = self.cnt.get(name, 0) + amt
        self.groups.add(name)
        return self.cnt[name]

    def add(self, eng, fn, waits=(), inc=None, amt=None):
        w = self.pending[eng] + [x for x in waits if x[1]]
        self.pending[eng] = []
        mark = None
        if inc is not None:
            if amt is None:
                amt = 1 if inc in ("v", "a", "t") else 16
            mark = self.bump(inc, amt)
        self.ops[eng].append((fn, tuple(w), inc, amt))
        return mark

    def barrier(self):
        snap = dict(self.cnt)
        for e in self.ENGS:
            own = self.OWN[e]
            self.pending[e] += [(c, v) for c, v in snap.items()
                                if c != own and v > 0]

    def emit(self, eng, handle, sems, selfsync=False):
        seen = {}
        own = self.OWN[eng]
        pos = 0
        for fn, waits, inc, amt in self.ops[eng]:
            best = {}
            forced = 0
            for w in waits:
                ctr, val = w[0], w[1]
                if ctr == own:
                    if len(w) >= 3:
                        forced = max(forced, val)
                    continue
                if val > best.get(ctr, seen.get(ctr, 0)):
                    best[ctr] = val
            if selfsync and own is not None:
                forced = max(forced, pos)
            if forced:
                handle.wait_ge(sems[own], forced)
            for ctr, val in best.items():
                handle.wait_ge(sems[ctr], val)
                seen[ctr] = val
            last = fn(handle)
            if inc is not None:
                last.then_inc(sems[inc], amt)
                if inc == own:
                    pos += amt


class Arena:
    def __init__(self, nc, ec, name, words):
        self.t = ec(nc.sbuf_tensor(name, [P, words], F32))
        self.off = 0
        self.words = words

    def carve(self, cols, dtype=F32):
        w = cols if dtype == F32 else (cols + 1) // 2
        w = (w + 15) // 16 * 16
        assert self.off + w <= self.words, f"arena overflow {self.off + w} > {self.words}"
        base = self.t[:, self.off:self.off + w]
        self.off += w
        if dtype != F32:
            return base.bitcast(dtype)[:, 0:cols]
        return base[:, 0:cols]

    def mark(self):
        return self.off

    def reset(self, m):
        self.off = m


def build_program():
    if "nc" in _CACHE:
        return _CACHE["nc"]
    nc = bass.Bass("TRN2", target_bir_lowering=False)

    dx = nc.dram_tensor("x", [4096, D], F32, kind="ExternalInput").ap()
    dxtail = nc.dram_tensor("xtail", [3, D], F32, kind="ExternalInput").ap()
    dhmask = nc.dram_tensor("hmask", [P, 1], F32, kind="ExternalInput").ap()
    dwin_f = nc.dram_tensor("win_f", [D, 2 * D], F32, kind="ExternalInput").ap()
    dwout_f = nc.dram_tensor("wout_f", [D, D], F32, kind="ExternalInput").ap()
    dwg_f = nc.dram_tensor("wg_f", [IF_, D], F32, kind="ExternalInput").ap()
    dwu_f = nc.dram_tensor("wu_f", [IF_, D], F32, kind="ExternalInput").ap()
    dwd_f = nc.dram_tensor("wd_f", [D, IF_], F32, kind="ExternalInput").ap()
    dprty = nc.dram_tensor("prty", [1, 1], mybir.dt.int32, kind="ExternalInput").ap()
    dwxp = nc.dram_tensor("wxp", [DT * P, 80], F32, kind="ExternalInput").ap()
    dbxp = nc.dram_tensor("bxp", [80, 1], F32, kind="ExternalInput").ap()
    dwdt = nc.dram_tensor("wdt", [NS, D], F32, kind="ExternalInput").ap()
    dbdt = nc.dram_tensor("bdt", [P, DT], F32, kind="ExternalInput").ap()
    dbin = nc.dram_tensor("bin", [P, 16], F32, kind="ExternalInput").ap()
    dtaps = nc.dram_tensor("taps", [P, DT * 4], F32, kind="ExternalInput").ap()
    dbconv = nc.dram_tensor("bconv", [P, DT], F32, kind="ExternalInput").ap()
    dalog = nc.dram_tensor("alog", [P, DT * NS], F32, kind="ExternalInput").ap()
    dbout_row = nc.dram_tensor("bout_row", [1, D], F32, kind="ExternalInput").ap()
    dout = nc.dram_tensor("out", [4096, D], F32, kind="ExternalOutput").ap()

    def idram(name, shape, dt=F32, shared=False):
        return nc.dram_tensor(name, shape, dt, kind="Internal",
                              addr_space="Shared" if shared else "Local").ap()

    g_win = idram("g_win", [D, 2 * D], BF16)
    g_wout = idram("g_wout", [D, D], BF16)
    g_wgT = idram("g_wgT", [D, IF_], BF16)
    g_wuT = idram("g_wuT", [D, IF_], BF16)
    g_wdT = idram("g_wdT", [IF_, D], BF16)
    sh_state = idram("sh_state", [2 * P, P], shared=True)
    sh_flag = idram("sh_flag", [2, 64], mybir.dt.int32, shared=True)
    sp_P = idram("sp_P", [NCH * DT, P, TC], F32)
    sp_y = idram("sp_y", [NCH * DT, P, TC], F32)
    sp_zs = idram("sp_zs", [NCH * DT, P, TC], BF16)
    sp_brow = idram("sp_brow", [NCH, 1, NS * TC], BF16)
    sp_crow = idram("sp_crow", [NCH, 1, NS * TC], BF16)

    dident = nc.inline_tensor(np.eye(P, dtype=ml_dtypes.bfloat16), name="ident").ap()

    pl = Plan()
    from contextlib import ExitStack
    stack = ExitStack()
    ec = stack.enter_context

    def sb(name, shape, dt=F32):
        return ec(nc.sbuf_tensor(name, shape, dt))

    s_ident = sb("s_ident", [P, P], BF16)
    s_hmask = sb("s_hmask", [P, 1])
    s_taps = sb("s_taps", [P, DT * 4])
    s_bconv = sb("s_bconv", [P, DT])
    s_bdt = sb("s_bdt", [P, DT])
    s_bin = sb("s_bin", [P, 16])
    s_A = sb("s_A", [P, DT * NS])
    s_wxp = sb("s_wxp", [P, DT * 80], BF16)
    s_wdt = sb("s_wdt", [NS, D], BF16)
    s_bxp = sb("s_bxp", [80, 1])
    s_ones1 = sb("s_ones1", [1, P], BF16)
    s_boutr = sb("s_boutr", [1, D], BF16)
    s_hscol = sb("s_hscol", [P, P])
    s_pcol = sb("s_pcol", [P, DT])
    s_hb = sb("s_hb", [P, P])
    s_col = sb("s_col", [P, 16])
    s_utail = sb("s_utail", [P, 3 * DT])
    s_fli = sb("s_fli", [1, 128], mybir.dt.int32)
    s_prk = sb("s_prk", [1, 1], mybir.dt.int32)
    s_spfl = sb("s_spfl", [1, 64], mybir.dt.int32)
    spin_sem = ec(nc.semaphore("spin_sem"))
    prty_reg = ec(nc.sync.register(name="prty_reg"))
    sh_flagv = sh_flag.rearrange("(s o) c -> s o c", o=1)
    sh_statev = sh_state.rearrange("(s p) c -> s p c", p=P)
    s_cst = sb("s_cst", [P, 12])
    CONSTS = [1e-5, -0.5, -1.0, 1.0, 1.0 / D, -6e-12, 6e-12, 1.0 / 12.0, 6.0,
              5e-13]
    for ci, cv in enumerate(CONSTS):
        nc.const_aps.register(F32, cv, s_cst[:, ci:ci + 1])

    AR = Arena(nc, ec, "arena", 50992)
    FR = AR.mark()

    q_m = AR.carve(IF_ // 16)
    q_r = AR.carve(IF_ // 16)
    q_rsq = AR.carve(IF_ // 16)
    q_sc2 = AR.carve(IF_ // 16)
    q_xsq = AR.carve(IF_)
    q_acc = AR.carve(IF_)
    q_b7 = AR.carve(IF_)
    q_sgn = AR.carve(IF_, BF16)
    QFR = AR.mark()

    psf = [ec(nc.psum_tensor(f"psf{i}", [P, 512], F32)) for i in range(8)]

    def psbf(i):
        return psf[i][:].bitcast(BF16)

    sems = {}
    block = ec(nc.Block())

    W = lambda *pairs: list(pairs)

    def dma(dst, src, grp, waits=(), gq=False):
        eng = "gpsimd" if gq else "sync"
        return pl.add(eng, lambda e, dst=dst, src=src: e.dma_start(out=dst, in_=src),
                      waits=waits, inc=grp)

    tap_t = {}
    def tap(name, shape, dtype, src_ap, waits):
        if not TAPS or src_ap is None:
            return
        tap_t[name] = nc.dram_tensor("tap_" + name, shape, dtype,
                                     kind="ExternalOutput").ap()
        dma(tap_t[name], src_ap, "tap", waits=waits)
        pl.barrier()

    def ostore(dst, src, waits=()):
        return pl.add("sync", lambda e, dst=dst, src=src: e.dma_start(out=dst, in_=src),
                      waits=waits, inc="outs")

    QS0 = dict(m=q_m, r=q_r, rsq=q_rsq, sc2=q_sc2, xsq=q_xsq, acc=q_acc,
               b7=q_b7, sgn=q_sgn, xsqb=q_xsq.bitcast(BF16),
               accb=q_acc.bitcast(BF16), b7b=q_b7.bitcast(BF16),
               rsqb=q_rsq.bitcast(BF16))

    def qhead(qs, xin_ap, PN, F, wait_v=(), wait_a=(), lowp=False):
        nb = F // 16
        xsq = qs["xsqb"] if lowp else qs["xsq"]
        m_red = pl.add("vector", lambda v: v.tensor_reduce(
            qs["m"][0:PN, 0:nb].unsqueeze(2),
            xin_ap.rearrange("p (a c) -> p a c", c=16),
            axis=AX.X, op=ALU.max, apply_absolute_value=True),
            waits=list(wait_v), inc="v")
        m_sq = pl.add("scalar", lambda a: a.activation(
            xsq[0:PN, 0:F], xin_ap, AF.Square), waits=list(wait_a), inc="a")
        m_sg = pl.add("scalar", lambda a: a.activation(
            qs["sgn"][0:PN, 0:F], xin_ap, AF.Sign), inc="a")
        m_cl1 = pl.add("scalar", lambda a: a.activation(
            qs["r"][0:PN, 0:nb], qs["m"][0:PN, 0:nb], AF.Relu, bias=-6e-12),
            waits=W(("v", m_red)), inc="a")
        pl.add("scalar", lambda a: a.activation(
            qs["sc2"][0:PN, 0:nb], qs["r"][0:PN, 0:nb], AF.Copy,
            scale=1.0 / 12.0, bias=5e-13), waits=W(("a", m_cl1, 1)), inc="a")
        m_cl = pl.add("scalar", lambda a: a.activation(
            qs["m"][0:PN, 0:nb], qs["r"][0:PN, 0:nb], AF.Copy, bias=6e-12),
            waits=W(("a", m_cl1, 1)), inc="a")
        m_rc = pl.add("vector", lambda v: v.reciprocal(
            qs["r"][0:PN, 0:nb], qs["m"][0:PN, 0:nb]),
            waits=W(("a", m_cl)), inc="v")
        m_rsq = pl.add("scalar", lambda a: a.activation(
            (qs["rsqb"] if lowp else qs["rsq"])[0:PN, 0:nb],
            qs["r"][0:PN, 0:nb], AF.Square, scale=6.0),
            waits=W(("v", m_rc)), inc="a")
        return (m_red, m_sq, m_sg, m_rsq)

    def qtail(qs, xout_ap, PN, F, marks, wait_v=(), lowp=False):
        nb = F // 16
        m_red, m_sq, m_sg, m_rsq = marks
        xsq = qs["xsqb"] if lowp else qs["xsq"]
        acc = qs["accb"] if lowp else qs["acc"]
        b7 = qs["b7b"] if lowp else qs["b7"]
        rsq = qs["rsqb"] if lowp else qs["rsq"]
        rsq_bc = rsq[0:PN, 0:nb].unsqueeze(2).broadcast_to([PN, nb, 16])
        sc2_bc = qs["sc2"][0:PN, 0:nb].unsqueeze(2).broadcast_to([PN, nb, 16])
        pl.add("vector", lambda v: v.tensor_tensor(
            xsq[0:PN, 0:F].rearrange("p (a c) -> p a c", c=16),
            xsq[0:PN, 0:F].rearrange("p (a c) -> p a c", c=16), rsq_bc,
            op=ALU.mult),
            waits=W(("a", m_sq), ("a", m_rsq)) + list(wait_v), inc="v")
        pl.add("vector", lambda v: v.tensor_scalar(
            acc[0:PN, 0:F], xsq[0:PN, 0:F], TAU[0], None, op0=ALU.is_gt),
            inc="v")
        for j in (1, 2, 3):
            pl.add("vector", lambda v, j=j: v.scalar_tensor_tensor(
                acc[0:PN, 0:F], xsq[0:PN, 0:F], TAU[j], acc[0:PN, 0:F],
                op0=ALU.is_gt, op1=ALU.add), inc="v")
        pl.add("vector", lambda v: v.tensor_scalar(
            b7[0:PN, 0:F], xsq[0:PN, 0:F], TAU[6], None, op0=ALU.is_gt),
            inc="v")
        for j in (6, 4, 5):
            pl.add("vector", lambda v, j=j: v.scalar_tensor_tensor(
                b7[0:PN, 0:F], xsq[0:PN, 0:F], TAU[j], b7[0:PN, 0:F],
                op0=ALU.is_gt, op1=ALU.add), inc="v")
        pl.add("vector", lambda v: v.scalar_tensor_tensor(
            acc[0:PN, 0:F], b7[0:PN, 0:F], 2.0, acc[0:PN, 0:F],
            op0=ALU.mult, op1=ALU.add), inc="v")
        pl.add("vector", lambda v: v.tensor_tensor(
            acc[0:PN, 0:F], acc[0:PN, 0:F], qs["sgn"][0:PN, 0:F],
            op=ALU.mult), waits=W(("a", m_sg)), inc="v")
        return pl.add("vector", lambda v: v.tensor_tensor(
            xout_ap.rearrange("p (a c) -> p a c", c=16),
            acc[0:PN, 0:F].rearrange("p (a c) -> p a c", c=16), sc2_bc,
            op=ALU.mult), inc="v")

    def quant_tile(xin_ap, xout_ap, PN, F, wait_v=(), wait_a=(), lowp=False):
        marks = qhead(QS0, xin_ap, PN, F, wait_v=wait_v, wait_a=wait_a,
                      lowp=lowp)
        return qtail(QS0, xout_ap, PN, F, marks, lowp=lowp)

    def ln_tile(x_ap, out_ap, scr, wait_v=(), wait_a=()):
        ms = pl.add("vector", lambda v: v.reduce_sum(s_col[:, 0:1], x_ap, axis=AX.X),
                    waits=list(wait_v), inc="v")
        ma = pl.add("scalar", lambda a: a.activation(scr, x_ap, AF.Square,
                                                     accum_out=s_col[:, 1:2]),
                    waits=list(wait_a), inc="a")

        mam1 = pl.add("scalar", lambda a: a.activation(
            s_col[:, 2:3], s_col[:, 0:1], AF.Copy, scale=1.0 / D),
            waits=W(("v", ms)), inc="a")
        mam = pl.add("scalar", lambda a: a.activation(
            s_col[:, 3:4], s_col[:, 2:3], AF.Square),
            waits=W(("a", mam1, 1)), inc="a")
        mr = pl.add("vector", lambda v: v.scalar_tensor_tensor(
            s_col[:, 4:5], s_col[:, 1:2], 1.0 / D, s_col[:, 3:4],
            op0=ALU.mult, op1=ALU.subtract), waits=W(("a", mam)), inc="v")

        mrs1 = pl.add("scalar", lambda a: a.activation(
            s_col[:, 5:6], s_col[:, 4:5], AF.Ln, bias=1e-5),
            waits=W(("v", mr)), inc="a")
        mrs = pl.add("scalar", lambda a: a.activation(
            s_col[:, 6:7], s_col[:, 5:6], AF.Exp, scale=-0.5),
            waits=W(("a", mrs1, 1)), inc="a")
        return pl.add("vector", lambda v: v.tensor_scalar(
            out_ap, x_ap, s_col[:, 2:3], s_col[:, 6:7],
            op0=ALU.subtract, op1=ALU.mult), waits=W(("a", mrs)), inc="v")

    AR.reset(QFR)
    wp_stg2 = [AR.carve(IF_) for _ in range(2)]
    wp_stg = wp_stg2[0]
    wp_bf2 = [AR.carve(IF_, BF16) for _ in range(2)]
    wp_bf = wp_bf2[0]
    wp_t = AR.carve(8 * P, BF16)
    def mk_qs():
        xsq = AR.carve(IF_)
        acc = AR.carve(IF_)
        b7 = AR.carve(IF_)
        sgn = AR.carve(IF_, BF16)
        sml = AR.carve(4 * (IF_ // 16))
        return dict(m=sml[:, 0:176], r=sml[:, 176:352],
                    rsq=sml[:, 352:528], sc2=sml[:, 528:704],
                    xsq=xsq, acc=acc, b7=b7, sgn=sgn,
                    xsqb=xsq.bitcast(BF16), accb=acc.bitcast(BF16),
                    b7b=b7.bitcast(BF16), rsqb=sml[:, 352:528].bitcast(BF16))
    QSW2 = [mk_qs(), mk_qs()]
    s_xt3 = AR.carve(D)
    s_xtln = AR.carve(D, BF16)
    s_xtT = sb("s_xtT", [P, 3 * DT], BF16)
    wp_scr = AR.carve(D)

    for dst, src in [(s_ident[:], dident), (s_hmask[:], dhmask), (s_taps[:], dtaps),
                     (s_bconv[:], dbconv), (s_bdt[:], dbdt), (s_bin[:], dbin),
                     (s_bxp[:], dbxp), (s_xt3[0:3, :], dxtail), (s_A[:], dalog)]:
        dma(dst, src, "wpl")
    dma(q_acc[0:NS, 0:D], dwdt, "wpl")
    dma(q_b7[:, 0:DT * 80].rearrange("p (kt c) -> p kt c", kt=DT),
        dwxp.rearrange("(kt p) c -> p kt c", p=P), "wpl")

    dpr = dma(s_prk[:], dprty, "wpl")
    d_consts = pl.cnt["wpl"]

    def wp_small(v):
        for ci, cv in enumerate(CONSTS):
            v.memset(s_cst[:, ci:ci + 1], cv)
        v.tensor_copy(s_wxp[:], q_b7[:, 0:DT * 80])
        v.tensor_copy(s_wdt[:], q_acc[0:NS, 0:D])
        v.memset(s_ones1[:], 1.0)
        v.memset(s_hscol[:], 0.0)
        v.memset(s_pcol[:], 0.0)
        return v.memset(s_utail[:], 0.0)
    m_wpsm = pl.add("vector", wp_small, waits=W(("wpl", d_consts)), inc="v")
    m_eA = pl.add("scalar", lambda a: a.activation(s_A[:], s_A[:], AF.Exp),
                  waits=W(("wpl", d_consts), ("v", m_wpsm)), inc="a")
    m_negA = pl.add("vector", lambda v: v.tensor_scalar_mul(s_A[:], s_A[:], -1.0),
                    waits=W(("a", m_eA)), inc="v")

    m_fli = pl.add("vector", lambda v: (v.memset(s_fli[0:1, 0:64], 0),
                                        v.memset(s_fli[0:1, 64:128], 1))[-1], inc="v")

    def zero_flag(sync):
        sync.reg_load(prty_reg, s_prk[0:1, 0:1])
        idx = sync.snap(prty_reg, min_val=0, max_val=1)
        return sync.dma_start(out=sh_flagv[idx], in_=s_fli[0:1, 0:64])
    pl.add("sync", zero_flag, waits=W(("wpl", dpr), ("v", m_fli)), inc="fz")

    last_v = m_fli
    for k in range(DT):
        dL = dma(wp_stg[:, 0:2 * D], dwin_f[k * P:(k + 1) * P, :], "wstg",
                 waits=W(("v", last_v)))
        last_v = pl.add("vector", lambda v: v.tensor_copy(
            wp_bf[:, 0:2 * D], wp_stg[:, 0:2 * D]),
            waits=W(("wstg", dL), ("wsto", pl.cnt.get("wsto", 0))), inc="v")
        dma(g_win[k * P:(k + 1) * P, :], wp_bf[:, 0:2 * D], "wsto", waits=W(("v", last_v)))
    for k in range(DT):
        dL = dma(wp_stg[:, 0:D], dwout_f[k * P:(k + 1) * P, :], "wstg",
                 waits=W(("v", last_v)))
        last_v = pl.add("vector", lambda v: v.tensor_copy(
            wp_bf[:, 0:D], wp_stg[:, 0:D]),
            waits=W(("wstg", dL), ("wsto", pl.cnt.get("wsto", 0))), inc="v")
        dma(g_wout[k * P:(k + 1) * P, :], wp_bf[:, 0:D], "wsto", waits=W(("v", last_v)))

    tj = [0]
    tstores = []

    def tstore(dst_ap, src_ps, mt):
        j = tj[0]
        tj[0] += 1
        sl = (j % 8) * P
        wts = [("t", mt)]
        if j >= 8:
            wts.append(("wsto", tstores[j - 8]))
        mv = pl.add("scalar", lambda a, sl=sl, src_ps=src_ps: a.activation(
            wp_t[:, sl:sl + P], src_ps, AF.Copy), waits=W(*wts), inc="a")
        ds = dma(dst_ap, wp_t[:, sl:sl + P], "wsto", waits=W(("a", mv)))
        tstores.append(ds)
        return mv

    jobs = []
    for gdst, gsrc in ((g_wgT, dwg_f), (g_wuT, dwu_f)):
        for it in range(IT):
            jobs.append((gdst, gsrc[it * P:(it + 1) * P, :], D, it, None))
    for k in range(DT):
        jobs.append((g_wdT, dwd_f[k * P:(k + 1) * P, :], IF_, None, k))
    job_mq = [None] * len(jobs)
    job_tr = [0] * len(jobs)
    dLs = [None] * len(jobs)
    heads = [None] * len(jobs)

    def issue_load(ji):
        gdst, gsrc, F, it, kk = jobs[ji]
        wl = W(("v", m_fli))
        if ji >= 2:
            wl += W(("v", heads[ji - 2][0]), ("a", heads[ji - 2][3]))
        dLs[ji] = dma(wp_stg2[ji % 2][:, 0:F], gsrc, "wstg", waits=wl)
    issue_load(0)
    issue_load(1)
    heads[0] = qhead(QSW2[0], wp_stg2[0][:, 0:jobs[0][2]], P, jobs[0][2],
                     wait_v=W(("wstg", dLs[0])), wait_a=W(("wstg", dLs[0])))
    for ji, (gdst, gsrc, F, it, kk) in enumerate(jobs):
        jb = ji % 2
        if ji + 1 < len(jobs):
            F1 = jobs[ji + 1][2]
            heads[ji + 1] = qhead(QSW2[(ji + 1) % 2],
                                  wp_stg2[(ji + 1) % 2][:, 0:F1], P, F1,
                                  wait_v=W(("wstg", dLs[ji + 1])),
                                  wait_a=W(("wstg", dLs[ji + 1])))
        mq = qtail(QSW2[jb], wp_bf2[jb][:, 0:F], P, F, heads[ji],
                   wait_v=W(("t", job_tr[ji - 2] if ji >= 2 else 0)))
        job_mq[ji] = mq
        last_w = ("v", mq)
        if ji + 2 < len(jobs):
            issue_load(ji + 2)
        nt = F // P
        for hk in range(nt):
            wt = W(last_w)
            if pl.cnt.get("t", 0) == 0:
                wt += W(("wpl", d_consts))
            mt = pl.add("tensor", lambda te, hk=hk, jb=jb: te.transpose(
                psbf(hk % 2)[0:P, 0:P], wp_bf2[jb][:, hk * P:(hk + 1) * P],
                s_ident[:]), waits=wt, inc="t")
            if it is not None:
                dst = gdst[hk * P:(hk + 1) * P, it * P:(it + 1) * P]
            else:
                dst = gdst[hk * P:(hk + 1) * P, kk * P:(kk + 1) * P]
            last_w = ("a", tstore(dst, psbf(hk % 2)[0:P, 0:P], mt))
        job_tr[ji] = pl.cnt["t"]
    dB = dma(wp_stg[0:1, 0:D], dbout_row, "wstg",
             waits=W(last_w, ("v", job_mq[-1])))
    m_br = pl.add("vector", lambda v: v.tensor_copy(s_boutr[:], wp_stg[0:1, 0:D]),
                  waits=W(("wstg", dB)), inc="v")

    m_ts = pl.add("vector", lambda v: v.reduce_sum(s_col[0:3, 0:1], s_xt3[0:3, :],
                                                   axis=AX.X),
                  waits=W(("wpl", d_consts), ("v", m_br)), inc="v")
    m_ta = pl.add("scalar", lambda a: a.activation(wp_scr[0:3, :], s_xt3[0:3, :],
                                                   AF.Square, accum_out=s_col[0:3, 1:2]),
                  waits=W(("wpl", d_consts)), inc="a")

    m_tam1 = pl.add("scalar", lambda a: a.activation(
        s_col[0:3, 2:3], s_col[0:3, 0:1], AF.Copy, scale=1.0 / D),
        waits=W(("v", m_ts)), inc="a")
    m_tam = pl.add("scalar", lambda a: a.activation(
        s_col[0:3, 3:4], s_col[0:3, 2:3], AF.Square),
        waits=W(("a", m_tam1, 1)), inc="a")
    m_tr = pl.add("vector", lambda v: v.scalar_tensor_tensor(
        s_col[0:3, 4:5], s_col[0:3, 1:2], 1.0 / D, s_col[0:3, 3:4],
        op0=ALU.mult, op1=ALU.subtract), waits=W(("a", m_tam)), inc="v")

    m_trs1 = pl.add("scalar", lambda a: a.activation(
        s_col[0:3, 5:6], s_col[0:3, 4:5], AF.Ln, bias=1e-5),
        waits=W(("v", m_tr)), inc="a")
    m_trs = pl.add("scalar", lambda a: a.activation(
        s_col[0:3, 6:7], s_col[0:3, 5:6], AF.Exp, scale=-0.5),
        waits=W(("a", m_trs1, 1)), inc="a")
    m_tap = pl.add("vector", lambda v: v.tensor_scalar(
        s_xtln[0:3, :], s_xt3[0:3, :], s_col[0:3, 2:3], s_col[0:3, 6:7],
        op0=ALU.subtract, op1=ALU.mult), waits=W(("a", m_trs)), inc="v")
    last_ev = m_tap
    for k in range(DT):
        mt = pl.add("tensor", lambda te, k=k: te.transpose(
            psbf(2)[0:P, 0:3], s_xtln[0:3, k * P:(k + 1) * P], s_ident[0:3, 0:3]),
            waits=W(("v", last_ev)), inc="t")
        last_ev = pl.add("vector", lambda v, k=k: v.tensor_copy(
            s_xtT[:, 3 * k:3 * k + 3], psbf(2)[0:P, 0:3]), waits=W(("t", mt)), inc="v")
    m_xtT = last_ev

    pl.barrier()

    AR.reset(FR)
    s_x = [AR.carve(D) for _ in range(TT4)]
    s_xln = [AR.carve(D, BF16) for _ in range(2)]
    s_winK_all = AR.carve(DT * 2 * D, BF16)
    s_winK = [s_winK_all[:, k * 2 * D:(k + 1) * 2 * D] for k in range(DT)]
    s_xlnT_all = AR.carve(DT * TC, BF16)
    s_xlnT = [s_xlnT_all[:, k * TC:(k + 1) * TC] for k in range(DT)]
    s_uext = [AR.carve(TC + 4) for _ in range(DT)]
    s_ufin = [AR.carve(TC) for _ in range(DT)]
    s_ubf_all = AR.carve(2 * TC, BF16)
    s_ubf = [s_ubf_all[:, k * TC:(k + 1) * TC] for k in range(2)]
    s_zsil = [AR.carve(TC, BF16) for _ in range(2)]
    s_dtb = [AR.carve(TC) for _ in range(2)]
    s_Pb = [AR.carve(TC) for _ in range(2)]
    s_dtu = [AR.carve(TC, BF16) for _ in range(2)]
    s_xdb = AR.carve(TC)
    s_dtin = AR.carve(TC, BF16)
    s_bbf = AR.carve(TC, BF16)
    s_cbf = AR.carve(TC, BF16)
    s_brep = AR.carve(NS * TC, BF16)
    s_crep1 = AR.carve(NS * TC, BF16)
    s_da = AR.carve(3 * TC)
    s_dbu = AR.carve(NS * TC, BF16)
    s_h = AR.carve(NS * TC, BF16)
    s_tree = AR.carve(8 * TC)
    s_scr = s_tree
    P1_END = AR.mark()

    for c in range(NCH):
        dxm = [dma(s_x[mi], dx[c * TC + mi * P:c * TC + (mi + 1) * P, :], "xl")
               for mi in range(TT4)]
        dwin = [dma(s_winK[k], g_win[k * P:(k + 1) * P, :], "winl", gq=True)
                for k in range(DT)]

        last_cp = None
        for mi in range(TT4):
            wl = W(("xl", dxm[-1]))
            if last_cp is not None:
                wl += W(("a", last_cp))
            mln = ln_tile(s_x[mi], s_xln[mi % 2], s_scr[:, 0:D],
                          wait_v=wl, wait_a=W(("xl", dxm[-1])))
            for g in range(2):
                bank = 2 + (g % 2)
                wt = W(("v", mln))
                if last_cp is not None:
                    wt += W(("a", last_cp))

                def tr4(te, g=g, mi=mi, bank=bank):
                    for j in range(4):
                        r = te.transpose(psbf(bank)[0:P, j * P:(j + 1) * P],
                                         s_xln[mi % 2][:, (g * 4 + j) * P:
                                                       (g * 4 + j + 1) * P],
                                         s_ident[:])
                    return r
                mt = pl.add("tensor", tr4, waits=wt, inc="t")
                last_cp = pl.add("scalar", lambda a, g=g, mi=mi, bank=bank: a.activation(
                    s_xlnT_all.rearrange("p (k l) -> p k l", l=TC)
                    [:, g * 4:(g + 1) * 4, mi * P:(mi + 1) * P],
                    psbf(bank)[0:P, 0:4 * P].rearrange("p (k l) -> p k l", l=P),
                    AF.Copy), waits=W(("t", mt)), inc="a")
        m_xlnT = last_cp
        if c == 0:
            tap("xlnT0", [P, TC], BF16, s_xlnT[0][:], W(("a", m_xlnT)))

        if c == 0:
            prev_a = None
            for m8 in range(DT):
                def u3mm(te, m8=m8):
                    last = None
                    for k in range(DT):
                        last = te.matmul(psf[4][0:P, 0:3],
                                         s_winK[k][:, m8 * P:(m8 + 1) * P],
                                         s_xtT[:, 3 * k:3 * k + 3],
                                         start=(k == 0), stop=(k == DT - 1))
                    return last
                wt = W(("v", m_xtT), ("winl", dwin[-1]))
                if prev_a is not None:
                    wt += W(("a", prev_a))
                mt = pl.add("tensor", u3mm, waits=wt, inc="t")
                prev_a = pl.add("scalar", lambda a, m8=m8: a.activation(
                    s_utail[:, 3 * m8:3 * m8 + 3], psf[4][0:P, 0:3], AF.Identity,
                    bias=s_bin[:, m8:m8 + 1]), waits=W(("t", mt)), inc="a")
            m_umask = pl.add("vector", lambda v: v.tensor_scalar(
                s_utail[:], s_utail[:], s_hmask[:, 0:1], None, op0=ALU.mult),
                waits=W(("a", prev_a)), inc="v")
            tap("ut0", [P, 3 * DT], F32, s_utail[:], W(("v", m_umask)))
            tap("xtT0", [P, 3 * DT], BF16, s_xtT[:], W(("v", m_xtT)))
            tap("win0", [P, 2 * D], BF16, s_winK[0][:], W(("winl", dwin[-1])))

        ev_u = [None] * DT
        ev_z = [None] * DT
        bank_free = {0: None, 1: None}
        for mm in range(16):
            bank = mm % 2

            def mmfn(te, mm=mm, bank=bank):
                last = None
                for k in range(DT):
                    last = te.matmul(psf[bank][:, 0:TC],
                                     s_winK[k][:, mm * P:(mm + 1) * P],
                                     s_xlnT[k][:], start=(k == 0), stop=(k == DT - 1))
                return last
            waits = W(("a", m_xlnT), ("winl", dwin[-1]))
            if bank_free[bank] is not None:
                waits += W(("a", bank_free[bank]))
            mt = pl.add("tensor", mmfn, waits=waits, inc="t")
            if mm < DT:
                k = mm
                tcw = W(("v", m_umask)) if c == 0 else ()
                tcp = pl.add("scalar", lambda a, k=k: a.activation(
                    s_uext[k][:, 0:3], s_utail[:, 3 * k:3 * k + 3], AF.Copy),
                    waits=tcw, inc="a")
                ev_u[k] = pl.add("scalar", lambda a, k=k, bank=bank: a.activation(
                    s_uext[k][:, 3:3 + TC], psf[bank][:, 0:TC], AF.Identity,
                    bias=s_bin[:, k:k + 1]), waits=W(("t", mt)), inc="a")
                bank_free[bank] = ev_u[k]
            else:
                k = mm - DT
                wz = W(("t", mt))
                if k >= 2:
                    wz += W(("zsp", pl.cnt.get("zsp", 0)))
                ev_z[k] = pl.add("scalar", lambda a, k=k, bank=bank: a.activation(
                    s_zsil[k % 2][:], psf[bank][:, 0:TC], AF.Silu,
                    bias=s_bin[:, DT + k:DT + k + 1]), waits=wz, inc="a")
                bank_free[bank] = ev_z[k]
                dma(sp_zs[c * DT + k], s_zsil[k % 2][:], "zsp", waits=W(("a", ev_z[k])), gq=True)
        for k in range(DT):
            pl.add("vector", lambda v, k=k: v.tensor_copy(
                s_utail[:, 3 * k:3 * k + 3], s_uext[k][:, TC:TC + 3]),
                waits=W(("a", ev_u[k])), inc="v")

        if c == 0:
            tap("uraw0", [P, TC], F32, s_uext[0][:, 3:3 + TC], W(("a", ev_u[0])))
        m_uf = [None] * DT
        for k in range(DT):
            t0 = pl.add("scalar", lambda a, k=k: a.activation(
                s_ufin[k], s_uext[k][:, 0:TC], AF.Copy,
                scale=s_taps[:, 4 * k:4 * k + 1]),
                waits=W(("a", ev_u[k]), ("v", pl.cnt["v"])), inc="a")
            pl.add("vector", lambda v, k=k: v.tensor_scalar(
                s_scr[:, 0:TC], s_uext[k][:, 1:1 + TC], s_taps[:, 4 * k + 1:4 * k + 2],
                None, op0=ALU.mult), waits=W(("a", ev_u[k])), inc="v")
            pl.add("vector", lambda v, k=k: v.scalar_tensor_tensor(
                s_scr[:, TC:2 * TC], s_uext[k][:, 2:2 + TC],
                s_taps[:, 4 * k + 2:4 * k + 3], s_scr[:, 0:TC],
                op0=ALU.mult, op1=ALU.add), inc="v")
            pl.add("vector", lambda v, k=k: v.scalar_tensor_tensor(
                s_scr[:, 0:TC], s_uext[k][:, 3:3 + TC], s_taps[:, 4 * k + 3:4 * k + 4],
                s_scr[:, TC:2 * TC], op0=ALU.mult, op1=ALU.add), inc="v")
            t4 = pl.add("vector", lambda v, k=k: v.tensor_add(
                s_ufin[k], s_ufin[k], s_scr[:, 0:TC]), waits=W(("a", t0)), inc="v")
            m_uf[k] = pl.add("scalar", lambda a, k=k: a.activation(
                s_ufin[k], s_ufin[k], AF.Silu, bias=s_bconv[:, k:k + 1]),
                waits=W(("v", t4)), inc="a")

        if c == 0:
            tap("ufin0", [P, TC], F32, s_ufin[0][:], W(("a", m_uf[0])))
        m_uc = [None] * DT
        xp_mark = [None] * DT
        for k in range(DT):
            wcast = W(("a", m_uf[k]))
            if k >= 2:
                wcast += W(("t", xp_mark[k - 2]))
            m_uc[k] = pl.add("scalar", lambda a, k=k: a.activation(
                s_ubf[k % 2], s_ufin[k], AF.Copy), waits=wcast, inc="a")
            xp_mark[k] = pl.add("tensor", lambda te, k=k: te.matmul(
                psf[4][0:80, 0:TC], s_wxp[:, 80 * k:80 * (k + 1)],
                s_ubf[k % 2], start=(k == 0), stop=(k == DT - 1)),
                waits=W(("a", m_uc[k])), inc="t")
        mt_xp = xp_mark[-1]
        m_xdb = pl.add("scalar", lambda a: a.activation(
            s_xdb[0:80, :], psf[4][0:80, 0:TC], AF.Identity, bias=s_bxp[:, 0:1]),
            waits=W(("t", mt_xp)), inc="a")
        if c == 0:
            tap("xdb0", [80, TC], F32, s_xdb[0:80, :], W(("a", m_xdb)))
        m_dtin = pl.add("scalar", lambda a: a.activation(
            s_dtin[0:NS, :], s_xdb[0:NS, :], AF.Copy), inc="a")
        m_bbf = pl.add("scalar", lambda a: a.activation(
            s_bbf[0:NS, :], s_xdb[32:48, :], AF.Copy), inc="a")
        m_cbf = pl.add("scalar", lambda a: a.activation(
            s_cbf[0:NS, :], s_xdb[64:80, :], AF.Copy), inc="a")
        dbr = dma(sp_brow[c][0:1, :].rearrange("p (a l) -> p a l", a=NS),
                  s_bbf[0:NS, :], "rows", waits=W(("a", m_bbf)))
        dcr = dma(sp_crow[c][0:1, :].rearrange("p (a l) -> p a l", a=NS),
                  s_cbf[0:NS, :], "rows", waits=W(("a", m_cbf)))
        dbrep = dma(s_brep, sp_brow[c][0:1, :].broadcast_to([P, NS * TC]), "reps",
                    waits=W(("rows", dcr)))
        dcrep = dma(s_crep1, sp_crow[c][0:1, :].broadcast_to([P, NS * TC]), "reps",
                    waits=W(("rows", dcr)))

        prev_sp = None
        prev_spill = None
        prev_scan = [None] * NS
        for k in range(DT):
            rb = k % 2

            def dtmm(te, k=k):
                return te.matmul(psf[5][:, 0:TC], s_wdt[:, k * P:(k + 1) * P],
                                 s_dtin[0:NS, :], start=True, stop=True)
            wt = W(("a", m_dtin))
            if prev_sp is not None:
                wt += W(("a", prev_sp))
            mt = pl.add("tensor", dtmm, waits=wt, inc="t")
            wv = W(("t", mt), ("v", pl.cnt["v"]))
            if prev_spill is not None:
                wv += W(("psp", prev_spill))
            a1 = pl.add("scalar", lambda a, k=k, rb=rb: a.activation(
                s_Pb[rb], psf[5][:, 0:TC], AF.Abs, bias=s_bdt[:, k:k + 1]),
                waits=wv, inc="a")
            a2 = pl.add("scalar", lambda a, rb=rb: a.activation(
                s_dtb[rb], s_Pb[rb], AF.Exp, scale=-1.0),
                waits=W(("a", a1, 1)), inc="a")
            a3 = pl.add("scalar", lambda a, rb=rb: a.activation(
                s_Pb[rb], s_dtb[rb], AF.Ln, bias=1.0),
                waits=W(("a", a2, 1)), inc="a")
            a4 = pl.add("scalar", lambda a, k=k, rb=rb: a.activation(
                s_dtb[rb], psf[5][:, 0:TC], AF.Relu, bias=s_bdt[:, k:k + 1]),
                waits=W(("a", a3, 1)), inc="a")
            prev_sp = a4
            v1 = pl.add("vector", lambda v, rb=rb: v.tensor_add(
                s_dtb[rb], s_dtb[rb], s_Pb[rb]), waits=W(("a", a4)), inc="v")
            v2 = pl.add("vector", lambda v, k=k, rb=rb: v.tensor_mul(
                s_dtu[rb], s_dtb[rb], s_ufin[k]), inc="v")
            if c == 0 and k == 0:
                tap("dt0", [P, TC], F32, s_dtb[0][:], W(("v", v2)))
            v3 = pl.add("vector", lambda v, k=k, rb=rb: v.tensor_tensor_scan(
                s_Pb[rb], s_dtb[rb], s_dtb[rb], s_pcol[:, k:k + 1],
                op0=ALU.add, op1=ALU.bypass), inc="v")
            pl.add("scalar", lambda a, k=k, rb=rb: a.activation(
                s_pcol[:, k:k + 1], s_Pb[rb][:, TC - 1:TC], AF.Copy),
                waits=W(("v", v3)), inc="a")
            prev_spill = dma(sp_P[c * DT + k], s_Pb[rb], "psp", waits=W(("v", v3)), gq=True)
            v5a = pl.add("vector", lambda v, rb=rb: v.tensor_tensor(
                s_dbu[:, 0:4 * TC].rearrange("p (a l) -> p a l", a=4),
                s_dtu[rb].unsqueeze(1).broadcast_to([P, 4, TC]),
                s_brep[:, 0:4 * TC].rearrange("p (a l) -> p a l", a=4),
                op=ALU.mult), waits=W(("v", v2), ("reps", dbrep)), inc="v")
            v5 = pl.add("vector", lambda v, rb=rb: v.tensor_tensor(
                s_dbu[:, 4 * TC:].rearrange("p (a l) -> p a l", a=NS - 4),
                s_dtu[rb].unsqueeze(1).broadcast_to([P, NS - 4, TC]),
                s_brep[:, 4 * TC:].rearrange("p (a l) -> p a l", a=NS - 4),
                op=ALU.mult), inc="v")
            scan_marks = [None] * NS
            for n in range(NS):
                pg = n % 3
                wda = W(("v", v1))
                if n >= 3:
                    wda += W(("v", scan_marks[n - 3]))
                elif prev_scan[NS - 3 + n] is not None:
                    wda += W(("v", prev_scan[NS - 3 + n]))
                mda = pl.add("scalar", lambda a, k=k, n=n, pg=pg, rb=rb: a.activation(
                    s_da[:, pg * TC:(pg + 1) * TC], s_dtb[rb], AF.Exp,
                    scale=s_A[:, NS * k + n:NS * k + n + 1]), waits=wda, inc="a")
                scan_marks[n] = pl.add("vector", lambda v, k=k, n=n, pg=pg: v.tensor_tensor_scan(
                    s_h[:, n * TC:(n + 1) * TC], s_da[:, pg * TC:(pg + 1) * TC],
                    s_dbu[:, n * TC:(n + 1) * TC],
                    s_hscol[:, NS * k + n:NS * k + n + 1],
                    op0=ALU.mult, op1=ALU.add),
                    waits=W(("a", mda), ("v", v5a if n < 4 else v5)), inc="v")
            pl.add("scalar", lambda a, k=k: a.activation(
                s_hscol[:, NS * k:NS * (k + 1)].rearrange("p (n o) -> p n o", o=1),
                s_h.rearrange("p (n l) -> p n l", l=TC)[:, :, TC - 1:TC],
                AF.Copy), waits=W(("v", scan_marks[NS - 1])), inc="a")
            if c == 0 and k == 0:
                tap("dbu0", [P, NS * TC], BF16, s_dbu[:], W(("v", scan_marks[0])))
                tap("h0", [P, NS * TC], BF16, s_h[:], W(("v", pl.cnt["v"])))
            prev_scan = scan_marks
            vz = pl.add("vector", lambda v: v.tensor_tensor(
                s_dbu, s_h, s_crep1, op=ALU.mult), waits=W(("reps", dcrep)), inc="v")
            tb = s_tree.bitcast(BF16)
            vt = pl.add("vector", lambda v, tb=tb: v.tensor_tensor(
                tb[:, 0:8 * TC].rearrange("p (a l) -> p a l", l=TC),
                s_dbu.rearrange("p (a b l) -> p a b l", b=2, l=TC)[:, :, 0, :],
                s_dbu.rearrange("p (a b l) -> p a b l", b=2, l=TC)[:, :, 1, :],
                op=ALU.add), waits=W(("v", vz)), inc="v")
            vt = pl.add("vector", lambda v, tb=tb: v.tensor_tensor(
                tb[:, 8 * TC:12 * TC].rearrange("p (a l) -> p a l", l=TC),
                tb[:, 0:8 * TC].rearrange("p (a b l) -> p a b l", b=2, l=TC)
                [:, :, 0, :],
                tb[:, 0:8 * TC].rearrange("p (a b l) -> p a b l", b=2, l=TC)
                [:, :, 1, :], op=ALU.add), inc="v")
            vt = pl.add("vector", lambda v, tb=tb: v.tensor_tensor(
                tb[:, 12 * TC:14 * TC].rearrange("p (a l) -> p a l", l=TC),
                tb[:, 8 * TC:12 * TC].rearrange("p (a b l) -> p a b l", b=2, l=TC)
                [:, :, 0, :],
                tb[:, 8 * TC:12 * TC].rearrange("p (a b l) -> p a b l", b=2, l=TC)
                [:, :, 1, :], op=ALU.add), inc="v")
            vt = pl.add("vector", lambda v, tb=tb: v.tensor_tensor(
                s_tree[:, 7 * TC:8 * TC], tb[:, 12 * TC:13 * TC],
                tb[:, 13 * TC:14 * TC], op=ALU.add), inc="v")
            dma(sp_y[c * DT + k], s_tree[:, 7 * TC:8 * TC], "ysp",
                waits=W(("v", vt)), gq=True)
        pl.barrier()

    def store_state(sync):
        idx = sync.snap(prty_reg, min_val=0, max_val=1)
        return sync.dma_start(out=sh_statev[idx], in_=s_hscol[:])
    dhs = pl.add("sync", store_state, waits=W(("v", pl.cnt["v"]),
                                              ("a", pl.cnt["a"])), inc="hss")

    def set_flag(sync):
        idx = sync.snap(prty_reg, min_val=0, max_val=1)
        return sync.dma_start(out=sh_flagv[idx], in_=s_fli[0:1, 64:128])
    pl.add("sync", set_flag, waits=W(("hss", dhs)), inc="hsf")

    def spin_and_load(sync):
        with (nc.sync.register(name="spin_fl") as flr,
              nc.sync.register(name="spin_cond") as cnd):
            sync.reg_mov(cnd, 1)
            with sync.While(cnd):
                sync.sem_clear(spin_sem)
                sync.dma_start(out=s_spfl[:], in_=sh_flag[0:1, :]
                               ).then_inc(spin_sem, 16)
                sync.wait_ge(spin_sem, 16)
                sync.reg_load(flr, s_spfl[0:1, 0:1])
                sync.reg_alu(cnd, flr, 0, ALU.is_equal)
        return sync.dma_start(out=s_hb[:], in_=sh_state[0:P, :])
    dhb = pl.add("sync", spin_and_load, inc="hbl")
    m_hb = pl.add("vector", lambda v: v.tensor_scalar(
        s_hb[:], s_hb[:], s_hmask[:, 0:1], None, op0=ALU.mult),
        waits=W(("hbl", dhb)), inc="v")

    pl.barrier()

    AR.reset(QFR)
    s2_x = [AR.carve(D) for _ in range(TT4)]
    s2_yz_all = AR.carve(DT * TC, BF16)
    s2_yz = [s2_yz_all[:, k * TC:(k + 1) * TC] for k in range(DT)]
    s2_xln2 = [AR.carve(D) for _ in range(2)]
    s2_xq = [AR.carve(D, BF16) for _ in range(2)]
    s2_xqT_all = AR.carve(DT * TC, BF16)
    s2_xqT = [s2_xqT_all[:, k * TC:(k + 1) * TC] for k in range(DT)]
    s2_wo_all = AR.carve(DT * 256, BF16)
    s2_wo = [s2_wo_all[:, k * 256:(k + 1) * 256] for k in range(DT)]
    s2_hqT_all = AR.carve(IT * TC, BF16)
    s2_hqT = [s2_hqT_all[:, k * TC:(k + 1) * TC] for k in range(IT)]
    s2_wg2 = [AR.carve(DT * 384, BF16) for _ in range(2)]
    s2_wu2 = [AR.carve(DT * 384, BF16) for _ in range(2)]
    OV = AR.mark()
    s2_Pl = [AR.carve(TC) for _ in range(2)]
    s2_yl = [AR.carve(TC) for _ in range(2)]
    s2_zl = [AR.carve(TC, BF16) for _ in range(2)]
    s2_crep = AR.carve(NS * TC, BF16)
    s2_E = AR.carve(2 * TC, BF16)
    s2_z2 = AR.carve(NS * TC, BF16)
    s2_tree = AR.carve(8 * TC)
    s2_y = AR.carve(TC)
    AR.reset(OV)
    s2_wd2 = [AR.carve(IT * 256, BF16) for _ in range(2)]
    hffn_buf = [AR.carve(IF_, BF16) for _ in range(4)]
    hq_sl = [AR.carve(1408, BF16) for _ in range(2)]
    s2_gs2 = [AR.carve(384, BF16) for _ in range(2)]


    for c in range(NCH):
        dxm = [dma(s2_x[mi], dx[c * TC + mi * P:c * TC + (mi + 1) * P, :], "x2l")
               for mi in range(TT4)]
        dcrep = dma(s2_crep, sp_crow[c][0:1, :].broadcast_to([P, NS * TC]), "crep2")

        m_yz = [None] * DT
        for k in range(DT):
            rb = k % 2
            tw = W(("v", m_yz[k - 2])) if k >= 2 else ()
            dPl = dma(s2_Pl[rb], sp_P[c * DT + k], "trio", waits=tw)
            dyl = dma(s2_yl[rb], sp_y[c * DT + k], "trio")
            dzl = dma(s2_zl[rb], sp_zs[c * DT + k], "trio")
            em = [None] * NS
            for n in range(NS):
                pg = n % 2
                we = W(("trio", dzl), ("v", m_hb))
                if n >= 2:
                    we += W(("v", em[n - 2]))
                me = pl.add("scalar", lambda a, k=k, n=n, pg=pg, rb=rb: a.activation(
                    s2_E[:, pg * TC:(pg + 1) * TC], s2_Pl[rb], AF.Exp,
                    scale=s_A[:, NS * k + n:NS * k + n + 1]), waits=we, inc="a")
                em[n] = pl.add("vector", lambda g, k=k, n=n, pg=pg:
                    g.scalar_tensor_tensor(
                        s2_z2[:, n * TC:(n + 1) * TC],
                        s2_E[:, pg * TC:(pg + 1) * TC],
                        s_hb[:, NS * k + n:NS * k + n + 1],
                        s2_crep[:, n * TC:(n + 1) * TC],
                        op0=ALU.mult, op1=ALU.mult),
                    waits=W(("a", me), ("crep2", dcrep), ("v", m_hb)), inc="v")
            tb2 = s2_tree.bitcast(BF16)
            vt = pl.add("vector", lambda v, tb2=tb2: v.tensor_tensor(
                tb2[:, 0:8 * TC].rearrange("p (a l) -> p a l", l=TC),
                s2_z2.rearrange("p (a b l) -> p a b l", b=2, l=TC)[:, :, 0, :],
                s2_z2.rearrange("p (a b l) -> p a b l", b=2, l=TC)[:, :, 1, :],
                op=ALU.add), inc="v")
            vt = pl.add("vector", lambda v, tb2=tb2: v.tensor_tensor(
                tb2[:, 8 * TC:12 * TC].rearrange("p (a l) -> p a l", l=TC),
                tb2[:, 0:8 * TC].rearrange("p (a b l) -> p a b l", b=2, l=TC)
                [:, :, 0, :],
                tb2[:, 0:8 * TC].rearrange("p (a b l) -> p a b l", b=2, l=TC)
                [:, :, 1, :], op=ALU.add), inc="v")
            vt = pl.add("vector", lambda v, tb2=tb2: v.tensor_tensor(
                tb2[:, 12 * TC:14 * TC].rearrange("p (a l) -> p a l", l=TC),
                tb2[:, 8 * TC:12 * TC].rearrange("p (a b l) -> p a b l", b=2,
                                                 l=TC)[:, :, 0, :],
                tb2[:, 8 * TC:12 * TC].rearrange("p (a b l) -> p a b l", b=2,
                                                 l=TC)[:, :, 1, :],
                op=ALU.add), inc="v")
            vy = pl.add("vector", lambda v, rb=rb, tb2=tb2: v.scalar_tensor_tensor(
                s2_y, tb2[:, 12 * TC:13 * TC], 1.0, tb2[:, 13 * TC:14 * TC],
                op0=ALU.bypass, op1=ALU.add, accum_out=None),
                waits=W(("trio", dzl)), inc="v")
            vy = pl.add("vector", lambda v, rb=rb: v.tensor_tensor(
                s2_y, s2_y, s2_yl[rb], op=ALU.add), inc="v")
            m_yz[k] = pl.add("vector", lambda v, k=k, rb=rb: v.tensor_tensor(
                s2_yz[k], s2_y, s2_zl[rb], op=ALU.mult),
                waits=W(("trio", dzl)), inc="v")

        m_res1 = {}
        for wq4 in range(4):
            dwo = [dma(s2_wo[k], g_wout[k * P:(k + 1) * P, wq4 * 256:(wq4 + 1) * 256],
                       "wol", waits=W(("t", pl.cnt.get("t", 0))) if wq4 > 0 else ())
                   for k in range(DT)]
            for mi in range(TT4):
                def womm(te, mi=mi, wq4=wq4, dwo=dwo):
                    te.wait_ge(sems["wol"], dwo[-1])
                    for k in range(DT):
                        te.matmul(psf[mi][:, 0:256], s2_yz[k][:, mi * P:(mi + 1) * P],
                                  s2_wo[k], start=(k == 0), stop=False)
                    return te.matmul(psf[mi][:, 0:256], s_ones1[0:1, :],
                                     s_boutr[0:1, wq4 * 256:(wq4 + 1) * 256],
                                     start=False, stop=True)
                wt = W(("v", m_yz[DT - 1]))
                prev = m_res1.get((wq4 - 1, mi))
                if prev is not None:
                    wt += W(("v", prev))
                mt = pl.add("tensor", womm, waits=wt, inc="t")
                m_res1[(wq4, mi)] = pl.add("vector",
                    lambda v, mi=mi, wq4=wq4: v.tensor_tensor(
                        s2_x[mi][:, wq4 * 256:(wq4 + 1) * 256],
                        s2_x[mi][:, wq4 * 256:(wq4 + 1) * 256],
                        psf[mi][:, 0:256], op=ALU.add),
                    waits=W(("t", mt), ("x2l", dxm[-1])), inc="v")

        if TAPS and c == 0:
            tap("xmid0", [P, D], F32, s2_x[0][:], W(("v", m_res1[(3, 0)])))
            tap("xmid3", [P, D], F32, s2_x[3][:], W(("v", m_res1[(3, 3)])))
            tap("yz0", [P, TC], BF16, s2_yz[0][:, 0:TC], W(("v", m_yz[0])))
        m_xqT_mi = [None] * TT4
        for mi in range(TT4):
            rb = mi % 2
            wv = W(("v", m_res1[(3, mi)]))
            mln = ln_tile(s2_x[mi], s2_xln2[rb], s2_xln2[1 - rb][:, 0:D],
                          wait_v=wv, wait_a=wv)
            mq = quant_tile(s2_xln2[rb], s2_xq[rb], P, D,
                            wait_v=W(("v", mln), ("t", pl.cnt.get("t", 0))),
                            wait_a=W(("v", mln)))
            if TAPS and c == 0 and mi == 0:
                tap("xq0", [P, D], BF16, s2_xq[0][:], W(("v", mq)))
            last_tc = None
            for g in range(2):
                bank = 4 + (g % 2)
                wt = W(("v", mq))
                if last_tc is not None:
                    wt += W(("a", last_tc))

                def tr4(te, g=g, rb=rb, bank=bank):
                    for j in range(4):
                        r = te.transpose(psbf(bank)[0:P, j * P:(j + 1) * P],
                                         s2_xq[rb][:, (g * 4 + j) * P:
                                                   (g * 4 + j + 1) * P],
                                         s_ident[:])
                    return r
                mt = pl.add("tensor", tr4, waits=wt, inc="t")
                last_tc = pl.add("scalar", lambda a, g=g, mi=mi, bank=bank: a.activation(
                    s2_xqT_all.rearrange("p (k l) -> p k l", l=TC)
                    [:, g * 4:(g + 1) * 4, mi * P:(mi + 1) * P],
                    psbf(bank)[0:P, 0:4 * P].rearrange("p (k l) -> p k l", l=P),
                    AF.Copy), waits=W(("t", mt)), inc="a")
            m_xqT_mi[mi] = last_tc
        m_xqT = pl.cnt["a"]

        SLICES = [(j * 384, 384) for j in range(7)] + [(2688, 128)]
        hq_marks = [None] * TT4
        prev_gu = {}
        bank_ev = {}
        prev_tr = {}
        prev_hq_w = {}
        last_mq2 = 0
        tpos = 0
        res1_last = m_res1[(3, 3)]
        for si, (w0, wN) in enumerate(SLICES):
            ws = si % 2
            wqw = W(("t", prev_gu[ws])) if ws in prev_gu else ()
            for k in range(DT):
                dlg = dma(s2_wg2[ws][:, k * 384:k * 384 + wN],
                          g_wgT[k * P:(k + 1) * P, w0:w0 + wN], "gul",
                          waits=wqw, gq=True)
                dls = dma(s2_wu2[ws][:, k * 384:k * 384 + wN],
                          g_wuT[k * P:(k + 1) * P, w0:w0 + wN], "gus",
                          waits=wqw)
            dlast = (dlg, dls)
            for mi in range(TT4):
                bp = mi % 2
                bg, bu = 2 * bp, 2 * bp + 1

                def gumm(te, mi=mi, ws=ws, wN=wN, bg=bg, bu=bu, dlast=dlast):
                    te.wait_ge(sems["gul"], dlast[0])
                    te.wait_ge(sems["gus"], dlast[1])
                    for k in range(DT):
                        te.matmul(psf[bg][:, 0:wN],
                                  s2_xqT[k][:, mi * P:(mi + 1) * P],
                                  s2_wg2[ws][:, k * 384:k * 384 + wN],
                                  start=(k == 0), stop=(k == DT - 1))
                    last = None
                    for k in range(DT):
                        last = te.matmul(psf[bu][:, 0:wN],
                                         s2_xqT[k][:, mi * P:(mi + 1) * P],
                                         s2_wu2[ws][:, k * 384:k * 384 + wN],
                                         start=(k == 0), stop=(k == DT - 1))
                    return last
                wt = W(("a", m_xqT_mi[mi]))
                if si == 0:
                    wt += W(("v", res1_last))
                if bp in bank_ev:
                    wt += W(("a", bank_ev[bp][0]), ("v", bank_ev[bp][1]))
                mt = pl.add("tensor", gumm, waits=wt, inc="t")
                prev_gu[ws] = mt
                ma = pl.add("scalar", lambda a, bp=bp, bg=bg, wN=wN: a.activation(
                    s2_gs2[bp][:, 0:wN], psf[bg][:, 0:wN], AF.Silu),
                    waits=W(("t", mt)), inc="a")
                h1 = pl.add("vector", lambda v, mi=mi, bu=bu, wN=wN, w0=w0:
                    v.scalar_tensor_tensor(
                        hffn_buf[mi][:, w0:w0 + wN], psf[bu][:, 0:wN], 0.0,
                        s2_gs2[mi % 2][:, 0:wN], op0=ALU.bypass, op1=ALU.mult),
                    waits=W(("a", ma)), inc="v")
                bank_ev[bp] = (ma, h1)
                if si in (3, 7):
                    h = 0 if si == 3 else 1
                    hw0 = h * 1408
                    wv = W(("v", h1))
                    if bp in prev_hq_w:
                        wv += W(("t", prev_hq_w[bp]))
                    mq = quant_tile(hffn_buf[mi][:, hw0:hw0 + 1408],
                                    hq_sl[bp][:, 0:1408], P, 1408,
                                    wait_v=wv,
                                    wait_a=W(("v", h1), ("v", last_mq2)),
                                    lowp=True)
                    last_mq2 = mq
                    for g in range(3):
                        ntile = (4, 4, 3)[g]
                        bank = 6 + (tpos % 2)
                        tpos += 1
                        wtr = W(("v", mq))
                        if bank in prev_tr:
                            wtr += W(("a", prev_tr[bank]))

                        def trN(te, bp=bp, bank=bank, ntile=ntile, g=g):
                            for j in range(ntile):
                                r = te.transpose(
                                    psbf(bank)[0:P, j * P:(j + 1) * P],
                                    hq_sl[bp][:, (g * 4 + j) * P:
                                              (g * 4 + j + 1) * P],
                                    s_ident[:])
                            return r
                        mtr = pl.add("tensor", trN, waits=wtr, inc="t")
                        ev = pl.add("scalar", lambda a, h=h, g=g, mi=mi,
                                    bank=bank, ntile=ntile: a.activation(
                            s2_hqT_all.rearrange("p (k l) -> p k l", l=TC)
                            [:, h * 11 + g * 4:h * 11 + g * 4 + ntile,
                             mi * P:(mi + 1) * P],
                            psbf(bank)[0:P, 0:ntile * P]
                            .rearrange("p (k l) -> p k l", l=P),
                            AF.Copy), waits=W(("t", mtr)), inc="a")
                        prev_tr[bank] = ev
                    prev_hq_w[bp] = mtr
                    hq_marks[mi] = ev

        res2_marks = {}
        prev_dmm = {}
        for dp in range(4):
            ws = dp % 2
            wqw = W(("t", prev_dmm[ws])) if ws in prev_dmm else ()
            for k in range(IT):
                dd = dma(s2_wd2[ws][:, k * 256:(k + 1) * 256],
                         g_wdT[k * P:(k + 1) * P,
                               dp * 256:(dp + 1) * 256],
                         "wdl" if k % 2 == 0 else "wds",
                         waits=wqw, gq=(k % 2 == 0))
                if k % 2 == 0:
                    dlw = dd
                else:
                    dlx = dd
            dlast = (dlw, dlx)
            for mi in range(TT4):
                bank = 4 + (mi % 2)

                def dmm(te, mi=mi, ws=ws, bank=bank, dlast=dlast):
                    te.wait_ge(sems["wdl"], dlast[0])
                    te.wait_ge(sems["wds"], dlast[1])
                    last = None
                    for k in range(IT):
                        last = te.matmul(psf[bank][:, 0:256],
                                         s2_hqT[k][:, mi * P:(mi + 1) * P],
                                         s2_wd2[ws][:, k * 256:(k + 1) * 256],
                                         start=(k == 0), stop=(k == IT - 1))
                    return last
                wt = W(("a", hq_marks[mi]))
                key = (dp - 1, mi) if dp > 0 else None
                prevu = res2_marks.get((dp, mi - 2), res2_marks.get((dp - 1, mi + 2)))
                if prevu is not None:
                    wt += W(("v", prevu))
                mt = pl.add("tensor", dmm, waits=wt, inc="t")
                prev_dmm[ws] = mt
                res2_marks[(dp, mi)] = pl.add("vector",
                    lambda v, mi=mi, dp=dp, bank=bank: v.tensor_tensor(
                        s2_x[mi][:, dp * 256:(dp + 1) * 256],
                        s2_x[mi][:, dp * 256:(dp + 1) * 256],
                        psf[bank][:, 0:256], op=ALU.add),
                    waits=W(("t", mt)), inc="v")
        if TAPS and c == 0:
            tap("hqT0", [P, TC], BF16, s2_hqT[0][:, 0:TC],
                W(("a", hq_marks[0])))
            tap("out0", [P, D], F32, s2_x[0][:], W(("v", res2_marks[(3, 0)])))
        for mi in range(TT4):
            ostore(dout[c * TC + mi * P:c * TC + (mi + 1) * P, :], s2_x[mi],
                   waits=W(("v", res2_marks[(3, mi)])))
        pl.barrier()

    tap("wgT0", [P, IF_], BF16, None, ())
    if TAPS:
        tap_t["wgT0"] = nc.dram_tensor("tap_wgT0", [P, IF_], BF16,
                                       kind="ExternalOutput").ap()
        dma(tap_t["wgT0"], g_wgT[0:P, :], "tap", waits=())
    o_final = pl.cnt["outs"]
    pl.add("sync", lambda e: e.wait_ge(sems["outs"], o_final))

    for g in sorted(pl.groups):
        sems[g] = ec(nc.semaphore("sem_" + g))

    @block.sync
    def _(e):
        pl.emit("sync", e, sems)

    @block.vector
    def _(e):
        pl.emit("vector", e, sems, selfsync=SELFSYNC)

    @block.scalar
    def _(e):
        pl.emit("scalar", e, sems, selfsync=SELFSYNC)

    @block.tensor
    def _(e):
        pl.emit("tensor", e, sems, selfsync=SELFSYNC)

    @block.gpsimd
    def _(e):
        pl.emit("gpsimd", e, sems)

    stack.close()
    _CACHE["nc"] = nc
    return nc


def shard_inputs(inputs):
    x = np.ascontiguousarray(inputs["L0_attn_x"])
    W_in = inputs["L0_attn_W_in"]
    b_in = inputs["L0_attn_b_in"]
    W_conv = np.asarray(inputs["L0_attn_W_conv"]).reshape(D, 4)
    b_conv = inputs["L0_attn_b_conv"]
    W_xp = inputs["L0_attn_W_xp"]
    b_xp = inputs["L0_attn_b_xp"]
    W_dt = inputs["L0_attn_W_dt"]
    b_dt = inputs["L0_attn_b_dt"]
    A_log = inputs["L0_attn_A_log"]
    W_out = inputs["L0_attn_W_out"]
    b_out = inputs["L0_attn_b_out"]
    w_gate = inputs["L0_ffn_w_gate"]
    w_up = inputs["L0_ffn_w_up"]
    w_down = inputs["L0_ffn_w_down"]

    f32 = np.float32
    wxp_pad = np.zeros((D, 80), f32)
    wxp_pad[:, 0:16] = W_xp[:, 0:16]
    wxp_pad[:, 32:48] = W_xp[:, 16:32]
    wxp_pad[:, 64:80] = W_xp[:, 32:48]
    bxp_pad = np.zeros((80, 1), f32)
    bxp_pad[0:16, 0] = b_xp[0:16]
    bxp_pad[32:48, 0] = b_xp[16:32]
    bxp_pad[64:80, 0] = b_xp[32:48]
    bdt_c = np.ascontiguousarray(np.asarray(b_dt).reshape(DT, P).T, dtype=f32)
    bconv_c = np.ascontiguousarray(np.asarray(b_conv).reshape(DT, P).T, dtype=f32)
    bin_c = np.ascontiguousarray(np.asarray(b_in).reshape(16, P).T, dtype=f32)
    taps_c = np.zeros((P, DT * 4), f32)
    for k in range(DT):
        taps_c[:, 4 * k:4 * (k + 1)] = W_conv[k * P:(k + 1) * P, :]
    alog_c = np.zeros((P, DT * NS), f32)
    for k in range(DT):
        alog_c[:, NS * k:NS * (k + 1)] = A_log[k * P:(k + 1) * P, :]

    win_f = np.ascontiguousarray(W_in, dtype=f32)
    wout_f = np.ascontiguousarray(W_out, dtype=f32)
    wg_f = np.ascontiguousarray(w_gate, dtype=f32)
    wu_f = np.ascontiguousarray(w_up, dtype=f32)
    wd_f = np.ascontiguousarray(w_down, dtype=f32)
    maps = []
    for c in range(8):
        b, h = c // 2, c % 2
        xs = np.ascontiguousarray(x[b, h * 4096:(h + 1) * 4096, :], dtype=f32)
        if h == 1:
            xt = np.ascontiguousarray(x[b, 4093:4096, :], dtype=f32)
            hm = np.ones((P, 1), f32)
        else:
            xt = np.zeros((3, D), f32)
            hm = np.zeros((P, 1), f32)
        maps.append({
            "x": xs, "xtail": xt, "hmask": hm,
            "win_f": win_f, "wout_f": wout_f, "wg_f": wg_f, "wu_f": wu_f,
            "wd_f": wd_f,
            "prty": np.array([[h]], np.int32),
            "wxp": wxp_pad, "bxp": bxp_pad,
            "wdt": np.ascontiguousarray(W_dt, dtype=f32),
            "bdt": bdt_c, "bin": bin_c, "taps": taps_c, "bconv": bconv_c,
            "alog": alog_c,
            "bout_row": np.ascontiguousarray(np.asarray(b_out).reshape(1, D), dtype=f32),
        })
    return maps


def kernel(**inputs):
    nc = build_program()
    maps = shard_inputs(inputs)
    res = run_bass_kernel_spmd(nc, maps, core_ids=list(range(8)))
    out = np.zeros((4, 8192, D), np.float32)
    for c in range(8):
        b, h = c // 2, c % 2
        out[b, h * 4096:(h + 1) * 4096, :] = np.asarray(res.results[c]["out"])
    if TAPS:
        global _LAST_TAPS
        _LAST_TAPS = {}
        for c in (0, 1):
            for k, v in res.results[c].items():
                if k.startswith("tap_"):
                    _LAST_TAPS[f"c{c}_{k[4:]}"] = np.asarray(v)
    return out


# version 27 (final)
# speedup vs baseline: 1.2037x; 1.0053x over previous
import numpy as np
import ml_dtypes
import concourse.bass as bass
import concourse.mybir as mybir
from concourse.bass_utils import run_bass_kernel_spmd

F32 = mybir.dt.float32
BF16 = mybir.dt.bfloat16
AF = mybir.ActivationFunctionType
ALU = mybir.AluOpType
AX = mybir.AxisListType

import os
P = 128
D = 1024
DT = 8
NS = 16
TC = 512
NCH = int(os.environ.get("ANT_NCHUNK", "8"))
TAPS = os.environ.get("ANT_TAPS", "") == "1"
SELFSYNC = os.environ.get("ANT_SELFSYNC", "0") == "1"
DR = mybir.MatmulPerfMode.DoubleRow
TT4 = 4
IF_ = 2816
IT = 22
ISH = 352
IQ = 704
TAU = [0.0625, 0.5625, 1.5625, 3.0625, 6.25, 12.25, 25.0]

_CACHE = {}


class Plan:
    ENGS = ("sync", "vector", "scalar", "tensor", "gpsimd")
    OWN = {"sync": None, "vector": "v", "scalar": "a", "tensor": "t",
           "gpsimd": "g"}

    def __init__(self):
        self.ops = {e: [] for e in self.ENGS}
        self.cnt = {}
        self.pending = {e: [] for e in self.ENGS}
        self.groups = set()

    def bump(self, name, amt):
        self.cnt[name] = self.cnt.get(name, 0) + amt
        self.groups.add(name)
        return self.cnt[name]

    def add(self, eng, fn, waits=(), inc=None, amt=None):
        w = self.pending[eng] + [x for x in waits if x[1]]
        self.pending[eng] = []
        mark = None
        if inc is not None:
            if amt is None:
                amt = 1 if inc in ("v", "a", "t") else 16
            mark = self.bump(inc, amt)
        self.ops[eng].append((fn, tuple(w), inc, amt))
        return mark

    def barrier(self):
        snap = dict(self.cnt)
        for e in self.ENGS:
            own = self.OWN[e]
            self.pending[e] += [(c, v) for c, v in snap.items()
                                if c != own and v > 0]

    def emit(self, eng, handle, sems, selfsync=False):
        seen = {}
        own = self.OWN[eng]
        pos = 0
        for fn, waits, inc, amt in self.ops[eng]:
            best = {}
            forced = 0
            for w in waits:
                ctr, val = w[0], w[1]
                if ctr == own:
                    if len(w) >= 3:
                        forced = max(forced, val)
                    continue
                if val > best.get(ctr, seen.get(ctr, 0)):
                    best[ctr] = val
            if selfsync and own is not None:
                forced = max(forced, pos)
            if forced:
                handle.wait_ge(sems[own], forced)
            for ctr, val in best.items():
                handle.wait_ge(sems[ctr], val)
                seen[ctr] = val
            last = fn(handle)
            if inc is not None:
                last.then_inc(sems[inc], amt)
                if inc == own:
                    pos += amt


class Arena:
    def __init__(self, nc, ec, name, words):
        self.t = ec(nc.sbuf_tensor(name, [P, words], F32))
        self.off = 0
        self.words = words

    def carve(self, cols, dtype=F32):
        w = cols if dtype == F32 else (cols + 1) // 2
        w = (w + 15) // 16 * 16
        assert self.off + w <= self.words, f"arena overflow {self.off + w} > {self.words}"
        base = self.t[:, self.off:self.off + w]
        self.off += w
        if dtype != F32:
            return base.bitcast(dtype)[:, 0:cols]
        return base[:, 0:cols]

    def mark(self):
        return self.off

    def reset(self, m):
        self.off = m


def build_program():
    if "nc" in _CACHE:
        return _CACHE["nc"]
    nc = bass.Bass("TRN2", target_bir_lowering=False)

    dx = nc.dram_tensor("x", [4096, D], F32, kind="ExternalInput").ap()
    dxtail = nc.dram_tensor("xtail", [3, D], F32, kind="ExternalInput").ap()
    dhmask = nc.dram_tensor("hmask", [P, 1], F32, kind="ExternalInput").ap()
    dwin_f = nc.dram_tensor("win_f", [D, 2 * D], F32, kind="ExternalInput").ap()
    dwout_f = nc.dram_tensor("wout_f", [D, D], F32, kind="ExternalInput").ap()
    dwg_f = nc.dram_tensor("wg_f", [IF_, D], F32, kind="ExternalInput").ap()
    dwu_f = nc.dram_tensor("wu_f", [IF_, D], F32, kind="ExternalInput").ap()
    dwd_f = nc.dram_tensor("wd_f", [D, IF_], F32, kind="ExternalInput").ap()
    dprty = nc.dram_tensor("prty", [1, 1], mybir.dt.int32, kind="ExternalInput").ap()
    dwxp = nc.dram_tensor("wxp", [DT * P, 80], F32, kind="ExternalInput").ap()
    dbxp = nc.dram_tensor("bxp", [80, 1], F32, kind="ExternalInput").ap()
    dwdt = nc.dram_tensor("wdt", [NS, D], F32, kind="ExternalInput").ap()
    dbdt = nc.dram_tensor("bdt", [P, DT], F32, kind="ExternalInput").ap()
    dbin = nc.dram_tensor("bin", [P, 16], F32, kind="ExternalInput").ap()
    dtaps = nc.dram_tensor("taps", [P, DT * 4], F32, kind="ExternalInput").ap()
    dbconv = nc.dram_tensor("bconv", [P, DT], F32, kind="ExternalInput").ap()
    dalog = nc.dram_tensor("alog", [P, DT * NS], F32, kind="ExternalInput").ap()
    dbout_row = nc.dram_tensor("bout_row", [1, D], F32, kind="ExternalInput").ap()
    dout = nc.dram_tensor("out", [4096, D], F32, kind="ExternalOutput").ap()

    def idram(name, shape, dt=F32, shared=False):
        return nc.dram_tensor(name, shape, dt, kind="Internal",
                              addr_space="Shared" if shared else "Local").ap()

    g_win = idram("g_win", [D, 2 * D], BF16)
    g_wout = idram("g_wout", [D, D], BF16)
    g_wgT = idram("g_wgT", [D, IF_], BF16)
    g_wuT = idram("g_wuT", [D, IF_], BF16)
    g_wdT = idram("g_wdT", [4 * P, IT * 256], BF16)
    sh_state = idram("sh_state", [2 * P, P], shared=True)
    sh_flag = idram("sh_flag", [2, 64], mybir.dt.int32, shared=True)
    sp_P = idram("sp_P", [NCH * DT, P, TC], F32)
    sp_y = idram("sp_y", [NCH * DT, P, TC], F32)
    sp_zs = idram("sp_zs", [NCH * DT, P, TC], BF16)
    sp_brow = idram("sp_brow", [NCH, 1, NS * TC], BF16)
    sp_crow = idram("sp_crow", [NCH, 1, NS * TC], BF16)

    dident = nc.inline_tensor(np.eye(P, dtype=ml_dtypes.bfloat16), name="ident").ap()

    pl = Plan()
    from contextlib import ExitStack
    stack = ExitStack()
    ec = stack.enter_context

    def sb(name, shape, dt=F32):
        return ec(nc.sbuf_tensor(name, shape, dt))

    s_ident = sb("s_ident", [P, P], BF16)
    s_hmask = sb("s_hmask", [P, 1])
    s_taps = sb("s_taps", [P, DT * 4])
    s_bconv = sb("s_bconv", [P, DT])
    s_bdt = sb("s_bdt", [P, DT])
    s_bin = sb("s_bin", [P, 16])
    s_A = sb("s_A", [P, DT * NS])
    s_wxp = sb("s_wxp", [P, DT * 80], BF16)
    s_wdt = sb("s_wdt", [NS, D], BF16)
    s_bxp = sb("s_bxp", [80, 1])
    s_ones1 = sb("s_ones1", [1, P], BF16)
    s_boutr = sb("s_boutr", [1, D], BF16)
    s_hscol = sb("s_hscol", [P, P])
    s_pcol = sb("s_pcol", [P, DT])
    s_hb = sb("s_hb", [P, P])
    s_col = sb("s_col", [P, 16])
    s_utail = sb("s_utail", [P, 3 * DT])
    s_fli = sb("s_fli", [1, 128], mybir.dt.int32)
    s_prk = sb("s_prk", [1, 1], mybir.dt.int32)
    s_spfl = sb("s_spfl", [1, 64], mybir.dt.int32)
    spin_sem = ec(nc.semaphore("spin_sem"))
    prty_reg = ec(nc.sync.register(name="prty_reg"))
    sh_flagv = sh_flag.rearrange("(s o) c -> s o c", o=1)
    sh_statev = sh_state.rearrange("(s p) c -> s p c", p=P)
    s_cst = sb("s_cst", [P, 12])
    CONSTS = [1e-5, -0.5, -1.0, 1.0, 1.0 / D, -6e-12, 6e-12, 1.0 / 12.0, 6.0,
              5e-13]
    for ci, cv in enumerate(CONSTS):
        nc.const_aps.register(F32, cv, s_cst[:, ci:ci + 1])

    AR = Arena(nc, ec, "arena", 50992)
    FR = AR.mark()

    q_m = AR.carve(IF_ // 16)
    q_r = AR.carve(IF_ // 16)
    q_rsq = AR.carve(IF_ // 16)
    q_sc2 = AR.carve(IF_ // 16)
    q_xsq = AR.carve(IF_)
    q_acc = AR.carve(IF_)
    q_b7 = AR.carve(IF_)
    q_sgn = AR.carve(IF_, BF16)
    QFR = AR.mark()

    psf = [ec(nc.psum_tensor(f"psf{i}", [P, 512], F32)) for i in range(8)]

    def psbf(i):
        return psf[i][:].bitcast(BF16)

    sems = {}
    block = ec(nc.Block())

    W = lambda *pairs: list(pairs)

    def dma(dst, src, grp, waits=(), gq=False):
        eng = "gpsimd" if gq else "sync"
        return pl.add(eng, lambda e, dst=dst, src=src: e.dma_start(out=dst, in_=src),
                      waits=waits, inc=grp)

    tap_t = {}
    def tap(name, shape, dtype, src_ap, waits):
        if not TAPS or src_ap is None:
            return
        tap_t[name] = nc.dram_tensor("tap_" + name, shape, dtype,
                                     kind="ExternalOutput").ap()
        dma(tap_t[name], src_ap, "tap", waits=waits)
        pl.barrier()

    def ostore(dst, src, waits=()):
        return pl.add("sync", lambda e, dst=dst, src=src: e.dma_start(out=dst, in_=src),
                      waits=waits, inc="outs")

    QS0 = dict(m=q_m, r=q_r, rsq=q_rsq, sc2=q_sc2, xsq=q_xsq, acc=q_acc,
               b7=q_b7, sgn=q_sgn, xsqb=q_xsq.bitcast(BF16),
               accb=q_acc.bitcast(BF16), b7b=q_b7.bitcast(BF16),
               rsqb=q_rsq.bitcast(BF16))

    def qhead(qs, xin_ap, PN, F, wait_v=(), wait_a=(), lowp=False):
        nb = F // 16
        xsq = qs["xsqb"] if lowp else qs["xsq"]
        m_red = pl.add("vector", lambda v: v.tensor_reduce(
            qs["m"][0:PN, 0:nb].unsqueeze(2),
            xin_ap.rearrange("p (a c) -> p a c", c=16),
            axis=AX.X, op=ALU.max, apply_absolute_value=True),
            waits=list(wait_v), inc="v")
        m_sq = pl.add("scalar", lambda a: a.activation(
            xsq[0:PN, 0:F], xin_ap, AF.Square), waits=list(wait_a), inc="a")
        m_sg = pl.add("scalar", lambda a: a.activation(
            qs["sgn"][0:PN, 0:F], xin_ap, AF.Sign), inc="a")
        m_cl1 = pl.add("scalar", lambda a: a.activation(
            qs["r"][0:PN, 0:nb], qs["m"][0:PN, 0:nb], AF.Relu, bias=-6e-12),
            waits=W(("v", m_red)), inc="a")
        pl.add("scalar", lambda a: a.activation(
            qs["sc2"][0:PN, 0:nb], qs["r"][0:PN, 0:nb], AF.Copy,
            scale=1.0 / 12.0, bias=5e-13), waits=W(("a", m_cl1, 1)), inc="a")
        m_cl = pl.add("scalar", lambda a: a.activation(
            qs["m"][0:PN, 0:nb], qs["r"][0:PN, 0:nb], AF.Copy, bias=6e-12),
            waits=W(("a", m_cl1, 1)), inc="a")
        m_rc = pl.add("vector", lambda v: v.reciprocal(
            qs["r"][0:PN, 0:nb], qs["m"][0:PN, 0:nb]),
            waits=W(("a", m_cl)), inc="v")
        m_rsq = pl.add("scalar", lambda a: a.activation(
            (qs["rsqb"] if lowp else qs["rsq"])[0:PN, 0:nb],
            qs["r"][0:PN, 0:nb], AF.Square, scale=6.0),
            waits=W(("v", m_rc)), inc="a")
        return (m_red, m_sq, m_sg, m_rsq)

    def qtail(qs, xout_ap, PN, F, marks, wait_v=(), lowp=False):
        nb = F // 16
        m_red, m_sq, m_sg, m_rsq = marks
        xsq = qs["xsqb"] if lowp else qs["xsq"]
        acc = qs["accb"] if lowp else qs["acc"]
        b7 = qs["b7b"] if lowp else qs["b7"]
        rsq = qs["rsqb"] if lowp else qs["rsq"]
        rsq_bc = rsq[0:PN, 0:nb].unsqueeze(2).broadcast_to([PN, nb, 16])
        sc2_bc = qs["sc2"][0:PN, 0:nb].unsqueeze(2).broadcast_to([PN, nb, 16])
        pl.add("vector", lambda v: v.tensor_tensor(
            xsq[0:PN, 0:F].rearrange("p (a c) -> p a c", c=16),
            xsq[0:PN, 0:F].rearrange("p (a c) -> p a c", c=16), rsq_bc,
            op=ALU.mult),
            waits=W(("a", m_sq), ("a", m_rsq)) + list(wait_v), inc="v")
        pl.add("vector", lambda v: v.tensor_scalar(
            acc[0:PN, 0:F], xsq[0:PN, 0:F], TAU[0], None, op0=ALU.is_gt),
            inc="v")
        for j in (1, 2, 3):
            pl.add("vector", lambda v, j=j: v.scalar_tensor_tensor(
                acc[0:PN, 0:F], xsq[0:PN, 0:F], TAU[j], acc[0:PN, 0:F],
                op0=ALU.is_gt, op1=ALU.add), inc="v")
        pl.add("vector", lambda v: v.tensor_scalar(
            b7[0:PN, 0:F], xsq[0:PN, 0:F], TAU[6], None, op0=ALU.is_gt),
            inc="v")
        for j in (6, 4, 5):
            pl.add("vector", lambda v, j=j: v.scalar_tensor_tensor(
                b7[0:PN, 0:F], xsq[0:PN, 0:F], TAU[j], b7[0:PN, 0:F],
                op0=ALU.is_gt, op1=ALU.add), inc="v")
        pl.add("vector", lambda v: v.scalar_tensor_tensor(
            acc[0:PN, 0:F], b7[0:PN, 0:F], 2.0, acc[0:PN, 0:F],
            op0=ALU.mult, op1=ALU.add), inc="v")
        pl.add("vector", lambda v: v.tensor_tensor(
            acc[0:PN, 0:F], acc[0:PN, 0:F], qs["sgn"][0:PN, 0:F],
            op=ALU.mult), waits=W(("a", m_sg)), inc="v")
        return pl.add("vector", lambda v: v.tensor_tensor(
            xout_ap.rearrange("p (a c) -> p a c", c=16),
            acc[0:PN, 0:F].rearrange("p (a c) -> p a c", c=16), sc2_bc,
            op=ALU.mult), inc="v")

    def quant_tile(xin_ap, xout_ap, PN, F, wait_v=(), wait_a=(), lowp=False):
        marks = qhead(QS0, xin_ap, PN, F, wait_v=wait_v, wait_a=wait_a,
                      lowp=lowp)
        return qtail(QS0, xout_ap, PN, F, marks, lowp=lowp)

    def ln_tile(x_ap, out_ap, scr, wait_v=(), wait_a=()):
        ms = pl.add("vector", lambda v: v.reduce_sum(s_col[:, 0:1], x_ap, axis=AX.X),
                    waits=list(wait_v), inc="v")
        ma = pl.add("scalar", lambda a: a.activation(scr, x_ap, AF.Square,
                                                     accum_out=s_col[:, 1:2]),
                    waits=list(wait_a), inc="a")

        mam1 = pl.add("scalar", lambda a: a.activation(
            s_col[:, 2:3], s_col[:, 0:1], AF.Copy, scale=1.0 / D),
            waits=W(("v", ms)), inc="a")
        mam = pl.add("scalar", lambda a: a.activation(
            s_col[:, 3:4], s_col[:, 2:3], AF.Square),
            waits=W(("a", mam1, 1)), inc="a")
        mr = pl.add("vector", lambda v: v.scalar_tensor_tensor(
            s_col[:, 4:5], s_col[:, 1:2], 1.0 / D, s_col[:, 3:4],
            op0=ALU.mult, op1=ALU.subtract), waits=W(("a", mam)), inc="v")

        mrs1 = pl.add("scalar", lambda a: a.activation(
            s_col[:, 5:6], s_col[:, 4:5], AF.Ln, bias=1e-5),
            waits=W(("v", mr)), inc="a")
        mrs = pl.add("scalar", lambda a: a.activation(
            s_col[:, 6:7], s_col[:, 5:6], AF.Exp, scale=-0.5),
            waits=W(("a", mrs1, 1)), inc="a")
        return pl.add("vector", lambda v: v.tensor_scalar(
            out_ap, x_ap, s_col[:, 2:3], s_col[:, 6:7],
            op0=ALU.subtract, op1=ALU.mult), waits=W(("a", mrs)), inc="v")

    AR.reset(QFR)
    wp_stg2 = [AR.carve(IF_) for _ in range(2)]
    wp_stg = wp_stg2[0]
    wp_bf2 = [AR.carve(IF_, BF16) for _ in range(2)]
    wp_bf = wp_bf2[0]
    wp_t = AR.carve(8 * P, BF16)
    def mk_qs():
        xsq = AR.carve(IF_)
        acc = AR.carve(IF_)
        b7 = AR.carve(IF_)
        sgn = AR.carve(IF_, BF16)
        sml = AR.carve(4 * (IF_ // 16))
        return dict(m=sml[:, 0:176], r=sml[:, 176:352],
                    rsq=sml[:, 352:528], sc2=sml[:, 528:704],
                    xsq=xsq, acc=acc, b7=b7, sgn=sgn,
                    xsqb=xsq.bitcast(BF16), accb=acc.bitcast(BF16),
                    b7b=b7.bitcast(BF16), rsqb=sml[:, 352:528].bitcast(BF16))
    QSW2 = [mk_qs(), mk_qs()]
    s_xt3 = AR.carve(D)
    s_xtln = AR.carve(D, BF16)
    s_xtT = sb("s_xtT", [P, 3 * DT], BF16)
    wp_scr = AR.carve(D)

    for dst, src in [(s_ident[:], dident), (s_hmask[:], dhmask), (s_taps[:], dtaps),
                     (s_bconv[:], dbconv), (s_bdt[:], dbdt), (s_bin[:], dbin),
                     (s_bxp[:], dbxp), (s_xt3[0:3, :], dxtail), (s_A[:], dalog)]:
        dma(dst, src, "wpl")
    dma(q_acc[0:NS, 0:D], dwdt, "wpl")
    dma(q_b7[:, 0:DT * 80].rearrange("p (kt c) -> p kt c", kt=DT),
        dwxp.rearrange("(kt p) c -> p kt c", p=P), "wpl")

    dpr = dma(s_prk[:], dprty, "wpl")
    d_consts = pl.cnt["wpl"]

    def wp_small(v):
        for ci, cv in enumerate(CONSTS):
            v.memset(s_cst[:, ci:ci + 1], cv)
        v.tensor_copy(s_wxp[:], q_b7[:, 0:DT * 80])
        v.tensor_copy(s_wdt[:], q_acc[0:NS, 0:D])
        v.memset(s_ones1[:], 1.0)
        v.memset(s_hscol[:], 0.0)
        v.memset(s_pcol[:], 0.0)
        return v.memset(s_utail[:], 0.0)
    m_wpsm = pl.add("vector", wp_small, waits=W(("wpl", d_consts)), inc="v")
    m_eA = pl.add("scalar", lambda a: a.activation(s_A[:], s_A[:], AF.Exp),
                  waits=W(("wpl", d_consts), ("v", m_wpsm)), inc="a")
    m_negA = pl.add("vector", lambda v: v.tensor_scalar_mul(s_A[:], s_A[:], -1.0),
                    waits=W(("a", m_eA)), inc="v")

    m_fli = pl.add("vector", lambda v: (v.memset(s_fli[0:1, 0:64], 0),
                                        v.memset(s_fli[0:1, 64:128], 1))[-1], inc="v")

    def zero_flag(sync):
        sync.reg_load(prty_reg, s_prk[0:1, 0:1])
        idx = sync.snap(prty_reg, min_val=0, max_val=1)
        return sync.dma_start(out=sh_flagv[idx], in_=s_fli[0:1, 0:64])
    pl.add("sync", zero_flag, waits=W(("wpl", dpr), ("v", m_fli)), inc="fz")

    last_v = m_fli
    for k in range(DT):
        dL = dma(wp_stg[:, 0:2 * D], dwin_f[k * P:(k + 1) * P, :], "wstg",
                 waits=W(("v", last_v)))
        last_v = pl.add("vector", lambda v: v.tensor_copy(
            wp_bf[:, 0:2 * D], wp_stg[:, 0:2 * D]),
            waits=W(("wstg", dL), ("wsto", pl.cnt.get("wsto", 0))), inc="v")
        dma(g_win[k * P:(k + 1) * P, :], wp_bf[:, 0:2 * D], "wsto", waits=W(("v", last_v)))
    for k in range(DT):
        dL = dma(wp_stg[:, 0:D], dwout_f[k * P:(k + 1) * P, :], "wstg",
                 waits=W(("v", last_v)))
        last_v = pl.add("vector", lambda v: v.tensor_copy(
            wp_bf[:, 0:D], wp_stg[:, 0:D]),
            waits=W(("wstg", dL), ("wsto", pl.cnt.get("wsto", 0))), inc="v")
        dma(g_wout[k * P:(k + 1) * P, :], wp_bf[:, 0:D], "wsto", waits=W(("v", last_v)))

    tj = [0]
    tstores = []

    def tstore(dst_ap, src_ps, mt):
        j = tj[0]
        tj[0] += 1
        sl = (j % 8) * P
        wts = [("t", mt)]
        if j >= 8:
            wts.append(("wsto", tstores[j - 8]))
        mv = pl.add("scalar", lambda a, sl=sl, src_ps=src_ps: a.activation(
            wp_t[:, sl:sl + P], src_ps, AF.Copy), waits=W(*wts), inc="a")
        ds = dma(dst_ap, wp_t[:, sl:sl + P], "wsto", waits=W(("a", mv)))
        tstores.append(ds)
        return mv

    jobs = []
    for gdst, gsrc in ((g_wgT, dwg_f), (g_wuT, dwu_f)):
        for it in range(IT):
            jobs.append((gdst, gsrc[it * P:(it + 1) * P, :], D, it, None))
    for k in range(DT):
        jobs.append((g_wdT, dwd_f[k * P:(k + 1) * P, :], IF_, None, k))
    job_mq = [None] * len(jobs)
    job_tr = [0] * len(jobs)
    dLs = [None] * len(jobs)
    heads = [None] * len(jobs)

    def issue_load(ji):
        gdst, gsrc, F, it, kk = jobs[ji]
        wl = W(("v", m_fli))
        if ji >= 2:
            wl += W(("v", heads[ji - 2][0]), ("a", heads[ji - 2][3]))
        dLs[ji] = dma(wp_stg2[ji % 2][:, 0:F], gsrc, "wstg", waits=wl)
    issue_load(0)
    issue_load(1)
    heads[0] = qhead(QSW2[0], wp_stg2[0][:, 0:jobs[0][2]], P, jobs[0][2],
                     wait_v=W(("wstg", dLs[0])), wait_a=W(("wstg", dLs[0])))
    for ji, (gdst, gsrc, F, it, kk) in enumerate(jobs):
        jb = ji % 2
        if ji + 1 < len(jobs):
            F1 = jobs[ji + 1][2]
            heads[ji + 1] = qhead(QSW2[(ji + 1) % 2],
                                  wp_stg2[(ji + 1) % 2][:, 0:F1], P, F1,
                                  wait_v=W(("wstg", dLs[ji + 1])),
                                  wait_a=W(("wstg", dLs[ji + 1])))
        mq = qtail(QSW2[jb], wp_bf2[jb][:, 0:F], P, F, heads[ji],
                   wait_v=W(("t", job_tr[ji - 2] if ji >= 2 else 0)))
        job_mq[ji] = mq
        last_w = ("v", mq)
        if ji + 2 < len(jobs):
            issue_load(ji + 2)
        nt = F // P
        for hk in range(nt):
            wt = W(last_w)
            if pl.cnt.get("t", 0) == 0:
                wt += W(("wpl", d_consts))
            mt = pl.add("tensor", lambda te, hk=hk, jb=jb: te.transpose(
                psbf(hk % 2)[0:P, 0:P], wp_bf2[jb][:, hk * P:(hk + 1) * P],
                s_ident[:]), waits=wt, inc="t")
            if it is not None:
                dst = gdst[hk * P:(hk + 1) * P, it * P:(it + 1) * P]
            else:
                dst = gdst[(kk // 2) * P:(kk // 2 + 1) * P,
                           hk * 256 + (kk % 2) * P:hk * 256 + (kk % 2) * P + P]
            last_w = ("a", tstore(dst, psbf(hk % 2)[0:P, 0:P], mt))
        job_tr[ji] = pl.cnt["t"]
    dB = dma(wp_stg[0:1, 0:D], dbout_row, "wstg",
             waits=W(last_w, ("v", job_mq[-1])))
    m_br = pl.add("vector", lambda v: v.tensor_copy(s_boutr[:], wp_stg[0:1, 0:D]),
                  waits=W(("wstg", dB)), inc="v")

    m_ts = pl.add("vector", lambda v: v.reduce_sum(s_col[0:3, 0:1], s_xt3[0:3, :],
                                                   axis=AX.X),
                  waits=W(("wpl", d_consts), ("v", m_br)), inc="v")
    m_ta = pl.add("scalar", lambda a: a.activation(wp_scr[0:3, :], s_xt3[0:3, :],
                                                   AF.Square, accum_out=s_col[0:3, 1:2]),
                  waits=W(("wpl", d_consts)), inc="a")

    m_tam1 = pl.add("scalar", lambda a: a.activation(
        s_col[0:3, 2:3], s_col[0:3, 0:1], AF.Copy, scale=1.0 / D),
        waits=W(("v", m_ts)), inc="a")
    m_tam = pl.add("scalar", lambda a: a.activation(
        s_col[0:3, 3:4], s_col[0:3, 2:3], AF.Square),
        waits=W(("a", m_tam1, 1)), inc="a")
    m_tr = pl.add("vector", lambda v: v.scalar_tensor_tensor(
        s_col[0:3, 4:5], s_col[0:3, 1:2], 1.0 / D, s_col[0:3, 3:4],
        op0=ALU.mult, op1=ALU.subtract), waits=W(("a", m_tam)), inc="v")

    m_trs1 = pl.add("scalar", lambda a: a.activation(
        s_col[0:3, 5:6], s_col[0:3, 4:5], AF.Ln, bias=1e-5),
        waits=W(("v", m_tr)), inc="a")
    m_trs = pl.add("scalar", lambda a: a.activation(
        s_col[0:3, 6:7], s_col[0:3, 5:6], AF.Exp, scale=-0.5),
        waits=W(("a", m_trs1, 1)), inc="a")
    m_tap = pl.add("vector", lambda v: v.tensor_scalar(
        s_xtln[0:3, :], s_xt3[0:3, :], s_col[0:3, 2:3], s_col[0:3, 6:7],
        op0=ALU.subtract, op1=ALU.mult), waits=W(("a", m_trs)), inc="v")
    last_ev = m_tap
    for k in range(DT):
        mt = pl.add("tensor", lambda te, k=k: te.transpose(
            psbf(2)[0:P, 0:3], s_xtln[0:3, k * P:(k + 1) * P], s_ident[0:3, 0:3]),
            waits=W(("v", last_ev)), inc="t")
        last_ev = pl.add("vector", lambda v, k=k: v.tensor_copy(
            s_xtT[:, 3 * k:3 * k + 3], psbf(2)[0:P, 0:3]), waits=W(("t", mt)), inc="v")
    m_xtT = last_ev

    pl.barrier()

    AR.reset(FR)
    s_x = [AR.carve(D) for _ in range(TT4)]
    s_xln = [AR.carve(D, BF16) for _ in range(2)]
    s_winK_all = AR.carve(DT * 2 * D, BF16)
    s_winK = [s_winK_all[:, k * 2 * D:(k + 1) * 2 * D] for k in range(DT)]
    s_xlnT_all = AR.carve(DT * TC, BF16)
    s_xlnT = [s_xlnT_all[:, k * TC:(k + 1) * TC] for k in range(DT)]
    s_uext = [AR.carve(TC + 4) for _ in range(DT)]
    s_ufin = [AR.carve(TC) for _ in range(DT)]
    s_ubf_all = AR.carve(2 * TC, BF16)
    s_ubf = [s_ubf_all[:, k * TC:(k + 1) * TC] for k in range(2)]
    s_zsil = [AR.carve(TC, BF16) for _ in range(2)]
    s_dtb = [AR.carve(TC) for _ in range(2)]
    s_Pb = [AR.carve(TC) for _ in range(2)]
    s_dtu = [AR.carve(TC, BF16) for _ in range(2)]
    s_xdb = AR.carve(TC)
    s_dtin = AR.carve(TC, BF16)
    s_bbf = AR.carve(TC, BF16)
    s_cbf = AR.carve(TC, BF16)
    s_brep = AR.carve(NS * TC, BF16)
    s_crep1 = AR.carve(NS * TC, BF16)
    s_da = AR.carve(3 * TC)
    s_dbu = AR.carve(NS * TC, BF16)
    s_h = AR.carve(NS * TC, BF16)
    s_tree = AR.carve(8 * TC)
    s_scr = s_tree
    P1_END = AR.mark()

    for c in range(NCH):
        dxm = [dma(s_x[mi], dx[c * TC + mi * P:c * TC + (mi + 1) * P, :], "xl")
               for mi in range(TT4)]
        dwin = [dma(s_winK[k], g_win[k * P:(k + 1) * P, :], "winl", gq=True)
                for k in range(DT)]

        last_cp = None
        for mi in range(TT4):
            wl = W(("xl", dxm[-1]))
            if last_cp is not None:
                wl += W(("a", last_cp))
            mln = ln_tile(s_x[mi], s_xln[mi % 2], s_scr[:, 0:D],
                          wait_v=wl, wait_a=W(("xl", dxm[-1])))
            for g in range(2):
                bank = 2 + (g % 2)
                wt = W(("v", mln))
                if last_cp is not None:
                    wt += W(("a", last_cp))

                def tr4(te, g=g, mi=mi, bank=bank):
                    for j in range(4):
                        r = te.transpose(psbf(bank)[0:P, j * P:(j + 1) * P],
                                         s_xln[mi % 2][:, (g * 4 + j) * P:
                                                       (g * 4 + j + 1) * P],
                                         s_ident[:])
                    return r
                mt = pl.add("tensor", tr4, waits=wt, inc="t")
                last_cp = pl.add("scalar", lambda a, g=g, mi=mi, bank=bank: a.activation(
                    s_xlnT_all.rearrange("p (k l) -> p k l", l=TC)
                    [:, g * 4:(g + 1) * 4, mi * P:(mi + 1) * P],
                    psbf(bank)[0:P, 0:4 * P].rearrange("p (k l) -> p k l", l=P),
                    AF.Copy), waits=W(("t", mt)), inc="a")
        m_xlnT = last_cp
        if c == 0:
            tap("xlnT0", [P, TC], BF16, s_xlnT[0][:], W(("a", m_xlnT)))

        if c == 0:
            prev_a = None
            for m8 in range(DT):
                def u3mm(te, m8=m8):
                    last = None
                    for k in range(DT):
                        last = te.matmul(psf[4][0:P, 0:3],
                                         s_winK[k][:, m8 * P:(m8 + 1) * P],
                                         s_xtT[:, 3 * k:3 * k + 3],
                                         start=(k == 0), stop=(k == DT - 1))
                    return last
                wt = W(("v", m_xtT), ("winl", dwin[-1]))
                if prev_a is not None:
                    wt += W(("a", prev_a))
                mt = pl.add("tensor", u3mm, waits=wt, inc="t")
                prev_a = pl.add("scalar", lambda a, m8=m8: a.activation(
                    s_utail[:, 3 * m8:3 * m8 + 3], psf[4][0:P, 0:3], AF.Identity,
                    bias=s_bin[:, m8:m8 + 1]), waits=W(("t", mt)), inc="a")
            m_umask = pl.add("vector", lambda v: v.tensor_scalar(
                s_utail[:], s_utail[:], s_hmask[:, 0:1], None, op0=ALU.mult),
                waits=W(("a", prev_a)), inc="v")
            tap("ut0", [P, 3 * DT], F32, s_utail[:], W(("v", m_umask)))
            tap("xtT0", [P, 3 * DT], BF16, s_xtT[:], W(("v", m_xtT)))
            tap("win0", [P, 2 * D], BF16, s_winK[0][:], W(("winl", dwin[-1])))

        ev_u = [None] * DT
        ev_z = [None] * DT
        bank_free = {0: None, 1: None}
        for mm in range(16):
            bank = mm % 2

            def mmfn(te, mm=mm, bank=bank):
                last = None
                for k in range(DT):
                    last = te.matmul(psf[bank][:, 0:TC],
                                     s_winK[k][:, mm * P:(mm + 1) * P],
                                     s_xlnT[k][:], start=(k == 0), stop=(k == DT - 1))
                return last
            waits = W(("a", m_xlnT), ("winl", dwin[-1]))
            if bank_free[bank] is not None:
                waits += W(("a", bank_free[bank]))
            mt = pl.add("tensor", mmfn, waits=waits, inc="t")
            if mm < DT:
                k = mm
                tcw = W(("v", m_umask)) if c == 0 else ()
                tcp = pl.add("scalar", lambda a, k=k: a.activation(
                    s_uext[k][:, 0:3], s_utail[:, 3 * k:3 * k + 3], AF.Copy),
                    waits=tcw, inc="a")
                ev_u[k] = pl.add("scalar", lambda a, k=k, bank=bank: a.activation(
                    s_uext[k][:, 3:3 + TC], psf[bank][:, 0:TC], AF.Identity,
                    bias=s_bin[:, k:k + 1]), waits=W(("t", mt)), inc="a")
                bank_free[bank] = ev_u[k]
            else:
                k = mm - DT
                wz = W(("t", mt))
                if k >= 2:
                    wz += W(("zsp", pl.cnt.get("zsp", 0)))
                ev_z[k] = pl.add("scalar", lambda a, k=k, bank=bank: a.activation(
                    s_zsil[k % 2][:], psf[bank][:, 0:TC], AF.Silu,
                    bias=s_bin[:, DT + k:DT + k + 1]), waits=wz, inc="a")
                bank_free[bank] = ev_z[k]
                dma(sp_zs[c * DT + k], s_zsil[k % 2][:], "zsp", waits=W(("a", ev_z[k])), gq=True)
        for k in range(DT):
            pl.add("vector", lambda v, k=k: v.tensor_copy(
                s_utail[:, 3 * k:3 * k + 3], s_uext[k][:, TC:TC + 3]),
                waits=W(("a", ev_u[k])), inc="v")

        if c == 0:
            tap("uraw0", [P, TC], F32, s_uext[0][:, 3:3 + TC], W(("a", ev_u[0])))
        m_uf = [None] * DT
        for k in range(DT):
            t0 = pl.add("scalar", lambda a, k=k: a.activation(
                s_ufin[k], s_uext[k][:, 0:TC], AF.Copy,
                scale=s_taps[:, 4 * k:4 * k + 1]),
                waits=W(("a", ev_u[k]), ("v", pl.cnt["v"])), inc="a")
            pl.add("vector", lambda v, k=k: v.tensor_scalar(
                s_scr[:, 0:TC], s_uext[k][:, 1:1 + TC], s_taps[:, 4 * k + 1:4 * k + 2],
                None, op0=ALU.mult), waits=W(("a", ev_u[k])), inc="v")
            pl.add("vector", lambda v, k=k: v.scalar_tensor_tensor(
                s_scr[:, TC:2 * TC], s_uext[k][:, 2:2 + TC],
                s_taps[:, 4 * k + 2:4 * k + 3], s_scr[:, 0:TC],
                op0=ALU.mult, op1=ALU.add), inc="v")
            pl.add("vector", lambda v, k=k: v.scalar_tensor_tensor(
                s_scr[:, 0:TC], s_uext[k][:, 3:3 + TC], s_taps[:, 4 * k + 3:4 * k + 4],
                s_scr[:, TC:2 * TC], op0=ALU.mult, op1=ALU.add), inc="v")
            t4 = pl.add("vector", lambda v, k=k: v.tensor_add(
                s_ufin[k], s_ufin[k], s_scr[:, 0:TC]), waits=W(("a", t0)), inc="v")
            m_uf[k] = pl.add("scalar", lambda a, k=k: a.activation(
                s_ufin[k], s_ufin[k], AF.Silu, bias=s_bconv[:, k:k + 1]),
                waits=W(("v", t4)), inc="a")

        if c == 0:
            tap("ufin0", [P, TC], F32, s_ufin[0][:], W(("a", m_uf[0])))
        m_uc = [None] * DT
        xp_mark = [None] * DT
        for k in range(DT):
            wcast = W(("a", m_uf[k]))
            if k >= 2:
                wcast += W(("t", xp_mark[k - 2]))
            m_uc[k] = pl.add("scalar", lambda a, k=k: a.activation(
                s_ubf[k % 2], s_ufin[k], AF.Copy), waits=wcast, inc="a")
            xp_mark[k] = pl.add("tensor", lambda te, k=k: te.matmul(
                psf[4][0:80, 0:TC], s_wxp[:, 80 * k:80 * (k + 1)],
                s_ubf[k % 2], start=(k == 0), stop=(k == DT - 1)),
                waits=W(("a", m_uc[k])), inc="t")
        mt_xp = xp_mark[-1]
        m_xdb = pl.add("scalar", lambda a: a.activation(
            s_xdb[0:80, :], psf[4][0:80, 0:TC], AF.Identity, bias=s_bxp[:, 0:1]),
            waits=W(("t", mt_xp)), inc="a")
        if c == 0:
            tap("xdb0", [80, TC], F32, s_xdb[0:80, :], W(("a", m_xdb)))
        m_dtin = pl.add("scalar", lambda a: a.activation(
            s_dtin[0:NS, :], s_xdb[0:NS, :], AF.Copy), inc="a")
        m_bbf = pl.add("scalar", lambda a: a.activation(
            s_bbf[0:NS, :], s_xdb[32:48, :], AF.Copy), inc="a")
        m_cbf = pl.add("scalar", lambda a: a.activation(
            s_cbf[0:NS, :], s_xdb[64:80, :], AF.Copy), inc="a")
        dbr = dma(sp_brow[c][0:1, :].rearrange("p (a l) -> p a l", a=NS),
                  s_bbf[0:NS, :], "rows", waits=W(("a", m_bbf)))
        dcr = dma(sp_crow[c][0:1, :].rearrange("p (a l) -> p a l", a=NS),
                  s_cbf[0:NS, :], "rows", waits=W(("a", m_cbf)))
        dbrep = dma(s_brep, sp_brow[c][0:1, :].broadcast_to([P, NS * TC]), "reps",
                    waits=W(("rows", dcr)))
        dcrep = dma(s_crep1, sp_crow[c][0:1, :].broadcast_to([P, NS * TC]), "reps",
                    waits=W(("rows", dcr)))

        prev_sp = None
        prev_spill = None
        prev_scan = [None] * NS
        for k in range(DT):
            rb = k % 2

            def dtmm(te, k=k):
                return te.matmul(psf[5][:, 0:TC], s_wdt[:, k * P:(k + 1) * P],
                                 s_dtin[0:NS, :], start=True, stop=True)
            wt = W(("a", m_dtin))
            if prev_sp is not None:
                wt += W(("a", prev_sp))
            mt = pl.add("tensor", dtmm, waits=wt, inc="t")
            wv = W(("t", mt), ("v", pl.cnt["v"]))
            if prev_spill is not None:
                wv += W(("psp", prev_spill))
            a1 = pl.add("scalar", lambda a, k=k, rb=rb: a.activation(
                s_Pb[rb], psf[5][:, 0:TC], AF.Abs, bias=s_bdt[:, k:k + 1]),
                waits=wv, inc="a")
            a2 = pl.add("scalar", lambda a, rb=rb: a.activation(
                s_dtb[rb], s_Pb[rb], AF.Exp, scale=-1.0),
                waits=W(("a", a1, 1)), inc="a")
            a3 = pl.add("scalar", lambda a, rb=rb: a.activation(
                s_Pb[rb], s_dtb[rb], AF.Ln, bias=1.0),
                waits=W(("a", a2, 1)), inc="a")
            a4 = pl.add("scalar", lambda a, k=k, rb=rb: a.activation(
                s_dtb[rb], psf[5][:, 0:TC], AF.Relu, bias=s_bdt[:, k:k + 1]),
                waits=W(("a", a3, 1)), inc="a")
            prev_sp = a4
            v1 = pl.add("vector", lambda v, rb=rb: v.tensor_add(
                s_dtb[rb], s_dtb[rb], s_Pb[rb]), waits=W(("a", a4)), inc="v")
            v2 = pl.add("vector", lambda v, k=k, rb=rb: v.tensor_mul(
                s_dtu[rb], s_dtb[rb], s_ufin[k]), inc="v")
            if c == 0 and k == 0:
                tap("dt0", [P, TC], F32, s_dtb[0][:], W(("v", v2)))
            v3 = pl.add("vector", lambda v, k=k, rb=rb: v.tensor_tensor_scan(
                s_Pb[rb], s_dtb[rb], s_dtb[rb], s_pcol[:, k:k + 1],
                op0=ALU.add, op1=ALU.bypass), inc="v")
            pl.add("scalar", lambda a, k=k, rb=rb: a.activation(
                s_pcol[:, k:k + 1], s_Pb[rb][:, TC - 1:TC], AF.Copy),
                waits=W(("v", v3)), inc="a")
            prev_spill = dma(sp_P[c * DT + k], s_Pb[rb], "psp", waits=W(("v", v3)), gq=True)
            v5a = pl.add("vector", lambda v, rb=rb: v.tensor_tensor(
                s_dbu[:, 0:4 * TC].rearrange("p (a l) -> p a l", a=4),
                s_dtu[rb].unsqueeze(1).broadcast_to([P, 4, TC]),
                s_brep[:, 0:4 * TC].rearrange("p (a l) -> p a l", a=4),
                op=ALU.mult), waits=W(("v", v2), ("reps", dbrep)), inc="v")
            v5 = pl.add("vector", lambda v, rb=rb: v.tensor_tensor(
                s_dbu[:, 4 * TC:].rearrange("p (a l) -> p a l", a=NS - 4),
                s_dtu[rb].unsqueeze(1).broadcast_to([P, NS - 4, TC]),
                s_brep[:, 4 * TC:].rearrange("p (a l) -> p a l", a=NS - 4),
                op=ALU.mult), inc="v")
            scan_marks = [None] * NS
            for n in range(NS):
                pg = n % 3
                wda = W(("v", v1))
                if n >= 3:
                    wda += W(("v", scan_marks[n - 3]))
                elif prev_scan[NS - 3 + n] is not None:
                    wda += W(("v", prev_scan[NS - 3 + n]))
                mda = pl.add("scalar", lambda a, k=k, n=n, pg=pg, rb=rb: a.activation(
                    s_da[:, pg * TC:(pg + 1) * TC], s_dtb[rb], AF.Exp,
                    scale=s_A[:, NS * k + n:NS * k + n + 1]), waits=wda, inc="a")
                scan_marks[n] = pl.add("vector", lambda v, k=k, n=n, pg=pg: v.tensor_tensor_scan(
                    s_h[:, n * TC:(n + 1) * TC], s_da[:, pg * TC:(pg + 1) * TC],
                    s_dbu[:, n * TC:(n + 1) * TC],
                    s_hscol[:, NS * k + n:NS * k + n + 1],
                    op0=ALU.mult, op1=ALU.add),
                    waits=W(("a", mda), ("v", v5a if n < 4 else v5)), inc="v")
            pl.add("scalar", lambda a, k=k: a.activation(
                s_hscol[:, NS * k:NS * (k + 1)].rearrange("p (n o) -> p n o", o=1),
                s_h.rearrange("p (n l) -> p n l", l=TC)[:, :, TC - 1:TC],
                AF.Copy), waits=W(("v", scan_marks[NS - 1])), inc="a")
            if c == 0 and k == 0:
                tap("dbu0", [P, NS * TC], BF16, s_dbu[:], W(("v", scan_marks[0])))
                tap("h0", [P, NS * TC], BF16, s_h[:], W(("v", pl.cnt["v"])))
            prev_scan = scan_marks
            vz = pl.add("vector", lambda v: v.tensor_tensor(
                s_dbu, s_h, s_crep1, op=ALU.mult), waits=W(("reps", dcrep)), inc="v")
            tb = s_tree.bitcast(BF16)
            vt = pl.add("vector", lambda v, tb=tb: v.tensor_tensor(
                tb[:, 0:8 * TC].rearrange("p (a l) -> p a l", l=TC),
                s_dbu.rearrange("p (a b l) -> p a b l", b=2, l=TC)[:, :, 0, :],
                s_dbu.rearrange("p (a b l) -> p a b l", b=2, l=TC)[:, :, 1, :],
                op=ALU.add), waits=W(("v", vz)), inc="v")
            vt = pl.add("vector", lambda v, tb=tb: v.tensor_tensor(
                tb[:, 8 * TC:12 * TC].rearrange("p (a l) -> p a l", l=TC),
                tb[:, 0:8 * TC].rearrange("p (a b l) -> p a b l", b=2, l=TC)
                [:, :, 0, :],
                tb[:, 0:8 * TC].rearrange("p (a b l) -> p a b l", b=2, l=TC)
                [:, :, 1, :], op=ALU.add), inc="v")
            vt = pl.add("vector", lambda v, tb=tb: v.tensor_tensor(
                tb[:, 12 * TC:14 * TC].rearrange("p (a l) -> p a l", l=TC),
                tb[:, 8 * TC:12 * TC].rearrange("p (a b l) -> p a b l", b=2, l=TC)
                [:, :, 0, :],
                tb[:, 8 * TC:12 * TC].rearrange("p (a b l) -> p a b l", b=2, l=TC)
                [:, :, 1, :], op=ALU.add), inc="v")
            vt = pl.add("vector", lambda v, tb=tb: v.tensor_tensor(
                s_tree[:, 7 * TC:8 * TC], tb[:, 12 * TC:13 * TC],
                tb[:, 13 * TC:14 * TC], op=ALU.add), inc="v")
            dma(sp_y[c * DT + k], s_tree[:, 7 * TC:8 * TC], "ysp",
                waits=W(("v", vt)), gq=True)
        pl.barrier()

    def store_state(sync):
        idx = sync.snap(prty_reg, min_val=0, max_val=1)
        return sync.dma_start(out=sh_statev[idx], in_=s_hscol[:])
    dhs = pl.add("sync", store_state, waits=W(("v", pl.cnt["v"]),
                                              ("a", pl.cnt["a"])), inc="hss")

    def set_flag(sync):
        idx = sync.snap(prty_reg, min_val=0, max_val=1)
        return sync.dma_start(out=sh_flagv[idx], in_=s_fli[0:1, 64:128])
    pl.add("sync", set_flag, waits=W(("hss", dhs)), inc="hsf")

    def spin_and_load(sync):
        with (nc.sync.register(name="spin_fl") as flr,
              nc.sync.register(name="spin_cond") as cnd):
            sync.reg_mov(cnd, 1)
            with sync.While(cnd):
                sync.sem_clear(spin_sem)
                sync.dma_start(out=s_spfl[:], in_=sh_flag[0:1, :]
                               ).then_inc(spin_sem, 16)
                sync.wait_ge(spin_sem, 16)
                sync.reg_load(flr, s_spfl[0:1, 0:1])
                sync.reg_alu(cnd, flr, 0, ALU.is_equal)
        return sync.dma_start(out=s_hb[:], in_=sh_state[0:P, :])
    dhb = pl.add("sync", spin_and_load, inc="hbl")
    m_hb = pl.add("vector", lambda v: v.tensor_scalar(
        s_hb[:], s_hb[:], s_hmask[:, 0:1], None, op0=ALU.mult),
        waits=W(("hbl", dhb)), inc="v")

    pl.barrier()

    AR.reset(QFR)
    s2_x = [AR.carve(D) for _ in range(TT4)]
    s2_yz_all = AR.carve(DT * TC, BF16)
    s2_yz = [s2_yz_all[:, k * TC:(k + 1) * TC] for k in range(DT)]
    s2_xln2 = [AR.carve(D) for _ in range(2)]
    s2_xq = [AR.carve(D, BF16) for _ in range(2)]
    s2_xqT_all = AR.carve(DT * TC, BF16)
    s2_xqT = [s2_xqT_all[:, k * TC:(k + 1) * TC] for k in range(DT)]
    s2_wo_all = AR.carve(DT * 256, BF16)
    s2_wo = [s2_wo_all[:, k * 256:(k + 1) * 256] for k in range(DT)]
    s2_hqT_all = AR.carve(IT * TC, BF16)
    s2_hqT = [s2_hqT_all[:, k * TC:(k + 1) * TC] for k in range(IT)]
    s2_wg2 = [AR.carve(DT * 384, BF16) for _ in range(2)]
    s2_wu2 = [AR.carve(DT * 384, BF16) for _ in range(2)]
    OV = AR.mark()
    s2_Pl = [AR.carve(TC) for _ in range(2)]
    s2_yl = [AR.carve(TC) for _ in range(2)]
    s2_zl = [AR.carve(TC, BF16) for _ in range(2)]
    s2_crep = AR.carve(NS * TC, BF16)
    s2_E = AR.carve(2 * TC, BF16)
    s2_z2 = AR.carve(NS * TC, BF16)
    s2_tree = AR.carve(8 * TC)
    s2_y = AR.carve(TC)
    AR.reset(OV)
    s2_wd2 = [AR.carve(IT * 256, BF16) for _ in range(2)]
    hffn_buf = [AR.carve(IF_, BF16) for _ in range(4)]
    hq_sl = [AR.carve(1408, BF16) for _ in range(2)]
    s2_gs2 = [AR.carve(384, BF16) for _ in range(2)]


    for c in range(NCH):
        dxm = [dma(s2_x[mi], dx[c * TC + mi * P:c * TC + (mi + 1) * P, :], "x2l")
               for mi in range(TT4)]
        dcrep = dma(s2_crep, sp_crow[c][0:1, :].broadcast_to([P, NS * TC]), "crep2")

        m_yz = [None] * DT
        for k in range(DT):
            rb = k % 2
            tw = W(("v", m_yz[k - 2])) if k >= 2 else ()
            dPl = dma(s2_Pl[rb], sp_P[c * DT + k], "trio", waits=tw)
            dyl = dma(s2_yl[rb], sp_y[c * DT + k], "trio")
            dzl = dma(s2_zl[rb], sp_zs[c * DT + k], "trio")
            em = [None] * NS
            for n in range(NS):
                pg = n % 2
                we = W(("trio", dzl), ("v", m_hb))
                if n >= 2:
                    we += W(("v", em[n - 2]))
                me = pl.add("scalar", lambda a, k=k, n=n, pg=pg, rb=rb: a.activation(
                    s2_E[:, pg * TC:(pg + 1) * TC], s2_Pl[rb], AF.Exp,
                    scale=s_A[:, NS * k + n:NS * k + n + 1]), waits=we, inc="a")
                em[n] = pl.add("vector", lambda g, k=k, n=n, pg=pg:
                    g.scalar_tensor_tensor(
                        s2_z2[:, n * TC:(n + 1) * TC],
                        s2_E[:, pg * TC:(pg + 1) * TC],
                        s_hb[:, NS * k + n:NS * k + n + 1],
                        s2_crep[:, n * TC:(n + 1) * TC],
                        op0=ALU.mult, op1=ALU.mult),
                    waits=W(("a", me), ("crep2", dcrep), ("v", m_hb)), inc="v")
            tb2 = s2_tree.bitcast(BF16)
            vt = pl.add("vector", lambda v, tb2=tb2: v.tensor_tensor(
                tb2[:, 0:8 * TC].rearrange("p (a l) -> p a l", l=TC),
                s2_z2.rearrange("p (a b l) -> p a b l", b=2, l=TC)[:, :, 0, :],
                s2_z2.rearrange("p (a b l) -> p a b l", b=2, l=TC)[:, :, 1, :],
                op=ALU.add), inc="v")
            vt = pl.add("vector", lambda v, tb2=tb2: v.tensor_tensor(
                tb2[:, 8 * TC:12 * TC].rearrange("p (a l) -> p a l", l=TC),
                tb2[:, 0:8 * TC].rearrange("p (a b l) -> p a b l", b=2, l=TC)
                [:, :, 0, :],
                tb2[:, 0:8 * TC].rearrange("p (a b l) -> p a b l", b=2, l=TC)
                [:, :, 1, :], op=ALU.add), inc="v")
            vt = pl.add("vector", lambda v, tb2=tb2: v.tensor_tensor(
                tb2[:, 12 * TC:14 * TC].rearrange("p (a l) -> p a l", l=TC),
                tb2[:, 8 * TC:12 * TC].rearrange("p (a b l) -> p a b l", b=2,
                                                 l=TC)[:, :, 0, :],
                tb2[:, 8 * TC:12 * TC].rearrange("p (a b l) -> p a b l", b=2,
                                                 l=TC)[:, :, 1, :],
                op=ALU.add), inc="v")
            vy = pl.add("vector", lambda v, rb=rb, tb2=tb2: v.scalar_tensor_tensor(
                s2_y, tb2[:, 12 * TC:13 * TC], 1.0, tb2[:, 13 * TC:14 * TC],
                op0=ALU.bypass, op1=ALU.add, accum_out=None),
                waits=W(("trio", dzl)), inc="v")
            vy = pl.add("vector", lambda v, rb=rb: v.tensor_tensor(
                s2_y, s2_y, s2_yl[rb], op=ALU.add), inc="v")
            m_yz[k] = pl.add("vector", lambda v, k=k, rb=rb: v.tensor_tensor(
                s2_yz[k], s2_y, s2_zl[rb], op=ALU.mult),
                waits=W(("trio", dzl)), inc="v")

        m_res1 = {}
        for wq4 in range(4):
            dwo = [dma(s2_wo[k], g_wout[k * P:(k + 1) * P, wq4 * 256:(wq4 + 1) * 256],
                       "wol", waits=W(("t", pl.cnt.get("t", 0))) if wq4 > 0 else ())
                   for k in range(DT)]
            for mi in range(TT4):
                def womm(te, mi=mi, wq4=wq4, dwo=dwo):
                    te.wait_ge(sems["wol"], dwo[-1])
                    for k in range(DT):
                        te.matmul(psf[mi][:, 0:256], s2_yz[k][:, mi * P:(mi + 1) * P],
                                  s2_wo[k], start=(k == 0), stop=False)
                    return te.matmul(psf[mi][:, 0:256], s_ones1[0:1, :],
                                     s_boutr[0:1, wq4 * 256:(wq4 + 1) * 256],
                                     start=False, stop=True)
                wt = W(("v", m_yz[DT - 1]))
                prev = m_res1.get((wq4 - 1, mi))
                if prev is not None:
                    wt += W(("v", prev))
                mt = pl.add("tensor", womm, waits=wt, inc="t")
                m_res1[(wq4, mi)] = pl.add("vector",
                    lambda v, mi=mi, wq4=wq4: v.tensor_tensor(
                        s2_x[mi][:, wq4 * 256:(wq4 + 1) * 256],
                        s2_x[mi][:, wq4 * 256:(wq4 + 1) * 256],
                        psf[mi][:, 0:256], op=ALU.add),
                    waits=W(("t", mt), ("x2l", dxm[-1])), inc="v")

        if TAPS and c == 0:
            tap("xmid0", [P, D], F32, s2_x[0][:], W(("v", m_res1[(3, 0)])))
            tap("xmid3", [P, D], F32, s2_x[3][:], W(("v", m_res1[(3, 3)])))
            tap("yz0", [P, TC], BF16, s2_yz[0][:, 0:TC], W(("v", m_yz[0])))
        m_xqT_mi = [None] * TT4
        for mi in range(TT4):
            rb = mi % 2
            wv = W(("v", m_res1[(3, mi)]))
            mln = ln_tile(s2_x[mi], s2_xln2[rb], s2_xln2[1 - rb][:, 0:D],
                          wait_v=wv, wait_a=wv)
            mq = quant_tile(s2_xln2[rb], s2_xq[rb], P, D,
                            wait_v=W(("v", mln), ("t", pl.cnt.get("t", 0))),
                            wait_a=W(("v", mln)))
            if TAPS and c == 0 and mi == 0:
                tap("xq0", [P, D], BF16, s2_xq[0][:], W(("v", mq)))
            last_tc = None
            for g in range(2):
                bank = 4 + (g % 2)
                wt = W(("v", mq))
                if last_tc is not None:
                    wt += W(("a", last_tc))

                def tr4(te, g=g, rb=rb, bank=bank):
                    for j in range(4):
                        r = te.transpose(psbf(bank)[0:P, j * P:(j + 1) * P],
                                         s2_xq[rb][:, (g * 4 + j) * P:
                                                   (g * 4 + j + 1) * P],
                                         s_ident[:])
                    return r
                mt = pl.add("tensor", tr4, waits=wt, inc="t")
                last_tc = pl.add("scalar", lambda a, g=g, mi=mi, bank=bank: a.activation(
                    s2_xqT_all.rearrange("p (k l) -> p k l", l=TC)
                    [:, g * 4:(g + 1) * 4, mi * P:(mi + 1) * P],
                    psbf(bank)[0:P, 0:4 * P].rearrange("p (k l) -> p k l", l=P),
                    AF.Copy), waits=W(("t", mt)), inc="a")
            m_xqT_mi[mi] = last_tc
        m_xqT = pl.cnt["a"]

        SLICES = [(j * 384, 384) for j in range(7)] + [(2688, 128)]
        hq_marks = [None] * TT4
        prev_gu = {}
        bank_ev = {}
        prev_tr = {}
        prev_hq_w = {}
        last_mq2 = 0
        tpos = 0
        res1_last = m_res1[(3, 3)]
        for si, (w0, wN) in enumerate(SLICES):
            ws = si % 2
            wqw = W(("t", prev_gu[ws])) if ws in prev_gu else ()
            for k in range(DT):
                dlg = dma(s2_wg2[ws][:, k * 384:k * 384 + wN],
                          g_wgT[k * P:(k + 1) * P, w0:w0 + wN], "gul",
                          waits=wqw, gq=True)
                dls = dma(s2_wu2[ws][:, k * 384:k * 384 + wN],
                          g_wuT[k * P:(k + 1) * P, w0:w0 + wN], "gus",
                          waits=wqw)
            dlast = (dlg, dls)
            for mi in range(TT4):
                bp = mi % 2
                bg, bu = 2 * bp, 2 * bp + 1

                def gumm(te, mi=mi, ws=ws, wN=wN, bg=bg, bu=bu, dlast=dlast):
                    te.wait_ge(sems["gul"], dlast[0])
                    te.wait_ge(sems["gus"], dlast[1])
                    for k in range(DT):
                        te.matmul(psf[bg][:, 0:wN],
                                  s2_xqT[k][:, mi * P:(mi + 1) * P],
                                  s2_wg2[ws][:, k * 384:k * 384 + wN],
                                  start=(k == 0), stop=(k == DT - 1))
                    last = None
                    for k in range(DT):
                        last = te.matmul(psf[bu][:, 0:wN],
                                         s2_xqT[k][:, mi * P:(mi + 1) * P],
                                         s2_wu2[ws][:, k * 384:k * 384 + wN],
                                         start=(k == 0), stop=(k == DT - 1))
                    return last
                wt = W(("a", m_xqT_mi[mi]))
                if si == 0:
                    wt += W(("v", res1_last))
                if bp in bank_ev:
                    wt += W(("a", bank_ev[bp][0]), ("v", bank_ev[bp][1]))
                mt = pl.add("tensor", gumm, waits=wt, inc="t")
                prev_gu[ws] = mt
                ma = pl.add("scalar", lambda a, bp=bp, bg=bg, wN=wN: a.activation(
                    s2_gs2[bp][:, 0:wN], psf[bg][:, 0:wN], AF.Silu),
                    waits=W(("t", mt)), inc="a")
                h1 = pl.add("vector", lambda v, mi=mi, bu=bu, wN=wN, w0=w0:
                    v.scalar_tensor_tensor(
                        hffn_buf[mi][:, w0:w0 + wN], psf[bu][:, 0:wN], 0.0,
                        s2_gs2[mi % 2][:, 0:wN], op0=ALU.bypass, op1=ALU.mult),
                    waits=W(("a", ma)), inc="v")
                bank_ev[bp] = (ma, h1)
                if si in (3, 7):
                    h = 0 if si == 3 else 1
                    hw0 = h * 1408
                    wv = W(("v", h1))
                    if bp in prev_hq_w:
                        wv += W(("t", prev_hq_w[bp]))
                    mq = quant_tile(hffn_buf[mi][:, hw0:hw0 + 1408],
                                    hq_sl[bp][:, 0:1408], P, 1408,
                                    wait_v=wv,
                                    wait_a=W(("v", h1), ("v", last_mq2)),
                                    lowp=True)
                    last_mq2 = mq
                    for g in range(3):
                        ntile = (4, 4, 3)[g]
                        bank = 6 + (tpos % 2)
                        tpos += 1
                        wtr = W(("v", mq))
                        if bank in prev_tr:
                            wtr += W(("a", prev_tr[bank]))

                        def trN(te, bp=bp, bank=bank, ntile=ntile, g=g):
                            for j in range(ntile):
                                r = te.transpose(
                                    psbf(bank)[0:P, j * P:(j + 1) * P],
                                    hq_sl[bp][:, (g * 4 + j) * P:
                                              (g * 4 + j + 1) * P],
                                    s_ident[:])
                            return r
                        mtr = pl.add("tensor", trN, waits=wtr, inc="t")
                        ev = pl.add("scalar", lambda a, h=h, g=g, mi=mi,
                                    bank=bank, ntile=ntile: a.activation(
                            s2_hqT_all.rearrange("p (k l) -> p k l", l=TC)
                            [:, h * 11 + g * 4:h * 11 + g * 4 + ntile,
                             mi * P:(mi + 1) * P],
                            psbf(bank)[0:P, 0:ntile * P]
                            .rearrange("p (k l) -> p k l", l=P),
                            AF.Copy), waits=W(("t", mtr)), inc="a")
                        prev_tr[bank] = ev
                    prev_hq_w[bp] = mtr
                    hq_marks[mi] = ev

        res2_marks = {}
        prev_dmm = {}
        for dp in range(4):
            ws = dp % 2
            wqw = W(("t", prev_dmm[ws])) if ws in prev_dmm else ()
            dlw = dma(s2_wd2[ws][:, 0:11 * 256],
                      g_wdT[dp * P:(dp + 1) * P, 0:11 * 256], "wdl",
                      waits=wqw, gq=True)
            dlx = dma(s2_wd2[ws][:, 11 * 256:IT * 256],
                      g_wdT[dp * P:(dp + 1) * P, 11 * 256:IT * 256], "wds",
                      waits=wqw)
            dlast = (dlw, dlx)
            for mi in range(TT4):
                bank = 4 + (mi % 2)

                def dmm(te, mi=mi, ws=ws, bank=bank, dlast=dlast):
                    te.wait_ge(sems["wdl"], dlast[0])
                    te.wait_ge(sems["wds"], dlast[1])
                    last = None
                    for k in range(IT):
                        last = te.matmul(psf[bank][:, 0:256],
                                         s2_hqT[k][:, mi * P:(mi + 1) * P],
                                         s2_wd2[ws][:, k * 256:(k + 1) * 256],
                                         start=(k == 0), stop=(k == IT - 1))
                    return last
                wt = W(("a", hq_marks[mi]))
                key = (dp - 1, mi) if dp > 0 else None
                prevu = res2_marks.get((dp, mi - 2), res2_marks.get((dp - 1, mi + 2)))
                if prevu is not None:
                    wt += W(("v", prevu))
                mt = pl.add("tensor", dmm, waits=wt, inc="t")
                prev_dmm[ws] = mt
                res2_marks[(dp, mi)] = pl.add("vector",
                    lambda v, mi=mi, dp=dp, bank=bank: v.tensor_tensor(
                        s2_x[mi][:, dp * 256:(dp + 1) * 256],
                        s2_x[mi][:, dp * 256:(dp + 1) * 256],
                        psf[bank][:, 0:256], op=ALU.add),
                    waits=W(("t", mt)), inc="v")
        if TAPS and c == 0:
            tap("hqT0", [P, TC], BF16, s2_hqT[0][:, 0:TC],
                W(("a", hq_marks[0])))
            tap("out0", [P, D], F32, s2_x[0][:], W(("v", res2_marks[(3, 0)])))
        for mi in range(TT4):
            ostore(dout[c * TC + mi * P:c * TC + (mi + 1) * P, :], s2_x[mi],
                   waits=W(("v", res2_marks[(3, mi)])))
        pl.barrier()

    tap("wgT0", [P, IF_], BF16, None, ())
    if TAPS:
        tap_t["wgT0"] = nc.dram_tensor("tap_wgT0", [P, IF_], BF16,
                                       kind="ExternalOutput").ap()
        dma(tap_t["wgT0"], g_wgT[0:P, :], "tap", waits=())
    o_final = pl.cnt["outs"]
    pl.add("sync", lambda e: e.wait_ge(sems["outs"], o_final))

    for g in sorted(pl.groups):
        sems[g] = ec(nc.semaphore("sem_" + g))

    @block.sync
    def _(e):
        pl.emit("sync", e, sems)

    @block.vector
    def _(e):
        pl.emit("vector", e, sems, selfsync=SELFSYNC)

    @block.scalar
    def _(e):
        pl.emit("scalar", e, sems, selfsync=SELFSYNC)

    @block.tensor
    def _(e):
        pl.emit("tensor", e, sems, selfsync=SELFSYNC)

    @block.gpsimd
    def _(e):
        pl.emit("gpsimd", e, sems)

    stack.close()
    _CACHE["nc"] = nc
    return nc


def shard_inputs(inputs):
    x = np.ascontiguousarray(inputs["L0_attn_x"])
    W_in = inputs["L0_attn_W_in"]
    b_in = inputs["L0_attn_b_in"]
    W_conv = np.asarray(inputs["L0_attn_W_conv"]).reshape(D, 4)
    b_conv = inputs["L0_attn_b_conv"]
    W_xp = inputs["L0_attn_W_xp"]
    b_xp = inputs["L0_attn_b_xp"]
    W_dt = inputs["L0_attn_W_dt"]
    b_dt = inputs["L0_attn_b_dt"]
    A_log = inputs["L0_attn_A_log"]
    W_out = inputs["L0_attn_W_out"]
    b_out = inputs["L0_attn_b_out"]
    w_gate = inputs["L0_ffn_w_gate"]
    w_up = inputs["L0_ffn_w_up"]
    w_down = inputs["L0_ffn_w_down"]

    f32 = np.float32
    wxp_pad = np.zeros((D, 80), f32)
    wxp_pad[:, 0:16] = W_xp[:, 0:16]
    wxp_pad[:, 32:48] = W_xp[:, 16:32]
    wxp_pad[:, 64:80] = W_xp[:, 32:48]
    bxp_pad = np.zeros((80, 1), f32)
    bxp_pad[0:16, 0] = b_xp[0:16]
    bxp_pad[32:48, 0] = b_xp[16:32]
    bxp_pad[64:80, 0] = b_xp[32:48]
    bdt_c = np.ascontiguousarray(np.asarray(b_dt).reshape(DT, P).T, dtype=f32)
    bconv_c = np.ascontiguousarray(np.asarray(b_conv).reshape(DT, P).T, dtype=f32)
    bin_c = np.ascontiguousarray(np.asarray(b_in).reshape(16, P).T, dtype=f32)
    taps_c = np.zeros((P, DT * 4), f32)
    for k in range(DT):
        taps_c[:, 4 * k:4 * (k + 1)] = W_conv[k * P:(k + 1) * P, :]
    alog_c = np.zeros((P, DT * NS), f32)
    for k in range(DT):
        alog_c[:, NS * k:NS * (k + 1)] = A_log[k * P:(k + 1) * P, :]

    win_f = np.ascontiguousarray(W_in, dtype=f32)
    wout_f = np.ascontiguousarray(W_out, dtype=f32)
    wg_f = np.ascontiguousarray(w_gate, dtype=f32)
    wu_f = np.ascontiguousarray(w_up, dtype=f32)
    wd_f = np.ascontiguousarray(w_down, dtype=f32)
    maps = []
    for c in range(8):
        b, h = c // 2, c % 2
        xs = np.ascontiguousarray(x[b, h * 4096:(h + 1) * 4096, :], dtype=f32)
        if h == 1:
            xt = np.ascontiguousarray(x[b, 4093:4096, :], dtype=f32)
            hm = np.ones((P, 1), f32)
        else:
            xt = np.zeros((3, D), f32)
            hm = np.zeros((P, 1), f32)
        maps.append({
            "x": xs, "xtail": xt, "hmask": hm,
            "win_f": win_f, "wout_f": wout_f, "wg_f": wg_f, "wu_f": wu_f,
            "wd_f": wd_f,
            "prty": np.array([[h]], np.int32),
            "wxp": wxp_pad, "bxp": bxp_pad,
            "wdt": np.ascontiguousarray(W_dt, dtype=f32),
            "bdt": bdt_c, "bin": bin_c, "taps": taps_c, "bconv": bconv_c,
            "alog": alog_c,
            "bout_row": np.ascontiguousarray(np.asarray(b_out).reshape(1, D), dtype=f32),
        })
    return maps


def kernel(**inputs):
    nc = build_program()
    maps = shard_inputs(inputs)
    res = run_bass_kernel_spmd(nc, maps, core_ids=list(range(8)))
    out = np.zeros((4, 8192, D), np.float32)
    for c in range(8):
        b, h = c // 2, c % 2
        out[b, h * 4096:(h + 1) * 4096, :] = np.asarray(res.results[c]["out"])
    if TAPS:
        global _LAST_TAPS
        _LAST_TAPS = {}
        for c in (0, 1):
            for k, v in res.results[c].items():
                if k.startswith("tap_"):
                    _LAST_TAPS[f"c{c}_{k[4:]}"] = np.asarray(v)
    return out
